# version 06
# baseline (speedup 1.0000x reference)
import contextlib

import numpy as np
import ml_dtypes
import concourse.bass as bass
import concourse.mybir as mybir
from concourse.bass_utils import run_bass_kernel_spmd

F32 = mybir.dt.float32
BF16 = mybir.dt.bfloat16
AF = mybir.ActivationFunctionType
ALU = mybir.AluOpType
AX = mybir.AxisListType

T = 4096
D = 1024
NPAIR = 8
DT = 8
NE = 16
HID = 2048
NTOK = 2048
NG = 4
LN_EPS = 1e-5
GATE_EPS = 1e-12


class Prog:
    ENGINES = ("sync", "gpsimd", "scalar", "vector", "tensor")
    ENG_OF_SEM = {"PE": "tensor", "ACT": "scalar", "DVE": "vector",
                  "PL": "gpsimd", "SP": "sync"}

    def __init__(self, nc):
        self.nc = nc
        self.ops = []
        self.ctr = {}
        self.sem_names = set()

    def op(self, eng, fn, waits=(), inc=None, amt=1, hard=()):
        evs = tuple(w for w in waits if w is not None)
        hevs = tuple(w for w in hard if w is not None)
        for (s, _v) in evs + hevs:
            self.sem_names.add(s)
        ev = None
        if inc is not None:
            self.ctr[inc] = self.ctr.get(inc, 0) + amt
            self.sem_names.add(inc)
            ev = (inc, self.ctr[inc])
        self.ops.append((eng, fn, evs, inc, amt, ev, hevs))
        return ev

    SER_ENGINES = ("vector", "scalar", "gpsimd")

    def emit(self, stack, sim_serial=False):
        for eng in self.ENGINES:
            if (sim_serial or eng in self.SER_ENGINES) and any(
                    o[0] == eng for o in self.ops):
                self.sem_names.add("ser_" + eng)
        sems = {s: stack.enter_context(self.nc.semaphore(s))
                for s in sorted(self.sem_names)}
        eng_of = self.ENG_OF_SEM
        with self.nc.Block() as blk:
            for eng_name in self.ENGINES:
                ops = [o for o in self.ops if o[0] == eng_name]
                if not ops:
                    continue

                ser = sim_serial or eng_name in self.SER_ENGINES

                def body(e, ops=ops, eng_name=eng_name, ser=ser):
                    waited = {}
                    ser_val = 0
                    last_ev = None
                    for (_eng, fn, evs, inc, amt, ev, hevs) in ops:
                        for (s, v) in evs:
                            if not ser and eng_of.get(s) == eng_name:
                                continue
                            if waited.get(s, 0) >= v:
                                continue
                            e.wait_ge(sems[s], v)
                            waited[s] = v
                        for (s, v) in hevs:
                            if waited.get(s, 0) >= v:
                                continue
                            e.wait_ge(sems[s], v)
                            waited[s] = v
                        if ser and last_ev is not None:
                            s, v = last_ev
                            if waited.get(s, 0) < v:
                                e.wait_ge(sems[s], v)
                                waited[s] = v
                        ins = fn(e)
                        if ins is not None:
                            if inc is not None:
                                ins.then_inc(sems[inc], amt)
                            is_dma = (inc or "").startswith("d_")
                            if ser and not is_dma:
                                if inc is None:
                                    ser_val += 1
                                    ins.then_inc(
                                        sems["ser_" + eng_name], 1)
                                    last_ev = ("ser_" + eng_name, ser_val)
                                else:
                                    last_ev = ev

                getattr(blk, eng_name)(body)


def build_program(tap=None, sim_serial=False):
    nc = bass.Bass("TRN2", target_bir_lowering=False)

    xin = nc.dram_tensor("xin", [T, D], F32, kind="ExternalInput")
    winT_in = nc.dram_tensor("winT", [D, 3072], F32, kind="ExternalInput")
    binT_in = nc.dram_tensor("binT", [128, 24], F32, kind="ExternalInput")
    woutT_in = nc.dram_tensor("woutT", [D, D], F32, kind="ExternalInput")
    bout_in = nc.dram_tensor("bout", [1, D], F32, kind="ExternalInput")
    gate_in = nc.dram_tensor("gate", [D, NE], F32, kind="ExternalInput")
    w1_in = nc.dram_tensor("w1", [NE, D, HID], F32, kind="ExternalInput")
    w2_in = nc.dram_tensor("w2", [NE, HID, D], F32, kind="ExternalInput")
    b1_in = nc.dram_tensor("b1", [NE, HID], F32, kind="ExternalInput")
    b2_in = nc.dram_tensor("b2", [NE, D], F32, kind="ExternalInput")
    c_idbf_in = nc.dram_tensor("c_idbf", [128, 128], BF16, kind="ExternalInput")
    c_idf_in = nc.dram_tensor("c_idf", [128, 128], F32, kind="ExternalInput")
    c_perm_in = nc.dram_tensor("c_perm", [128, 128], BF16, kind="ExternalInput")
    c_csk_in = nc.dram_tensor("c_csk", [128, T], BF16, kind="ExternalInput")
    c_cwk_in = nc.dram_tensor("c_cwk", [128, T], BF16, kind="ExternalInput")
    c_msk_in = nc.dram_tensor("c_msk", [128, 4 * 512], BF16,
                              kind="ExternalInput")
    c_mscal_in = nc.dram_tensor("c_mscal", [128, 4], F32, kind="ExternalInput")
    c_kill_in = nc.dram_tensor("c_kill", [128, 1], F32, kind="ExternalInput")
    c_esel_in = nc.dram_tensor("c_esel", [16, NE * 128], BF16,
                               kind="ExternalInput")
    c_or_in = nc.dram_tensor("c_onesrow", [1, 128], F32, kind="ExternalInput")
    c_hsel_in = nc.dram_tensor("c_hsel", [128, 33], BF16, kind="ExternalInput")
    c_or33_in = nc.dram_tensor("c_or33", [33, 128], F32, kind="ExternalInput")
    c_oc_in = nc.dram_tensor("c_onescol", [128, 1], F32, kind="ExternalInput")
    c_o512_in = nc.dram_tensor("c_ones512", [1, 512], BF16,
                               kind="ExternalInput")

    y_out = nc.dram_tensor("y", [NTOK, D], F32, kind="ExternalOutput")
    tap_shapes = {
        "xT": ([128, DT * T], BF16),
        "qT": ([128, NTOK], BF16), "kT": ([128, T], BF16),
        "V": ([128, 32 * 130], BF16), "yT": ([128, NPAIR * NTOK], BF16),
        "s1": ([128, DT * NTOK], F32), "x2": ([128, DT * NTOK], F32),
        "gT": ([16, NTOK], BF16), "gw": ([128, 96], F32),
        "acc0": ([128, DT * NTOK], F32),
        "acc": ([128, DT * NTOK], F32),
    }
    tap_out = None
    if tap is not None:
        shp, tdt = tap_shapes[tap]
        tap_out = nc.dram_tensor("tapout", shp, tdt, kind="ExternalOutput")

    P = Prog(nc)
    stack = contextlib.ExitStack()

    def sbl(name, shape, dt):
        return stack.enter_context(nc.sbuf_tensor("sb_" + name, shape, dt))

    def finish_tap(src_ap, waits):
        ev = P.op("sync", lambda e: e.dma_start(tap_out.ap(), src_ap),
                  waits=waits, inc="d_y", amt=16)
        tails = [(s, P.ctr[s]) for s in sorted(P.ctr) if s.startswith("d_")]
        P.op("sync", lambda e: e.nop(), waits=[ev] + tails)
        P.emit(stack, sim_serial=sim_serial)
        stack.close()
        return nc

    with stack:
        c_idbf = sbl("c_idbf", [128, 128], BF16)
        c_idf = sbl("c_idf", [128, 128], F32)
        c_perm = sbl("c_perm", [128, 128], BF16)
        c_or = sbl("c_or", [1, 128], F32)
        c_oc = sbl("c_oc", [128, 1], F32)
        c_o512 = sbl("c_o512", [1, 512], BF16)
        binT = sbl("binT", [128, 24], F32)
        boutbf = sbl("boutbf", [1, D], BF16)
        b2bf = sbl("b2bf", [16, D], BF16)
        gatef = sbl("gatef", [128, DT * NE], F32)
        c_mscal = sbl("c_mscal", [128, 4], F32)
        c_kill = sbl("c_kill", [128, 1], F32)
        c_hsel = sbl("c_hsel", [128, 33], BF16)
        c_or33 = sbl("c_or33", [33, 128], F32)
        normcol = sbl("normcol", [33, 16], F32)
        gwork = sbl("gwork", [128, 96], F32)
        fwork = sbl("fwork", [128, 8], F32)
        negMcol = sbl("negMcol", [128, 4], F32)
        mwork = sbl("mwork", [1, 8], F32)

        pbd = [stack.enter_context(nc.psum_tensor(f"pbd{i}", [128, 1024],
                                                   F32)) for i in range(2)]
        pbs = [stack.enter_context(nc.psum_tensor(f"pb{i}", [128, 512], F32))
               for i in range(4, 8)]
        pb = [pbd[0][:, 0:512], pbd[0][:, 512:1024],
              pbd[1][:, 0:512], pbd[1][:, 512:1024]] + \
             [t[:] for t in pbs]

        st_ac = contextlib.ExitStack()

        def sbr(st, name, shape, dt):
            return st.enter_context(
                nc.sbuf_tensor("sb_" + name, shape, dt, side="right"))

        yT_all = sbr(st_ac, "yT_all", [128, NPAIR * NTOK], BF16)
        wrow = sbr(st_ac, "wrow", [1, 2048], F32)
        xOwn2 = sbr(st_ac, "xOwn2", [128, DT * NTOK], BF16)
        st_x = contextlib.ExitStack()
        xT = sbr(st_x, "xT", [128, DT * T], BF16)

        def dma_to(dst, src_ap):
            return lambda e: e.dma_start(dst, src_ap)

        for dst, src in ((c_idbf, c_idbf_in), (c_idf, c_idf_in),
                         (c_perm, c_perm_in),
                         (c_or, c_or_in), (c_oc, c_oc_in), (c_o512, c_o512_in),
                         (binT, binT_in), (c_mscal, c_mscal_in),
                         (c_hsel, c_hsel_in), (c_or33, c_or33_in),
                         (c_kill, c_kill_in)):
            P.op("sync", dma_to(dst[:], src.ap()), inc="d_c", amt=16)
        ev_consts = P.op("sync", lambda e: e.dma_start(
            gatef[:].rearrange("p (i e) -> p i e", i=DT),
            gate_in.ap().rearrange("(i p) e -> p i e", p=128)),
            inc="d_c", amt=16)
        P.op("gpsimd", lambda e: e.dma_start(boutbf[:], bout_in.ap()),
             inc="d_cg", amt=16)
        ev_cg = P.op("gpsimd", lambda e: e.dma_start(b2bf[:], b2_in.ap()),
                     inc="d_cg", amt=16)

        from concourse.hw_specs import get_activation_tables
        set_idx = list(get_activation_tables(nc.m.arch)).index(
            "natural_log_exp_and_others")

        def load_tab(e):
            return e.add_instruction(mybir.InstLoadActFuncSet(
                name=nc.get_next_instruction_name(), act_func_set_id=set_idx,
                ins=[], outs=[]))
        P.op("scalar", load_tab)

        st_a = contextlib.ExitStack()
        astage = sbr(st_a, "astage", [128, 2 * 4096], BF16)
        ev_chunk_mm = [None] * 8
        evac_ev = [None] * 4
        last_evac = {"vector": None, "scalar": None}
        for c in range(8):
            par = c % 2
            ev_dx = P.op(
                "gpsimd", lambda e, c=c, par=par: e.dma_start(
                    astage[:, par * 4096:(par + 1) * 4096]
                    .rearrange("p (n d) -> p n d", n=4),
                    xin.ap()[c * 512:(c + 1) * 512, :]
                    .rearrange("(n p) d -> p n d", p=128)),
                waits=[ev_chunk_mm[c - 2]] if c >= 2 else [],
                inc=f"d_x{par}", amt=16)
            for nsub in range(4):
                for i in range(DT):
                    tl = nsub * DT + i
                    bank = tl % 4
                    src = astage[:, par * 4096 + nsub * 1024 + i * 128:
                                 par * 4096 + nsub * 1024 + (i + 1) * 128]
                    mm = P.op("tensor", lambda e, src=src, bank=bank:
                              e.matmul(pb[bank][:, 0:128], src, c_idbf[:],
                                       start=True, stop=True),
                              waits=[ev_dx, evac_ev[bank], ev_consts],
                              inc="PE")
                    ev_chunk_mm[c] = mm
                    dst = xT[:, i * T + c * 512 + nsub * 128:
                             i * T + c * 512 + (nsub + 1) * 128]
                    veng = "vector" if (tl % 2 == 0) else "scalar"
                    sem = "DVE" if veng == "vector" else "ACT"

                    def evac(e, dst=dst, bank=bank, veng=veng):
                        if veng == "vector":
                            return e.tensor_copy(dst, pb[bank][:, 0:128])
                        return e.copy(dst, pb[bank][:, 0:128])
                    evac_ev[bank] = P.op(veng, evac, waits=[mm], inc=sem)
                    last_evac[veng] = evac_ev[bank]
        ev_x_ready = [last_evac["vector"], last_evac["scalar"]]
        ev_a_pe = ev_chunk_mm[7]
        st_a.close()

        if tap == "xT":
            return finish_tap(xT[:], ev_x_ready)

        st_b = contextlib.ExitStack()
        kT = sbr(st_b, "kT", [128, T], BF16)
        V_sb = sbr(st_b, "V_sb", [128, 32 * 130], BF16)
        qT = sbr(st_b, "qT", [128, NTOK], BF16)
        winp = sbr(st_b, "winp", [128, 3072], BF16)
        masks = sbr(st_b, "masks", [128, 4 * 512], BF16)
        cs_k = sbr(st_b, "cs_k", [128, T], BF16)
        cw_k = sbr(st_b, "cw_k", [128, T], BF16)
        qkstage = sbr(st_b, "qkstage", [128, 2 * 512], BF16)
        vstage = sbr(st_b, "vstage", [128, 2 * 512], BF16)
        rtmp1 = sbr(st_b, "rtmp1", [128, 512], BF16)
        rtmp2 = sbr(st_b, "rtmp2", [128, 512], BF16)
        Abuf = sbr(st_b, "Abuf", [128, 4 * 1024], BF16)
        r1sb = rtmp1[0:64, :]
        r2sb = rtmp2[0:64, :]

        ev_btab = None
        for dst, src in ((masks, c_msk_in), (cs_k, c_csk_in),
                         (cw_k, c_cwk_in)):
            ev_btab = P.op("sync", dma_to(dst[:], src.ap()),
                           waits=[ev_a_pe] + ev_x_ready, inc="d_c", amt=16)

        ev_vones = P.op("vector", lambda e: e.memset(
            V_sb[:].rearrange("p (k two g) -> p k two g", two=2, g=65)
            [:, :, :, 64:65], 1.0), waits=[ev_a_pe])

        ev_winp = [None] * (NPAIR + 1)
        ev_winp[0] = P.op("gpsimd", lambda e: e.dma_start(
            winp[:].rearrange("p (i c) -> p i c", i=DT),
            winT_in.ap()[:, 0:384].rearrange("(i p) c -> p i c", p=128)),
            waits=[ev_consts, ev_a_pe] + ev_x_ready, inc="d_wp", amt=16)

        ev_inproj_done = [None] * NPAIR
        ev_norm = [None, None]
        ev_rbc_used = None
        b_free = [None, None, None, None]
        ev_vcopy = None
        ev_stage_free = [None, None]
        ev_vstage_free = [None, None]

        for p in range(NPAIR):
            ev_rope_q = None
            ev_rope_k = None
            last_inproj_mm = None
            pend = None

            chunk_list = ([(0, g) for g in range(NG)]
                          + [(1, kc) for kc in range(8)]
                          + [(2, kc) for kc in range(8)])

            for idx, (kind, ci) in enumerate(chunk_list):
                bank = idx % 2
                par = idx % 2
                tok0 = (2 * ci) * 512 if kind == 0 else ci * 512
                rhs_l = [xT[:, i * T + tok0:i * T + tok0 + 512]
                         for i in range(DT)]
                mm = None
                for i in range(DT):
                    lhs = winp[:, i * 384 + kind * 128:
                               i * 384 + (kind + 1) * 128]
                    mm = P.op("tensor", lambda e, lhs=lhs, rhs=rhs_l[i],
                              bank=bank, st=(i == 0), sp=(i == DT - 1):
                              e.matmul(pb[bank][:, 0:512], lhs, rhs,
                                       start=st, stop=sp),
                              waits=[ev_winp[p], b_free[bank]] + ev_x_ready,
                              inc="PE" if i == DT - 1 else None)
                last_inproj_mm = mm

                def make_stage2(kind=kind, ci=ci, bank=bank, par=par, mm=mm,
                                p=p, tok0=tok0):
                    def stage2():
                        nonlocal ev_rope_q, ev_rope_k, ev_vcopy
                        bcol = binT[:, 3 * p + kind:3 * p + kind + 1]
                        if kind < 2:
                            stg = qkstage[:, par * 512:(par + 1) * 512]
                            evs = P.op("scalar", lambda e, bank=bank,
                                       bcol=bcol, stg=stg: e.activation(
                                           stg, pb[bank][:, 0:512],
                                           AF.Identity, bias=bcol),
                                       waits=[mm, ev_stage_free[par],
                                              ev_btab], inc="ACT")
                            b_free[bank] = evs
                            swb = 2 + par
                            sw = P.op("tensor", lambda e, swb=swb, stg=stg:
                                      e.matmul(pb[swb][:, 0:512], c_perm[:],
                                               stg, start=True, stop=True),
                                      waits=[evs, b_free[swb], ev_consts],
                                      inc="PE")
                            dst = (qT if kind == 0 else kT)
                            off = ci * 512
                            P.op("vector", lambda e, tok0=tok0, stg=stg:
                                 e.tensor_mul(rtmp1[:], stg,
                                              cs_k[:, tok0:tok0 + 512]),
                                 waits=[evs, ev_btab])
                            P.op("vector", lambda e, tok0=tok0, swb=swb:
                                 e.tensor_mul(rtmp2[:], pb[swb][:, 0:512],
                                              cw_k[:, tok0:tok0 + 512]),
                                 waits=[sw, ev_btab])
                            t3 = P.op("vector", lambda e, dst=dst, off=off:
                                      e.tensor_add(dst[:, off:off + 512],
                                                   rtmp1[:], rtmp2[:]),
                                      inc="DVE")
                            b_free[swb] = t3
                            ev_stage_free[par] = t3
                            if kind == 0:
                                ev_rope_q = t3
                            else:
                                ev_rope_k = t3
                        else:
                            vstg = vstage[:, par * 512:(par + 1) * 512]
                            evs = P.op("scalar", lambda e, bank=bank,
                                       bcol=bcol, vstg=vstg: e.activation(
                                           vstg, pb[bank][:, 0:512],
                                           AF.Identity, bias=bcol),
                                       waits=[mm, ev_vstage_free[par],
                                              ev_btab], inc="ACT")
                            b_free[bank] = evs
                            cp = None
                            for m in range(4):
                                swb = 2 + m % 2
                                tr = P.op("tensor", lambda e, m=m, swb=swb,
                                          vstg=vstg: e.matmul(
                                              pb[swb][:, 0:128],
                                              vstg[:, m * 128:(m + 1) * 128],
                                              c_idbf[:], start=True,
                                              stop=True),
                                          waits=[evs, b_free[swb],
                                                 ev_consts], inc="PE")
                                base = (ci * 4 + m) * 130

                                def vcopy(e, base=base, swb=swb):
                                    dst = V_sb[:, base:base + 130].rearrange(
                                        "p (two g) -> p two g",
                                        two=2)[:, :, 0:64]
                                    src = pb[swb][:, 0:128].rearrange(
                                        "p (two g) -> p two g", two=2)
                                    return e.tensor_copy(dst, src)
                                cp = P.op("vector", vcopy,
                                          waits=[tr, ev_vones], inc="DVE")
                                b_free[swb] = cp
                            ev_vstage_free[par] = cp
                            ev_vcopy = cp
                    return stage2

                if pend is not None:
                    pend()
                pend = make_stage2()
            pend()
            ev_inproj_done[p] = last_inproj_mm

            if p + 1 < NPAIR:
                ev_winp[p + 1] = P.op("gpsimd", lambda e, p=p: e.dma_start(
                    winp[:].rearrange("p (i c) -> p i c", i=DT),
                    winT_in.ap()[:, (p + 1) * 384:(p + 2) * 384]
                    .rearrange("(i p) c -> p i c", p=128)),
                    waits=[last_inproj_mm], inc="d_wp", amt=16)

            if tap == "qT" and p == 0:
                return finish_tap(qT[:], [ev_rope_q])
            if tap == "kT" and p == 0:
                return finish_tap(kT[:], [ev_rope_k])
            if tap == "V" and p == 0:
                return finish_tap(V_sb[:], [ev_vcopy])

            ev_stats = None
            for ci in range(12):
                src_buf = qT if ci < 4 else kT
                off = (ci * 512) if ci < 4 else ((ci - 4) * 512)
                spar = ci % 2
                sqev = P.op("vector", lambda e, src_buf=src_buf, off=off,
                            spar=spar: e.tensor_mul(
                                qkstage[:, spar * 512:(spar + 1) * 512],
                                src_buf[:, off:off + 512],
                                src_buf[:, off:off + 512]),
                            waits=[ev_rope_q if ci < 4 else ev_rope_k,
                                   ev_stats, ev_stage_free[spar]],
                            inc="DVE")
                smm = P.op("tensor", lambda e, spar=spar: e.matmul(
                    pb[6][0:33, 0:512], c_hsel[:],
                    qkstage[:, spar * 512:(spar + 1) * 512],
                    start=True, stop=True),
                    waits=[sqev, ev_rbc_used, ev_consts], inc="PE")
                ev_stats = P.op("vector", lambda e, ci=ci: e.reduce_max(
                    normcol[:, ci:ci + 1], pb[6][0:33, 0:512], axis=AX.X),
                    waits=[smm], inc="DVE")
                ev_rbc_used = ev_stats
                ev_stage_free[spar] = smm
            ch = ev_stats
            for fng in (
                lambda e: e.reduce_max(normcol[:, 12:13], normcol[:, 0:4],
                                       axis=AX.X),
                lambda e: e.reduce_max(normcol[:, 13:14], normcol[:, 4:12],
                                       axis=AX.X),
                lambda e: e.tensor_mul(normcol[:, 14:15], normcol[:, 12:13],
                                       normcol[:, 13:14]),
                lambda e: e.tensor_scalar_max(normcol[:, 14:15],
                                              normcol[:, 14:15], 1e-12),
            ):
                ch = P.op("vector", fng, hard=[ch], inc="DVE")
            lnp = ch
            ln1 = P.op("scalar", lambda e: e.activation(
                normcol[:, 15:16], normcol[:, 14:15], AF.Ln), waits=[lnp])
            sq1 = P.op("scalar", lambda e: e.activation(
                normcol[:, 14:15], normcol[:, 15:16], AF.Exp, scale=0.5),
                inc="ACT")
            ngm = P.op("vector", lambda e: e.tensor_scalar(
                normcol[:, 15:16], normcol[:, 14:15], -0.128875, -1.0,
                ALU.mult, ALU.add), waits=[sq1], hard=[lnp], inc="DVE")
            ev_negM = [None, None]
            for head in (0, 1):
                nb = P.op("tensor", lambda e, head=head: e.matmul(
                    pb[6][:, 0:1],
                    c_or33[32 * head:32 * head + 1, :],
                    normcol[32 * head:32 * head + 1, 15:16],
                    start=True, stop=True),
                    waits=[ngm, ev_rbc_used, ev_consts], inc="PE")
                evn = P.op(
                    "vector", lambda e, head=head: e.tensor_copy(
                        negMcol[:, head:head + 1], pb[6][:, 0:1]),
                    waits=[nb], inc="DVE")
                ev_negM[head] = P.op(
                    "vector", lambda e, head=head: e.tensor_sub(
                        negMcol[:, 2 + head:3 + head],
                        negMcol[:, head:head + 1], c_kill[:]),
                    waits=[ev_consts], hard=[evn], inc="DVE")
                ev_rbc_used = ev_negM[head]

            for g in range(NG):
                nkt = 8 * g + 8
                nvp = nkt // 2
                ev_exp = {}
                ev_av = {}
                ev_y_last = [None, None]

                def emit_av(vp, head, g=g, nkt=nkt, nvp=nvp):
                    ypb = 4 + head
                    av = None
                    ab = (head * 2 + vp % 2) * 1024
                    for v in (2 * vp, 2 * vp + 1):
                        vsl = V_sb[:, v * 130 + head * 65:
                                   v * 130 + head * 65 + 65]
                        asl = Abuf[:, ab + (v % 2) * 512:
                                   ab + (v % 2) * 512 + 512]
                        av = P.op("tensor", lambda e, ypb=ypb, vsl=vsl,
                                  asl=asl, st=(v == 0), sp=(v == nkt - 1):
                                  e.matmul(pb[ypb][0:65, :], vsl, asl,
                                           start=st, stop=sp),
                                  waits=[ev_exp[(head, vp)], ev_vcopy,
                                         ev_norm[head] if v == 0 else None],
                                  inc="PE" if v % 2 == 1 else None)
                    ev_av[(head, vp)] = av
                    if vp == nvp - 1:
                        ev_y_last[head] = av

                for vp in range(nvp):
                    v0 = 2 * vp
                    for head in (0, 1):
                        smm = None
                        for v in (v0, v0 + 1):
                            hb = head * 2 + v % 2
                            ks = kT[64 * head:64 * head + 64,
                                    v * 128:(v + 1) * 128]
                            qs = qT[64 * head:64 * head + 64,
                                    g * 512:(g + 1) * 512]
                            smm = P.op("tensor", lambda e, hb=hb, ks=ks,
                                       qs=qs, tp=(64 * head, 0): e.matmul(
                                           pb[hb][:, 0:512], ks, qs,
                                           start=True, stop=True,
                                           tile_position=tp),
                                       waits=[ev_rope_q, ev_rope_k,
                                              b_free[hb]],
                                       inc="PE" if v % 2 == 1 else None)
                        u = v0 - 8 * g
                        bcol_i = head if u < 4 else 2 + head
                        ab = (head * 2 + vp % 2) * 1024
                        asl2 = Abuf[:, ab:ab + 1024]
                        ex = P.op("scalar", lambda e, head=head, asl2=asl2,
                                  bcol_i=bcol_i: e.activation(
                                      asl2, pbd[head][:, 0:1024], AF.Exp,
                                      bias=negMcol[:, bcol_i:bcol_i + 1],
                                      scale=0.125),
                                  waits=[smm, ev_negM[head]],
                                  inc="ACT")
                        b_free[head * 2] = ex
                        b_free[head * 2 + 1] = ex
                        if 0 <= u < 4:
                            ex = P.op("vector",
                                      lambda e, asl2=asl2, u=u:
                                      e.tensor_mul(
                                          asl2, asl2,
                                          masks[:, u * 512:(u + 2) * 512]),
                                      waits=[ex, ev_btab], inc="DVE")
                        ev_exp[(head, vp)] = ex
                    if vp >= 1:
                        for head in (0, 1):
                            emit_av(vp - 1, head)
                for head in (0, 1):
                    emit_av(nvp - 1, head)

                for head in (0, 1):
                    ypb = 4 + head
                    rrow = wrow[0:1, head * 512:head * 512 + 512]
                    lnrow = wrow[0:1, 1024 + head * 512:1536 + head * 512]
                    rsb = r1sb if head == 0 else r2sb
                    rc1 = P.op("scalar", lambda e, ypb=ypb, lnrow=lnrow:
                               e.activation(lnrow, pb[ypb][64:65, :],
                                            AF.Ln),
                               waits=[ev_y_last[head]], inc="ACT")
                    rc = P.op("scalar", lambda e, rrow=rrow, lnrow=lnrow:
                              e.activation(rrow, lnrow, AF.Exp,
                                           scale=-1.0),
                              waits=[rc1], inc="ACT")
                    rb = P.op("tensor", lambda e, rrow=rrow: e.matmul(
                        pb[6][0:64, :], c_or[0:1, 0:64], rrow,
                        start=True, stop=True),
                        waits=[rc, ev_rbc_used, ev_consts], inc="PE")
                    rcp = P.op("vector", lambda e, rsb=rsb: e.tensor_copy(
                        rsb, pb[6][0:64, :]), waits=[rb], inc="DVE")
                    ev_rbc_used = rcp
                    dst = yT_all[64 * head:64 * head + 64,
                                 p * NTOK + g * 512:p * NTOK + (g + 1) * 512]
                    ev_norm[head] = P.op(
                        "vector", lambda e, dst=dst, ypb=ypb, rsb=rsb:
                        e.tensor_mul(dst, pb[ypb][0:64, :], rsb),
                        inc="DVE")

        ev_yT_done = [ev_norm[0], ev_norm[1]]
        if tap == "yT":
            return finish_tap(yT_all[:], ev_yT_done)

        st_b.close()

        ev_xo = None
        for g in range(NG):
            for i in range(DT):
                src = xT[:, i * T + (2 * g) * 512:i * T + (2 * g) * 512 + 512]
                dst = xOwn2[:, i * NTOK + g * 512:i * NTOK + (g + 1) * 512]
                ev_xo = P.op("vector", lambda e, src=src, dst=dst:
                             e.tensor_copy(dst, src),
                             waits=[ev_inproj_done[NPAIR - 1]], inc="DVE")
        st_x.close()

        ev_b_done = [ev_inproj_done[NPAIR - 1]] + ev_yT_done + \
            [ev_rbc_used, ev_xo]

        s_f32 = sbl("s_f32", [128, DT * NTOK], F32)
        x2bf = sbl("x2bf", [128, DT * NTOK], BF16)
        gT_sb = sbl("gT_sb", [16, NTOK], BF16)
        c_esel = sbl("c_esel", [16, NE * 128], BF16)
        ev_esel = P.op("sync", dma_to(c_esel[:], c_esel_in.ap()),
                       waits=ev_b_done, inc="d_c", amt=16)

        st_c = contextlib.ExitStack()
        woutsb = sbr(st_c, "woutsb", [128, NPAIR * D], BF16)
        sqstage = sbr(st_c, "sqstage", [128, 512], F32)

        ev_wout = P.op("gpsimd", lambda e: e.dma_start(
            woutsb[:].rearrange("p (i d) -> p i d", i=DT),
            woutT_in.ap().rearrange("(i p) d -> p i d", p=128)),
            waits=ev_b_done, inc="d_wo", amt=16)

        ev_s = {}
        op_bank_free = [None, None]
        for g in range(NG):
            for d in range(DT):
                obk = d % 2
                for p in range(NPAIR):
                    lhs = woutsb[:, p * D + d * 128:p * D + (d + 1) * 128]
                    rhs = yT_all[:, p * NTOK + g * 512:
                                 p * NTOK + (g + 1) * 512]
                    P.op("tensor", lambda e, lhs=lhs, rhs=rhs, obk=obk,
                         st=(p == 0): e.matmul(pb[obk][:, 0:512], lhs, rhs,
                                               start=st, stop=False),
                         waits=[ev_wout, ev_yT_done[0], ev_yT_done[1],
                                op_bank_free[obk]])
                mm = P.op("tensor", lambda e, d=d, obk=obk: e.matmul(
                    pb[obk][:, 0:512], boutbf[0:1, d * 128:(d + 1) * 128],
                    c_o512[:], start=False, stop=True),
                    waits=[ev_consts, ev_cg], inc="PE")
                xo = xOwn2[:, d * NTOK + g * 512:d * NTOK + (g + 1) * 512]
                ssl = s_f32[:, d * NTOK + g * 512:d * NTOK + (g + 1) * 512]
                ev_s[(g, d)] = P.op("vector", lambda e, ssl=ssl, obk=obk,
                                    xo=xo: e.tensor_add(
                                        ssl, pb[obk][:, 0:512], xo),
                                    waits=[mm, ev_xo], inc="DVE")
                op_bank_free[obk] = ev_s[(g, d)]

        if tap == "s1":
            return finish_tap(s_f32[:], [ev_s[(NG - 1, DT - 1)]])

        ev_x2 = {}
        sq_free = None
        for g in range(NG):
            st_mm = None
            sq_mm = None
            for d in range(DT):
                ssl = s_f32[:, d * NTOK + g * 512:d * NTOK + (g + 1) * 512]
                sq = P.op("vector", lambda e, ssl=ssl: e.tensor_mul(
                    sqstage[:], ssl, ssl),
                    waits=[ev_s[(g, d)], sq_free], inc="DVE")
                st_mm = P.op("tensor", lambda e, ssl=ssl, st=(d == 0),
                             sp=(d == DT - 1): e.matmul(
                                 pb[2][0:1, :], c_oc[:], ssl, start=st,
                                 stop=sp),
                             waits=[ev_s[(g, d)], ev_consts], inc="PE")
                sq_mm = P.op("tensor", lambda e, st=(d == 0),
                             sp=(d == DT - 1): e.matmul(
                                 pb[3][0:1, :], c_oc[:], sqstage[:],
                                 start=st, stop=sp),
                             waits=[sq], inc="PE")
                sq_free = sq_mm
            ch = P.op("vector", lambda e: e.tensor_scalar_mul(
                wrow[0:1, 0:512], pb[2][0:1, :], 1.0 / D), waits=[st_mm],
                inc="DVE")
            for fng in (
                lambda e: e.tensor_scalar_mul(wrow[0:1, 512:1024],
                                              pb[3][0:1, :], 1.0 / D),
                lambda e: e.tensor_mul(wrow[0:1, 1024:1536], wrow[0:1, 0:512],
                                       wrow[0:1, 0:512]),
                lambda e: e.tensor_sub(wrow[0:1, 512:1024],
                                       wrow[0:1, 512:1024],
                                       wrow[0:1, 1024:1536]),
                lambda e: e.tensor_scalar_add(wrow[0:1, 512:1024],
                                              wrow[0:1, 512:1024], LN_EPS),
            ):
                ch = P.op("vector", fng, waits=[sq_mm], hard=[ch], inc="DVE")
            ve_r = ch
            P.op("scalar", lambda e: e.activation(
                wrow[0:1, 1024:1536], wrow[0:1, 512:1024], AF.Ln),
                waits=[ve_r])
            inv_r = P.op("scalar", lambda e: e.activation(
                wrow[0:1, 1536:2048], wrow[0:1, 1024:1536], AF.Exp,
                scale=-0.5), inc="ACT")
            mb = P.op("tensor", lambda e: e.matmul(
                pb[4][:, 0:512], c_or[:], wrow[0:1, 0:512], start=True,
                stop=True), waits=[ve_r, ev_consts], inc="PE")
            ib = P.op("tensor", lambda e: e.matmul(
                pb[5][:, 0:512], c_or[:], wrow[0:1, 1536:2048], start=True,
                stop=True), waits=[inv_r, ev_consts], inc="PE")
            last_n = None
            for d in range(DT):
                ssl = s_f32[:, d * NTOK + g * 512:d * NTOK + (g + 1) * 512]
                P.op("vector", lambda e, ssl=ssl: e.tensor_sub(
                    ssl, ssl, pb[4][:, 0:512]), waits=[mb])
                last_n = P.op("vector", lambda e, ssl=ssl: e.tensor_mul(
                    ssl, ssl, pb[5][:, 0:512]), waits=[ib], inc="DVE")
                x2sl = x2bf[:, d * NTOK + g * 512:d * NTOK + (g + 1) * 512]
                ev_x2[(g, d)] = P.op("scalar", lambda e, ssl=ssl, x2sl=x2sl:
                                     e.copy(x2sl, ssl), waits=[last_n],
                                     inc="ACT")
            ev_s[(g, "ln")] = last_n

        ev_c_done = [ev_s[(NG - 1, "ln")], ev_x2[(NG - 1, DT - 1)]]
        if tap == "x2":
            return finish_tap(s_f32[:], ev_c_done)

        st_c.close()
        st_ac.close()

        gw = gwork
        ev_gT = {}
        ev_pb1_free = None
        ev_exl_prev = None
        for tt in range(16):
            g = tt // 4
            mm = None
            for d in range(DT):
                lhs = s_f32[:, d * NTOK + tt * 128:d * NTOK + (tt + 1) * 128]
                mm = P.op("tensor", lambda e, lhs=lhs, d=d, st=(d == 0),
                          sp=(d == DT - 1): e.matmul(
                              pb[0][:, 0:16], lhs,
                              gatef[:, d * NE:(d + 1) * NE], start=st,
                              stop=sp),
                          waits=[ev_s[(g, "ln")], ev_consts, ev_exl_prev],
                          inc="PE" if d == DT - 1 else None)
            nm = P.op("vector", lambda e: e.reduce_max(
                gw[:, 80:81], pb[0][:, 0:16], axis=AX.X, negate=True),
                waits=[mm], inc="DVE")
            exl = P.op("scalar", lambda e: e.activation(
                gw[:, 0:16], pb[0][:, 0:16], AF.Exp, bias=gw[:, 80:81]),
                waits=[nm, mm], inc="ACT")
            ev_exl_prev = exl
            ch = exl
            gate_ops = [
                lambda e: e.reduce_sum(gw[:, 81:82], gw[:, 0:16], axis=AX.X),
                lambda e: e.reciprocal(gw[:, 82:83], gw[:, 81:82]),
                lambda e: e.tensor_scalar_mul(gw[:, 16:32], gw[:, 0:16],
                                              gw[:, 82:83]),
                lambda e: e.reduce_max(gw[:, 83:84], gw[:, 16:32], axis=AX.X),
                lambda e: e.tensor_scalar(gw[:, 32:48], gw[:, 16:32],
                                          gw[:, 83:84], None, ALU.is_ge),
                lambda e: e.scalar_tensor_tensor(gw[:, 48:64], gw[:, 32:48],
                                                 -1e9, gw[:, 16:32],
                                                 ALU.mult, ALU.add),
                lambda e: e.reduce_max(gw[:, 84:85], gw[:, 48:64], axis=AX.X),
                lambda e: e.tensor_scalar(gw[:, 32:48], gw[:, 48:64],
                                          gw[:, 84:85], None, ALU.is_ge),
                lambda e: e.scalar_tensor_tensor(gw[:, 48:64], gw[:, 32:48],
                                                 -1e9, gw[:, 48:64],
                                                 ALU.mult, ALU.add),
                lambda e: e.reduce_max(gw[:, 85:86], gw[:, 48:64], axis=AX.X),
                lambda e: e.tensor_scalar(gw[:, 32:48], gw[:, 16:32],
                                          gw[:, 85:86], None, ALU.is_ge),
                lambda e: e.tensor_mul(gw[:, 48:64], gw[:, 32:48],
                                       gw[:, 16:32]),
                lambda e: e.reduce_sum(gw[:, 86:87], gw[:, 48:64], axis=AX.X),
                lambda e: e.tensor_scalar_max(gw[:, 86:87], gw[:, 86:87],
                                              GATE_EPS),
                lambda e: e.reciprocal(gw[:, 87:88], gw[:, 86:87]),
                lambda e: e.tensor_scalar_mul(gw[:, 64:80], gw[:, 48:64],
                                              gw[:, 87:88]),
            ]
            for fng in gate_ops:
                ch = P.op("vector", fng, waits=[exl], hard=[ch], inc="DVE")
            gtl = ch
            gtr = P.op("tensor", lambda e: e.matmul(
                pb[1][0:16, 0:128], gw[:, 64:80], c_idf[:], start=True,
                stop=True), waits=[gtl, ev_pb1_free, ev_consts],
                inc="PE")
            ev_gT[tt] = P.op("vector", lambda e, tt=tt: e.tensor_copy(
                gT_sb[:, tt * 128:(tt + 1) * 128], pb[1][0:16, 0:128]),
                waits=[gtr], inc="DVE")
            ev_pb1_free = ev_gT[tt]
        ev_gT_done = ev_gT[15]

        if tap == "gT":
            return finish_tap(gT_sb[:], [ev_gT_done])
        if tap == "gw":
            return finish_tap(gwork[:], [ev_gT_done])

        st_e = contextlib.ExitStack()
        w1bf = sbr(st_e, "w1bf", [128, DT * HID], BF16)
        w2bf = sbr(st_e, "w2bf", [128, 16 * D], BF16)
        hgT = sbr(st_e, "hgT", [128, 16 * 512], BF16)
        xg = sbr(st_e, "xg", [128, DT * 512], BF16)
        b1bf = sbr(st_e, "b1bf", [33, HID], BF16)
        growbf = sbr(st_e, "growbf", [33, 1024], BF16)

        ev_w1 = [[None, None] for _ in range(NE + 1)]
        ev_w2 = [[None, None] for _ in range(NE + 1)]
        ev_b1 = [None] * (NE + 2)
        ev_ffn_space = ev_b_done + ev_c_done + [ev_gT_done]

        def dma_w1_half(e, ei, hlf):
            return e.dma_start(
                w1bf[:].rearrange("p (i m) -> p i m", i=DT)
                [:, :, hlf * 1024:(hlf + 1) * 1024],
                w1_in.ap()[ei].rearrange("(i p) m -> p i m", p=128)
                [:, :, hlf * 1024:(hlf + 1) * 1024])

        def dma_w2_half(e, ei, hlf):
            return e.dma_start(
                w2bf[:].rearrange("p (j d) -> p j d", j=16)
                [:, :, hlf * 512:(hlf + 1) * 512],
                w2_in.ap()[ei].rearrange("(j p) d -> p j d", p=128)
                [:, :, hlf * 512:(hlf + 1) * 512])

        for hlf in (0, 1):
            ev_w1[0][hlf] = P.op(
                "gpsimd", lambda e, hlf=hlf: dma_w1_half(e, 0, hlf),
                waits=ev_ffn_space, inc=f"d_w1{hlf}", amt=16)
            ev_w2[0][hlf] = P.op(
                "gpsimd", lambda e, hlf=hlf: dma_w2_half(e, 0, hlf),
                waits=ev_ffn_space, inc=f"d_w2{hlf}", amt=16)
        ev_b1[0] = P.op("gpsimd", lambda e: e.dma_start(
            b1bf[0:1, :], b1_in.ap()[0:1, :]),
            waits=ev_ffn_space, inc="d_b10", amt=16)
        ev_b1[1] = P.op("gpsimd", lambda e: e.dma_start(
            b1bf[32:33, :], b1_in.ap()[1:2, :]),
            waits=ev_ffn_space, inc="d_b11", amt=16)

        ev_b2p = {}
        b2_bank_free = [None, None]
        for cc in range(NG):
            for i in range(DT):
                ob = 5 + i % 2
                mm = P.op("tensor", lambda e, i=i, cc=cc, ob=ob: e.matmul(
                    pb[ob][:, 0:512], b2bf[:, i * 128:(i + 1) * 128],
                    gT_sb[:, cc * 512:(cc + 1) * 512], start=True, stop=True),
                    waits=[ev_consts, ev_cg, ev_gT_done,
                           b2_bank_free[i % 2]],
                    inc="PE")
                ssl = s_f32[:, i * NTOK + cc * 512:i * NTOK + (cc + 1) * 512]
                ev_b2p[(cc, i)] = P.op(
                    "vector", lambda e, ssl=ssl, ob=ob: e.tensor_add(
                        ssl, pb[ob][:, 0:512], ssl),
                    waits=[mm, ev_s[(cc, "ln")]], inc="DVE")
                b2_bank_free[i % 2] = ev_b2p[(cc, i)]

        ev_gb_free = None
        ev_xg_free = None
        ev_h_free = [None, None]
        ev_o_free = [None, None]
        ev_hg_free = {}
        ev_evac = {}
        for ei in range(NE):
            for cc in range(NG):
                gb = P.op("tensor", lambda e, ei=ei, cc=cc: e.matmul(
                    pb[0][:, 0:512], c_esel[:, ei * 128:(ei + 1) * 128],
                    gT_sb[:, cc * 512:(cc + 1) * 512], start=True, stop=True),
                    waits=[ev_gT_done, ev_gb_free, ev_esel], inc="PE")
                grw = P.op("vector", lambda e, cc=cc, ei=ei: e.tensor_copy(
                    growbf[(ei % 2) * 32:(ei % 2) * 32 + 1,
                           (cc % 2) * 512:(cc % 2) * 512 + 512],
                    pb[0][0:1, :]), waits=[gb], inc="DVE")
                xgl = None
                for i in range(DT):
                    x2sl = x2bf[:, i * NTOK + cc * 512:
                                i * NTOK + (cc + 1) * 512]
                    xgl = P.op("vector", lambda e, i=i, x2sl=x2sl:
                               e.tensor_mul(xg[:, i * 512:(i + 1) * 512],
                                            x2sl, pb[0][:, 0:512]),
                               waits=[gb, ev_xg_free, ev_x2[(cc, i)]],
                               inc="DVE" if i == DT - 1 else None)
                ev_gb_free = xgl
                m1 = None
                ev_relu_last = None
                for j in range(16):
                    hb = 1 + j % 2
                    for i in range(DT):
                        lhs = w1bf[:, i * HID + j * 128:
                                   i * HID + (j + 1) * 128]
                        P.op("tensor", lambda e, lhs=lhs, i=i, hb=hb,
                             st=(i == 0): e.matmul(
                                 pb[hb][:, 0:512], lhs,
                                 xg[:, i * 512:(i + 1) * 512], start=st,
                                 stop=False),
                             waits=[xgl, ev_w1[ei][j // 8],
                                    ev_h_free[hb - 1]])
                    m1 = P.op("tensor", lambda e, j=j, hb=hb, ei=ei, cc=cc:
                              e.matmul(
                                  pb[hb][:, 0:512],
                                  b1bf[(ei % 2) * 32:(ei % 2) * 32 + 1,
                                       j * 128:(j + 1) * 128],
                                  growbf[(ei % 2) * 32:(ei % 2) * 32 + 1,
                                         (cc % 2) * 512:
                                         (cc % 2) * 512 + 512],
                                  start=False, stop=True),
                              waits=[ev_b1[ei], grw], inc="PE")
                    rl = P.op("scalar", lambda e, j=j, hb=hb: e.activation(
                        hgT[:, j * 512:(j + 1) * 512], pb[hb][:, 0:512],
                        AF.Relu), waits=[m1, ev_hg_free.get(j)], inc="ACT")
                    ev_h_free[hb - 1] = rl
                    ev_relu_last = rl
                    if cc == NG - 1 and j == 7:
                        ev_w1A_used = m1
                ev_xg_free = m1
                m2 = None
                for i in range(DT):
                    ob = 3 + i % 2
                    for j in range(16):
                        lhs = w2bf[:, j * D + i * 128:j * D + (i + 1) * 128]
                        m2 = P.op("tensor", lambda e, lhs=lhs, ob=ob, j=j,
                                  st=(j == 0), sp=(j == 15): e.matmul(
                                      pb[ob][:, 0:512], lhs,
                                      hgT[:, j * 512:(j + 1) * 512],
                                      start=st, stop=sp),
                                  waits=[ev_relu_last, ev_w2[ei][i // 4],
                                         ev_o_free[ob - 3]],
                                  inc="PE" if j == 15 else None)
                    ssl = s_f32[:, i * NTOK + cc * 512:
                                i * NTOK + (cc + 1) * 512]
                    ee = P.op("vector", lambda e, ssl=ssl, ob=ob:
                              e.tensor_add(ssl, pb[ob][:, 0:512], ssl),
                              waits=[m2, ev_s[(cc, "ln")]], inc="DVE")
                    ev_evac[(ei, cc, i)] = ee
                    ev_o_free[ob - 3] = ee
                    if cc == NG - 1 and i == 3:
                        ev_w2C_used = m2
                for jj in range(16):
                    ev_hg_free[jj] = m2
            if ei + 1 < NE:
                ev_w1[ei + 1][0] = P.op(
                    "gpsimd", lambda e, ei=ei: dma_w1_half(e, ei + 1, 0),
                    waits=[ev_w1A_used], inc="d_w10", amt=16)
                ev_w1[ei + 1][1] = P.op(
                    "gpsimd", lambda e, ei=ei: dma_w1_half(e, ei + 1, 1),
                    waits=[ev_xg_free], inc="d_w11", amt=16)
                ev_w2[ei + 1][0] = P.op(
                    "gpsimd", lambda e, ei=ei: dma_w2_half(e, ei + 1, 0),
                    waits=[ev_w2C_used], inc="d_w20", amt=16)
                ev_w2[ei + 1][1] = P.op(
                    "gpsimd", lambda e, ei=ei: dma_w2_half(e, ei + 1, 1),
                    waits=[ev_hg_free[0]], inc="d_w21", amt=16)
                if ei + 2 < NE:
                    ev_b1[ei + 2] = P.op(
                        "gpsimd", lambda e, ei=ei: e.dma_start(
                            b1bf[(ei % 2) * 32:(ei % 2) * 32 + 1, :],
                            b1_in.ap()[ei + 2:ei + 3, :]),
                        waits=[ev_xg_free], inc=f"d_b1{ei % 2}", amt=16)
            if tap == "acc0" and ei == 0:
                return finish_tap(
                    s_f32[:], [ev_evac[(0, c2, i2)] for c2 in range(NG)
                               for i2 in range(DT)])

        if tap == "acc":
            return finish_tap(
                s_f32[:], [ev_evac[(NE - 1, c2, i2)] for c2 in range(NG)
                           for i2 in range(DT)])

        st_e.close()

        st_f = contextlib.ExitStack()
        xtok = sbr(st_f, "xtok", [128, 2 * D], F32)
        otile = sbr(st_f, "otile", [128, 2 * D], F32)
        sqf = sbr(st_f, "sqf", [128, D], F32)
        ev_out = [None] * 16
        ev_tr_free = [None, None]
        ev_sq_free = None
        ev_ot_free = [None, None]
        ev_f_space = [ev_evac[(NE - 1, NG - 1, DT - 1)], ev_hg_free[0]]
        for tt in range(16):
            par = tt % 2
            g = tt // 4
            cps = None
            for i in range(DT):
                trb = i % 2
                lhs = s_f32[:, i * NTOK + tt * 128:i * NTOK + (tt + 1) * 128]
                tr = P.op("tensor", lambda e, lhs=lhs, trb=trb: e.matmul(
                    pb[trb][:, 0:128], lhs, c_idf[:], start=True, stop=True),
                    waits=[ev_evac[(NE - 1, g, i)], ev_tr_free[trb],
                           ev_consts],
                    inc="PE")
                cps = P.op("vector", lambda e, i=i, par=par, trb=trb:
                           e.tensor_copy(
                               xtok[:, par * D + i * 128:
                                    par * D + (i + 1) * 128],
                               pb[trb][:, 0:128]),
                           waits=[tr] + (ev_f_space if tt == 0 else []),
                           inc="DVE")
                ev_tr_free[trb] = cps
            ch = P.op("vector", lambda e, par=par: e.tensor_mul(
                sqf[:], xtok[:, par * D:(par + 1) * D],
                xtok[:, par * D:(par + 1) * D]),
                waits=[cps, ev_sq_free], inc="DVE")
            for fng in (
                lambda e, par=par: e.reduce_sum(
                    fwork[:, 0:1], xtok[:, par * D:(par + 1) * D], axis=AX.X),
                lambda e: e.reduce_sum(fwork[:, 1:2], sqf[:], axis=AX.X),
                lambda e: e.tensor_scalar_mul(fwork[:, 2:3], fwork[:, 0:1],
                                              1.0 / D),
                lambda e: e.tensor_scalar_mul(fwork[:, 3:4], fwork[:, 1:2],
                                              1.0 / D),
                lambda e: e.tensor_mul(fwork[:, 4:5], fwork[:, 2:3],
                                       fwork[:, 2:3]),
                lambda e: e.tensor_sub(fwork[:, 5:6], fwork[:, 3:4],
                                       fwork[:, 4:5]),
                lambda e: e.tensor_scalar_add(fwork[:, 5:6], fwork[:, 5:6],
                                              LN_EPS),
            ):
                ch = P.op("vector", fng, hard=[ch], inc="DVE")
            vv = ch
            ev_sq_free = vv
            P.op("scalar", lambda e: e.activation(
                fwork[:, 6:7], fwork[:, 5:6], AF.Ln), waits=[vv])
            inv = P.op("scalar", lambda e: e.activation(
                fwork[:, 7:8], fwork[:, 6:7], AF.Exp, scale=-0.5), inc="ACT")
            ot = P.op("vector", lambda e, par=par: e.tensor_scalar(
                otile[:, par * D:(par + 1) * D],
                xtok[:, par * D:(par + 1) * D],
                fwork[:, 2:3], fwork[:, 7:8], ALU.subtract, ALU.mult),
                waits=[inv, ev_ot_free[par]], hard=[vv], inc="DVE")
            ev_out[tt] = P.op("sync", lambda e, tt=tt, par=par: e.dma_start(
                y_out.ap()[tt * 128:(tt + 1) * 128, :],
                otile[:, par * D:(par + 1) * D]),
                waits=[ot], inc=f"d_y{par}", amt=16)
            ev_ot_free[par] = ev_out[tt]

        tails = [(s, P.ctr[s]) for s in sorted(P.ctr) if s.startswith("d_")]
        P.op("sync", lambda e: e.nop(),
             waits=[ev_out[14], ev_out[15]] + tails)
        P.emit(stack, sim_serial=sim_serial)
    return nc


def _rope_tables():
    half = 32
    p = np.arange(half, dtype=np.float64) / (half - 1)
    thetas = 1.0 * (10000.0 / 1.0) ** p
    pos = np.arange(T, dtype=np.float64)[:, None] / thetas[None, :]
    sin = np.sin(pos).T
    cos = np.cos(pos).T
    cs = np.concatenate([sin, -sin, sin, -sin], axis=0)
    cw = np.concatenate([cos, cos, cos, cos], axis=0)
    return cs, cw


def _prog_perm(h):
    blocks = np.arange(8)
    if h == 1:
        blocks = blocks ^ 1
    return blocks


def _consts(h):
    idn = np.eye(128, dtype=np.float64)
    perm = np.zeros((128, 128), dtype=np.float64)
    for j in range(128):
        swap = j + 32 if (j % 64) < 32 else j - 32
        perm[swap, j] = 1.0
    cs, cw = _rope_tables()
    pos_prog = np.concatenate(
        [np.arange(512 * gb, 512 * gb + 512) for gb in _prog_perm(h)])
    msk = np.zeros((128, 4 * 512), dtype=np.float64)
    pcol = np.arange(128)[:, None]
    jcol = np.arange(512)[None, :]
    for u in range(4):
        msk[:, u * 512:(u + 1) * 512] = (128 * u + pcol <= jcol)
    mscal = np.full((128, 4), float(h), dtype=np.float32)
    kill = np.full((128, 1), 0.0 if h == 1 else 40000.0, dtype=np.float32)
    esel = np.zeros((16, NE * 128), dtype=np.float64)
    for e in range(NE):
        esel[e, e * 128:(e + 1) * 128] = 1.0
    hsel = np.zeros((128, 33), dtype=np.float64)
    hsel[0:64, 0] = 1.0
    hsel[64:128, 32] = 1.0
    bf = ml_dtypes.bfloat16
    return {
        "c_hsel": hsel.astype(bf),
        "c_idbf": idn.astype(bf), "c_idf": idn.astype(np.float32),
        "c_perm": perm.astype(bf),
        "c_csk": np.ascontiguousarray(cs[:, pos_prog]).astype(bf),
        "c_cwk": np.ascontiguousarray(cw[:, pos_prog]).astype(bf),
        "c_msk": msk.astype(bf), "c_mscal": mscal, "c_kill": kill,
        "c_esel": esel.astype(bf),
        "c_onesrow": np.ones((1, 128), np.float32),
        "c_or33": np.ones((33, 128), np.float32),
        "c_onescol": np.ones((128, 1), np.float32),
        "c_ones512": np.ones((1, 512), bf),
    }


def make_in_maps(L0_attn_x, L0_attn_W_in, L0_attn_b_in, L0_attn_W_out,
                 L0_attn_b_out, L0_ffn_gate, L0_ffn_W1, L0_ffn_b1, L0_ffn_W2,
                 L0_ffn_b2):
    x = np.ascontiguousarray(np.asarray(L0_attn_x, dtype=np.float32))
    W_in = np.asarray(L0_attn_W_in, dtype=np.float32)
    b_in = np.asarray(L0_attn_b_in, dtype=np.float32)
    W_out = np.asarray(L0_attn_W_out, dtype=np.float32)
    b_out = np.asarray(L0_attn_b_out, dtype=np.float32)

    winT = np.concatenate(
        [np.stack([W_in[128 * p:128 * (p + 1)].T,
                   W_in[1024 + 128 * p:1024 + 128 * (p + 1)].T,
                   W_in[2048 + 128 * p:2048 + 128 * (p + 1)].T], axis=1)
         .reshape(D, 384) for p in range(NPAIR)], axis=1)
    winT = np.ascontiguousarray(winT)
    binT = np.zeros((128, 24), dtype=np.float32)
    for p in range(NPAIR):
        binT[:, 3 * p] = b_in[128 * p:128 * (p + 1)]
        binT[:, 3 * p + 1] = b_in[1024 + 128 * p:1024 + 128 * (p + 1)]
        binT[:, 3 * p + 2] = b_in[2048 + 128 * p:2048 + 128 * (p + 1)]
    shared = dict(
        winT=winT, binT=binT,
        woutT=np.ascontiguousarray(W_out.T),
        bout=np.ascontiguousarray(b_out[None, :]),
        gate=np.ascontiguousarray(np.asarray(L0_ffn_gate, np.float32)),
        w1=np.ascontiguousarray(np.asarray(L0_ffn_W1, np.float32)),
        w2=np.ascontiguousarray(np.asarray(L0_ffn_W2, np.float32)),
        b1=np.ascontiguousarray(np.asarray(L0_ffn_b1, np.float32)),
        b2=np.ascontiguousarray(np.asarray(L0_ffn_b2, np.float32)),
    )
    consts = [_consts(0), _consts(1)]
    in_maps = []
    for c in range(8):
        b, h = c // 2, c % 2
        xb = x[b].reshape(8, 512, D)[_prog_perm(h)].reshape(T, D)
        m = dict(shared)
        m.update(consts[h])
        m["xin"] = np.ascontiguousarray(xb)
        in_maps.append(m)
    return in_maps


def kernel(L0_attn_x, L0_attn_W_in, L0_attn_b_in, L0_attn_W_out, L0_attn_b_out,
           L0_ffn_gate, L0_ffn_W1, L0_ffn_b1, L0_ffn_W2, L0_ffn_b2):
    in_maps = make_in_maps(L0_attn_x, L0_attn_W_in, L0_attn_b_in,
                           L0_attn_W_out, L0_attn_b_out, L0_ffn_gate,
                           L0_ffn_W1, L0_ffn_b1, L0_ffn_W2, L0_ffn_b2)
    nc = build_program()
    res = run_bass_kernel_spmd(nc, in_maps, core_ids=list(range(8)))
    out = np.empty((4, T, D), dtype=np.float32)
    for c in range(8):
        b, h = c // 2, c % 2
        yc = res.results[c]["y"]
        for g in range(NG):
            out[b, 512 * (2 * g + h):512 * (2 * g + h) + 512] = \
                yc[g * 512:(g + 1) * 512]
    return out


# version 07
# speedup vs baseline: 1.3128x; 1.3128x over previous
import contextlib

import numpy as np
import ml_dtypes
import concourse.bass as bass
import concourse.mybir as mybir
from concourse.bass_utils import run_bass_kernel_spmd

F32 = mybir.dt.float32
BF16 = mybir.dt.bfloat16
AF = mybir.ActivationFunctionType
ALU = mybir.AluOpType
AX = mybir.AxisListType

T = 4096
D = 1024
NPAIR = 8
DT = 8
NE = 16
HID = 2048
NTOK = 2048
NG = 4
LN_EPS = 1e-5
GATE_EPS = 1e-12


class Prog:
    ENGINES = ("sync", "gpsimd", "scalar", "vector", "tensor")
    ENG_OF_SEM = {"PE": "tensor", "ACT": "scalar", "DVE": "vector",
                  "PL": "gpsimd", "SP": "sync"}

    def __init__(self, nc):
        self.nc = nc
        self.ops = []
        self.ctr = {}
        self.sem_names = set()

    def op(self, eng, fn, waits=(), inc=None, amt=1, hard=()):
        evs = tuple(w for w in waits if w is not None)
        hevs = tuple(w for w in hard if w is not None)
        for (s, _v) in evs + hevs:
            self.sem_names.add(s)
        ev = None
        if inc is not None:
            self.ctr[inc] = self.ctr.get(inc, 0) + amt
            self.sem_names.add(inc)
            ev = (inc, self.ctr[inc])
        self.ops.append((eng, fn, evs, inc, amt, ev, hevs))
        return ev

    SER_ENGINES = ("vector", "scalar", "gpsimd")

    def emit(self, stack, sim_serial=False):
        for eng in self.ENGINES:
            if (sim_serial or eng in self.SER_ENGINES) and any(
                    o[0] == eng for o in self.ops):
                self.sem_names.add("ser_" + eng)
        sems = {s: stack.enter_context(self.nc.semaphore(s))
                for s in sorted(self.sem_names)}
        eng_of = self.ENG_OF_SEM
        with self.nc.Block() as blk:
            for eng_name in self.ENGINES:
                ops = [o for o in self.ops if o[0] == eng_name]
                if not ops:
                    continue

                ser = sim_serial or eng_name in self.SER_ENGINES

                def body(e, ops=ops, eng_name=eng_name, ser=ser):
                    waited = {}
                    ser_val = 0
                    last_ev = None
                    for (_eng, fn, evs, inc, amt, ev, hevs) in ops:
                        for (s, v) in evs:
                            if not ser and eng_of.get(s) == eng_name:
                                continue
                            if waited.get(s, 0) >= v:
                                continue
                            e.wait_ge(sems[s], v)
                            waited[s] = v
                        for (s, v) in hevs:
                            if waited.get(s, 0) >= v:
                                continue
                            e.wait_ge(sems[s], v)
                            waited[s] = v
                        if ser and last_ev is not None:
                            s, v = last_ev
                            if waited.get(s, 0) < v:
                                e.wait_ge(sems[s], v)
                                waited[s] = v
                        ins = fn(e)
                        if ins is not None:
                            if inc is not None:
                                ins.then_inc(sems[inc], amt)
                            is_dma = (inc or "").startswith("d_")
                            if ser and not is_dma:
                                if inc is None:
                                    ser_val += 1
                                    ins.then_inc(
                                        sems["ser_" + eng_name], 1)
                                    last_ev = ("ser_" + eng_name, ser_val)
                                else:
                                    last_ev = ev

                getattr(blk, eng_name)(body)


def build_program(tap=None, sim_serial=False):
    nc = bass.Bass("TRN2", target_bir_lowering=False)

    xin = nc.dram_tensor("xin", [T, D], F32, kind="ExternalInput")
    winT_in = nc.dram_tensor("winT", [D, 3072], F32, kind="ExternalInput")
    binT_in = nc.dram_tensor("binT", [128, 24], F32, kind="ExternalInput")
    woutT_in = nc.dram_tensor("woutT", [D, D], F32, kind="ExternalInput")
    bout_in = nc.dram_tensor("bout", [1, D], F32, kind="ExternalInput")
    gate_in = nc.dram_tensor("gate", [D, NE], F32, kind="ExternalInput")
    w1_in = nc.dram_tensor("w1", [NE, D, HID], F32, kind="ExternalInput")
    w2_in = nc.dram_tensor("w2", [NE, HID, D], F32, kind="ExternalInput")
    b1_in = nc.dram_tensor("b1", [NE, HID], F32, kind="ExternalInput")
    b2_in = nc.dram_tensor("b2", [NE, D], F32, kind="ExternalInput")
    c_idbf_in = nc.dram_tensor("c_idbf", [128, 128], BF16, kind="ExternalInput")
    c_idf_in = nc.dram_tensor("c_idf", [128, 128], F32, kind="ExternalInput")
    c_perm_in = nc.dram_tensor("c_perm", [128, 128], BF16, kind="ExternalInput")
    c_csk_in = nc.dram_tensor("c_csk", [128, T], BF16, kind="ExternalInput")
    c_cwk_in = nc.dram_tensor("c_cwk", [128, T], BF16, kind="ExternalInput")
    c_msk_in = nc.dram_tensor("c_msk", [128, 4 * 512], BF16,
                              kind="ExternalInput")
    c_mscal_in = nc.dram_tensor("c_mscal", [128, 4], F32, kind="ExternalInput")
    c_kill_in = nc.dram_tensor("c_kill", [128, 1], F32, kind="ExternalInput")
    c_esel_in = nc.dram_tensor("c_esel", [16, NE * 128], BF16,
                               kind="ExternalInput")
    c_or_in = nc.dram_tensor("c_onesrow", [1, 128], F32, kind="ExternalInput")
    c_hsel_in = nc.dram_tensor("c_hsel", [128, 33], BF16, kind="ExternalInput")
    c_or33_in = nc.dram_tensor("c_or33", [33, 128], F32, kind="ExternalInput")
    c_oc_in = nc.dram_tensor("c_onescol", [128, 1], F32, kind="ExternalInput")
    c_o512_in = nc.dram_tensor("c_ones512", [1, 512], BF16,
                               kind="ExternalInput")

    y_out = nc.dram_tensor("y", [NTOK, D], F32, kind="ExternalOutput")
    tap_shapes = {
        "xT": ([128, DT * T], BF16),
        "qT": ([128, NTOK], BF16), "kT": ([128, T], BF16),
        "V": ([128, 32 * 130], BF16), "yT": ([128, NPAIR * NTOK], BF16),
        "s1": ([128, DT * NTOK], F32), "x2": ([128, DT * NTOK], F32),
        "gT": ([16, NTOK], BF16), "gw": ([128, 96], F32),
        "acc0": ([128, DT * NTOK], F32),
        "acc": ([128, DT * NTOK], F32),
    }
    tap_out = None
    if tap is not None:
        shp, tdt = tap_shapes[tap]
        tap_out = nc.dram_tensor("tapout", shp, tdt, kind="ExternalOutput")

    P = Prog(nc)
    stack = contextlib.ExitStack()

    def sbl(name, shape, dt):
        return stack.enter_context(nc.sbuf_tensor("sb_" + name, shape, dt))

    def finish_tap(src_ap, waits):
        ev = P.op("sync", lambda e: e.dma_start(tap_out.ap(), src_ap),
                  waits=waits, inc="d_y", amt=16)
        tails = [(s, P.ctr[s]) for s in sorted(P.ctr) if s.startswith("d_")]
        P.op("sync", lambda e: e.nop(), waits=[ev] + tails)
        P.emit(stack, sim_serial=sim_serial)
        stack.close()
        return nc

    with stack:
        c_idbf = sbl("c_idbf", [128, 128], BF16)
        c_idf = sbl("c_idf", [128, 128], F32)
        c_perm = sbl("c_perm", [128, 128], BF16)
        c_or = sbl("c_or", [1, 128], F32)
        c_oc = sbl("c_oc", [128, 1], F32)
        c_o512 = sbl("c_o512", [1, 512], BF16)
        binT = sbl("binT", [128, 24], F32)
        boutbf = sbl("boutbf", [1, D], BF16)
        b2bf = sbl("b2bf", [16, D], BF16)
        gatef = sbl("gatef", [128, DT * NE], F32)
        c_mscal = sbl("c_mscal", [128, 4], F32)
        c_kill = sbl("c_kill", [128, 1], F32)
        c_hsel = sbl("c_hsel", [128, 33], BF16)
        c_or33 = sbl("c_or33", [33, 128], F32)
        normcol = sbl("normcol", [33, 16], F32)
        gwork = sbl("gwork", [128, 96], F32)
        fwork = sbl("fwork", [128, 8], F32)
        negMcol = sbl("negMcol", [128, 4], F32)
        mwork = sbl("mwork", [1, 8], F32)

        pbd = [stack.enter_context(nc.psum_tensor(f"pbd{i}", [128, 1024],
                                                   F32)) for i in range(2)]
        pbs = [stack.enter_context(nc.psum_tensor(f"pb{i}", [128, 512], F32))
               for i in range(4, 8)]
        pb = [pbd[0][:, 0:512], pbd[0][:, 512:1024],
              pbd[1][:, 0:512], pbd[1][:, 512:1024]] + \
             [t[:] for t in pbs]

        st_ac = contextlib.ExitStack()

        def sbr(st, name, shape, dt):
            return st.enter_context(
                nc.sbuf_tensor("sb_" + name, shape, dt, side="right"))

        yT_all = sbr(st_ac, "yT_all", [128, NPAIR * NTOK], BF16)
        wrow = sbr(st_ac, "wrow", [1, 2048], F32)
        xOwn2 = sbr(st_ac, "xOwn2", [128, DT * NTOK], BF16)
        st_x = contextlib.ExitStack()
        xT = sbr(st_x, "xT", [128, DT * T], BF16)

        def dma_to(dst, src_ap):
            return lambda e: e.dma_start(dst, src_ap)

        for dst, src in ((c_idbf, c_idbf_in), (c_idf, c_idf_in),
                         (c_perm, c_perm_in),
                         (c_or, c_or_in), (c_oc, c_oc_in), (c_o512, c_o512_in),
                         (binT, binT_in), (c_mscal, c_mscal_in),
                         (c_hsel, c_hsel_in), (c_or33, c_or33_in),
                         (c_kill, c_kill_in)):
            P.op("sync", dma_to(dst[:], src.ap()), inc="d_c", amt=16)
        ev_consts = P.op("sync", lambda e: e.dma_start(
            gatef[:].rearrange("p (i e) -> p i e", i=DT),
            gate_in.ap().rearrange("(i p) e -> p i e", p=128)),
            inc="d_c", amt=16)
        P.op("gpsimd", lambda e: e.dma_start(boutbf[:], bout_in.ap()),
             inc="d_cg", amt=16)
        ev_cg = P.op("gpsimd", lambda e: e.dma_start(b2bf[:], b2_in.ap()),
                     inc="d_cg", amt=16)

        from concourse.hw_specs import get_activation_tables
        set_idx = list(get_activation_tables(nc.m.arch)).index(
            "natural_log_exp_and_others")

        def load_tab(e):
            return e.add_instruction(mybir.InstLoadActFuncSet(
                name=nc.get_next_instruction_name(), act_func_set_id=set_idx,
                ins=[], outs=[]))
        P.op("scalar", load_tab)

        st_a = contextlib.ExitStack()
        astage = sbr(st_a, "astage", [128, 2 * 4096], BF16)
        ev_chunk_mm = [None] * 8
        evac_ev = [None] * 4
        last_evac = {"vector": None, "scalar": None}
        for c in range(8):
            par = c % 2
            ev_dx = P.op(
                "gpsimd", lambda e, c=c, par=par: e.dma_start(
                    astage[:, par * 4096:(par + 1) * 4096]
                    .rearrange("p (n d) -> p n d", n=4),
                    xin.ap()[c * 512:(c + 1) * 512, :]
                    .rearrange("(n p) d -> p n d", p=128)),
                waits=[ev_chunk_mm[c - 2]] if c >= 2 else [],
                inc=f"d_x{par}", amt=16)
            for nsub in range(4):
                for i in range(DT):
                    tl = nsub * DT + i
                    bank = tl % 4
                    src = astage[:, par * 4096 + nsub * 1024 + i * 128:
                                 par * 4096 + nsub * 1024 + (i + 1) * 128]
                    mm = P.op("tensor", lambda e, src=src, bank=bank:
                              e.matmul(pb[bank][:, 0:128], src, c_idbf[:],
                                       start=True, stop=True),
                              waits=[ev_dx, evac_ev[bank], ev_consts],
                              inc="PE")
                    ev_chunk_mm[c] = mm
                    dst = xT[:, i * T + c * 512 + nsub * 128:
                             i * T + c * 512 + (nsub + 1) * 128]
                    veng = "vector" if (tl % 2 == 0) else "scalar"
                    sem = "DVE" if veng == "vector" else "ACT"

                    def evac(e, dst=dst, bank=bank, veng=veng):
                        if veng == "vector":
                            return e.tensor_copy(dst, pb[bank][:, 0:128])
                        return e.copy(dst, pb[bank][:, 0:128])
                    evac_ev[bank] = P.op(veng, evac, waits=[mm], inc=sem)
                    last_evac[veng] = evac_ev[bank]
        ev_x_ready = [last_evac["vector"], last_evac["scalar"]]
        ev_a_pe = ev_chunk_mm[7]
        st_a.close()

        if tap == "xT":
            return finish_tap(xT[:], ev_x_ready)

        st_b = contextlib.ExitStack()
        kT = sbr(st_b, "kT", [128, T], BF16)
        V_sb = sbr(st_b, "V_sb", [128, 32 * 130], BF16)
        qT = sbr(st_b, "qT", [128, NTOK], BF16)
        winp = sbr(st_b, "winp", [128, 3072], BF16)
        masks = sbr(st_b, "masks", [128, 4 * 512], BF16)
        cs_k = sbr(st_b, "cs_k", [128, T], BF16)
        cw_k = sbr(st_b, "cw_k", [128, T], BF16)
        qkstage = sbr(st_b, "qkstage", [128, 2 * 512], BF16)
        vstage = sbr(st_b, "vstage", [128, 2 * 512], BF16)
        rtmp1 = sbr(st_b, "rtmp1", [128, 512], BF16)
        rtmp2 = sbr(st_b, "rtmp2", [128, 512], BF16)
        Abuf = sbr(st_b, "Abuf", [128, 4 * 1024], BF16)
        r1sb = rtmp1[0:64, :]
        r2sb = rtmp2[0:64, :]

        ev_btab = None
        for dst, src in ((masks, c_msk_in), (cs_k, c_csk_in),
                         (cw_k, c_cwk_in)):
            ev_btab = P.op("sync", dma_to(dst[:], src.ap()),
                           waits=[ev_a_pe] + ev_x_ready, inc="d_c", amt=16)

        ev_vones = P.op("vector", lambda e: e.memset(
            V_sb[:].rearrange("p (k two g) -> p k two g", two=2, g=65)
            [:, :, :, 64:65], 1.0), waits=[ev_a_pe])

        ev_winp = [None] * (NPAIR + 1)
        ev_winp[0] = P.op("gpsimd", lambda e: e.dma_start(
            winp[:].rearrange("p (i c) -> p i c", i=DT),
            winT_in.ap()[:, 0:384].rearrange("(i p) c -> p i c", p=128)),
            waits=[ev_consts, ev_a_pe] + ev_x_ready, inc="d_wp", amt=16)

        ev_inproj_done = [None] * NPAIR
        ev_norm = [None, None]
        ev_rbc_used = None
        b_free = [None, None, None, None]
        ev_vcopy = None
        ev_stage_free = [None, None]
        ev_vstage_free = [None, None]

        for p in range(NPAIR):
            ev_rope_q = None
            ev_rope_k = None
            last_inproj_mm = None
            pend = None

            chunk_list = ([(0, g) for g in range(NG)]
                          + [(1, kc) for kc in range(8)]
                          + [(2, kc) for kc in range(8)])

            for idx, (kind, ci) in enumerate(chunk_list):
                bank = idx % 2
                par = idx % 2
                tok0 = (2 * ci) * 512 if kind == 0 else ci * 512
                rhs_l = [xT[:, i * T + tok0:i * T + tok0 + 512]
                         for i in range(DT)]
                mm = None
                for i in range(DT):
                    lhs = winp[:, i * 384 + kind * 128:
                               i * 384 + (kind + 1) * 128]
                    mm = P.op("tensor", lambda e, lhs=lhs, rhs=rhs_l[i],
                              bank=bank, st=(i == 0), sp=(i == DT - 1):
                              e.matmul(pb[bank][:, 0:512], lhs, rhs,
                                       start=st, stop=sp),
                              waits=[ev_winp[p], b_free[bank]] + ev_x_ready,
                              inc="PE" if i == DT - 1 else None)
                last_inproj_mm = mm

                def make_stage2(kind=kind, ci=ci, bank=bank, par=par, mm=mm,
                                p=p, tok0=tok0):
                    def stage2():
                        nonlocal ev_rope_q, ev_rope_k, ev_vcopy
                        bcol = binT[:, 3 * p + kind:3 * p + kind + 1]
                        if kind < 2:
                            stg = qkstage[:, par * 512:(par + 1) * 512]
                            evs = P.op("scalar", lambda e, bank=bank,
                                       bcol=bcol, stg=stg: e.activation(
                                           stg, pb[bank][:, 0:512],
                                           AF.Identity, bias=bcol),
                                       waits=[mm, ev_stage_free[par],
                                              ev_btab], inc="ACT")
                            b_free[bank] = evs
                            swb = 2 + par
                            sw = P.op("tensor", lambda e, swb=swb, stg=stg:
                                      e.matmul(pb[swb][:, 0:512], c_perm[:],
                                               stg, start=True, stop=True),
                                      waits=[evs, b_free[swb], ev_consts],
                                      inc="PE")
                            dst = (qT if kind == 0 else kT)
                            off = ci * 512
                            P.op("vector", lambda e, tok0=tok0, stg=stg:
                                 e.tensor_mul(rtmp1[:], stg,
                                              cs_k[:, tok0:tok0 + 512]),
                                 waits=[evs, ev_btab])
                            P.op("vector", lambda e, tok0=tok0, swb=swb:
                                 e.tensor_mul(rtmp2[:], pb[swb][:, 0:512],
                                              cw_k[:, tok0:tok0 + 512]),
                                 waits=[sw, ev_btab])
                            t3 = P.op("vector", lambda e, dst=dst, off=off:
                                      e.tensor_add(dst[:, off:off + 512],
                                                   rtmp1[:], rtmp2[:]),
                                      inc="DVE")
                            b_free[swb] = t3
                            ev_stage_free[par] = t3
                            if kind == 0:
                                ev_rope_q = t3
                            else:
                                ev_rope_k = t3
                        else:
                            vstg = vstage[:, par * 512:(par + 1) * 512]
                            evs = P.op("scalar", lambda e, bank=bank,
                                       bcol=bcol, vstg=vstg: e.activation(
                                           vstg, pb[bank][:, 0:512],
                                           AF.Identity, bias=bcol),
                                       waits=[mm, ev_vstage_free[par],
                                              ev_btab], inc="ACT")
                            b_free[bank] = evs
                            cp = None
                            for m in range(4):
                                swb = 2 + m % 2
                                tr = P.op("tensor", lambda e, m=m, swb=swb,
                                          vstg=vstg: e.matmul(
                                              pb[swb][:, 0:128],
                                              vstg[:, m * 128:(m + 1) * 128],
                                              c_idbf[:], start=True,
                                              stop=True),
                                          waits=[evs, b_free[swb],
                                                 ev_consts], inc="PE")
                                base = (ci * 4 + m) * 130

                                def vcopy(e, base=base, swb=swb):
                                    dst = V_sb[:, base:base + 130].rearrange(
                                        "p (two g) -> p two g",
                                        two=2)[:, :, 0:64]
                                    src = pb[swb][:, 0:128].rearrange(
                                        "p (two g) -> p two g", two=2)
                                    return e.tensor_copy(dst, src)
                                cp = P.op("vector", vcopy,
                                          waits=[tr, ev_vones], inc="DVE")
                                b_free[swb] = cp
                            ev_vstage_free[par] = cp
                            ev_vcopy = cp
                    return stage2

                if pend is not None:
                    pend()
                pend = make_stage2()
            pend()
            ev_inproj_done[p] = last_inproj_mm

            if p + 1 < NPAIR:
                ev_winp[p + 1] = P.op("gpsimd", lambda e, p=p: e.dma_start(
                    winp[:].rearrange("p (i c) -> p i c", i=DT),
                    winT_in.ap()[:, (p + 1) * 384:(p + 2) * 384]
                    .rearrange("(i p) c -> p i c", p=128)),
                    waits=[last_inproj_mm], inc="d_wp", amt=16)

            if tap == "qT" and p == 0:
                return finish_tap(qT[:], [ev_rope_q])
            if tap == "kT" and p == 0:
                return finish_tap(kT[:], [ev_rope_k])
            if tap == "V" and p == 0:
                return finish_tap(V_sb[:], [ev_vcopy])

            ev_stats = None
            for ci in range(12):
                src_buf = qT if ci < 4 else kT
                off = (ci * 512) if ci < 4 else ((ci - 4) * 512)
                spar = ci % 2
                sqev = P.op("vector", lambda e, src_buf=src_buf, off=off,
                            spar=spar: e.tensor_mul(
                                qkstage[:, spar * 512:(spar + 1) * 512],
                                src_buf[:, off:off + 512],
                                src_buf[:, off:off + 512]),
                            waits=[ev_rope_q if ci < 4 else ev_rope_k,
                                   ev_stats, ev_stage_free[spar]],
                            inc="DVE")
                smm = P.op("tensor", lambda e, spar=spar: e.matmul(
                    pb[6][0:33, 0:512], c_hsel[:],
                    qkstage[:, spar * 512:(spar + 1) * 512],
                    start=True, stop=True),
                    waits=[sqev, ev_rbc_used, ev_consts], inc="PE")
                ev_stats = P.op("vector", lambda e, ci=ci: e.reduce_max(
                    normcol[:, ci:ci + 1], pb[6][0:33, 0:512], axis=AX.X),
                    waits=[smm], inc="DVE")
                ev_rbc_used = ev_stats
                ev_stage_free[spar] = smm
            ch = ev_stats
            for fng in (
                lambda e: e.reduce_max(normcol[:, 12:13], normcol[:, 0:4],
                                       axis=AX.X),
                lambda e: e.reduce_max(normcol[:, 13:14], normcol[:, 4:12],
                                       axis=AX.X),
                lambda e: e.tensor_mul(normcol[:, 14:15], normcol[:, 12:13],
                                       normcol[:, 13:14]),
                lambda e: e.tensor_scalar_max(normcol[:, 14:15],
                                              normcol[:, 14:15], 1e-12),
            ):
                ch = P.op("vector", fng, hard=[ch], inc="DVE")
            lnp = ch
            ln1 = P.op("scalar", lambda e: e.activation(
                normcol[:, 15:16], normcol[:, 14:15], AF.Ln), waits=[lnp])
            sq1 = P.op("scalar", lambda e: e.activation(
                normcol[:, 14:15], normcol[:, 15:16], AF.Exp, scale=0.5),
                inc="ACT")
            ngm = P.op("vector", lambda e: e.tensor_scalar(
                normcol[:, 15:16], normcol[:, 14:15], -0.128875, -1.0,
                ALU.mult, ALU.add), waits=[sq1], hard=[lnp], inc="DVE")
            ev_negM = [None, None]
            for head in (0, 1):
                nb = P.op("tensor", lambda e, head=head: e.matmul(
                    pb[6][:, 0:1],
                    c_or33[32 * head:32 * head + 1, :],
                    normcol[32 * head:32 * head + 1, 15:16],
                    start=True, stop=True),
                    waits=[ngm, ev_rbc_used, ev_consts], inc="PE")
                evn = P.op(
                    "vector", lambda e, head=head: e.tensor_copy(
                        negMcol[:, head:head + 1], pb[6][:, 0:1]),
                    waits=[nb], inc="DVE")
                ev_negM[head] = P.op(
                    "vector", lambda e, head=head: e.tensor_sub(
                        negMcol[:, 2 + head:3 + head],
                        negMcol[:, head:head + 1], c_kill[:]),
                    waits=[ev_consts], hard=[evn], inc="DVE")
                ev_rbc_used = ev_negM[head]

            for g in range(NG):
                nkt = 8 * g + 8
                nvp = nkt // 2
                ev_exp = {}
                ev_av = {}
                ev_y_last = [None, None]

                def emit_av(vp, head, g=g, nkt=nkt, nvp=nvp):
                    ypb = 4 + head
                    av = None
                    ab = (head * 2 + vp % 2) * 1024
                    for v in (2 * vp, 2 * vp + 1):
                        vsl = V_sb[:, v * 130 + head * 65:
                                   v * 130 + head * 65 + 65]
                        asl = Abuf[:, ab + (v % 2) * 512:
                                   ab + (v % 2) * 512 + 512]
                        av = P.op("tensor", lambda e, ypb=ypb, vsl=vsl,
                                  asl=asl, st=(v == 0), sp=(v == nkt - 1):
                                  e.matmul(pb[ypb][0:65, :], vsl, asl,
                                           start=st, stop=sp),
                                  waits=[ev_exp[(head, vp)], ev_vcopy,
                                         ev_norm[head] if v == 0 else None],
                                  inc="PE" if v % 2 == 1 else None)
                    ev_av[(head, vp)] = av
                    if vp == nvp - 1:
                        ev_y_last[head] = av

                for vp in range(nvp):
                    v0 = 2 * vp
                    for head in (0, 1):
                        smm = None
                        for v in (v0, v0 + 1):
                            hb = head * 2 + v % 2
                            ks = kT[64 * head:64 * head + 64,
                                    v * 128:(v + 1) * 128]
                            qs = qT[64 * head:64 * head + 64,
                                    g * 512:(g + 1) * 512]
                            smm = P.op("tensor", lambda e, hb=hb, ks=ks,
                                       qs=qs, tp=(64 * head, 0): e.matmul(
                                           pb[hb][:, 0:512], ks, qs,
                                           start=True, stop=True,
                                           tile_position=tp),
                                       waits=[ev_rope_q, ev_rope_k,
                                              b_free[hb]],
                                       inc="PE" if v % 2 == 1 else None)
                        u = v0 - 8 * g
                        bcol_i = head if u < 4 else 2 + head
                        ab = (head * 2 + vp % 2) * 1024
                        asl2 = Abuf[:, ab:ab + 1024]
                        ex = P.op("scalar", lambda e, head=head, asl2=asl2,
                                  bcol_i=bcol_i: e.activation(
                                      asl2, pbd[head][:, 0:1024], AF.Exp,
                                      bias=negMcol[:, bcol_i:bcol_i + 1],
                                      scale=0.125),
                                  waits=[smm, ev_negM[head]],
                                  inc="ACT")
                        b_free[head * 2] = ex
                        b_free[head * 2 + 1] = ex
                        if 0 <= u < 4:
                            ex = P.op("vector",
                                      lambda e, asl2=asl2, u=u:
                                      e.tensor_mul(
                                          asl2, asl2,
                                          masks[:, u * 512:(u + 2) * 512]),
                                      waits=[ex, ev_btab], inc="DVE")
                        ev_exp[(head, vp)] = ex
                    if vp >= 1:
                        for head in (0, 1):
                            emit_av(vp - 1, head)
                for head in (0, 1):
                    emit_av(nvp - 1, head)

                for head in (0, 1):
                    ypb = 4 + head
                    rrow = wrow[0:1, head * 512:head * 512 + 512]
                    lnrow = wrow[0:1, 1024 + head * 512:1536 + head * 512]
                    rsb = r1sb if head == 0 else r2sb
                    rc1 = P.op("scalar", lambda e, ypb=ypb, lnrow=lnrow:
                               e.activation(lnrow, pb[ypb][64:65, :],
                                            AF.Ln),
                               waits=[ev_y_last[head]], inc="ACT")
                    rc = P.op("scalar", lambda e, rrow=rrow, lnrow=lnrow:
                              e.activation(rrow, lnrow, AF.Exp,
                                           scale=-1.0),
                              waits=[rc1], inc="ACT")
                    rb = P.op("tensor", lambda e, rrow=rrow: e.matmul(
                        pb[6][0:64, :], c_or[0:1, 0:64], rrow,
                        start=True, stop=True),
                        waits=[rc, ev_rbc_used, ev_consts], inc="PE")
                    rcp = P.op("vector", lambda e, rsb=rsb: e.tensor_copy(
                        rsb, pb[6][0:64, :]), waits=[rb], inc="DVE")
                    ev_rbc_used = rcp
                    dst = yT_all[64 * head:64 * head + 64,
                                 p * NTOK + g * 512:p * NTOK + (g + 1) * 512]
                    ev_norm[head] = P.op(
                        "vector", lambda e, dst=dst, ypb=ypb, rsb=rsb:
                        e.tensor_mul(dst, pb[ypb][0:64, :], rsb),
                        inc="DVE")

        ev_yT_done = [ev_norm[0], ev_norm[1]]
        if tap == "yT":
            return finish_tap(yT_all[:], ev_yT_done)

        st_b.close()

        ev_xo = None
        for g in range(NG):
            for i in range(DT):
                src = xT[:, i * T + (2 * g) * 512:i * T + (2 * g) * 512 + 512]
                dst = xOwn2[:, i * NTOK + g * 512:i * NTOK + (g + 1) * 512]
                ev_xo = P.op("vector", lambda e, src=src, dst=dst:
                             e.tensor_copy(dst, src),
                             waits=[ev_inproj_done[NPAIR - 1]], inc="DVE")
        st_x.close()

        ev_b_done = [ev_inproj_done[NPAIR - 1]] + ev_yT_done + \
            [ev_rbc_used, ev_xo]

        s_f32 = sbl("s_f32", [128, DT * NTOK], F32)
        x2bf = sbl("x2bf", [128, DT * NTOK], BF16)
        gT_sb = sbl("gT_sb", [16, NTOK], BF16)
        c_esel = sbl("c_esel", [16, NE * 128], BF16)
        ev_esel = P.op("sync", dma_to(c_esel[:], c_esel_in.ap()),
                       waits=ev_b_done, inc="d_c", amt=16)

        st_c = contextlib.ExitStack()
        woutsb = sbr(st_c, "woutsb", [128, NPAIR * D], BF16)
        sqstage = sbr(st_c, "sqstage", [128, 512], F32)

        ev_wout = P.op("gpsimd", lambda e: e.dma_start(
            woutsb[:].rearrange("p (i d) -> p i d", i=DT),
            woutT_in.ap().rearrange("(i p) d -> p i d", p=128)),
            waits=ev_b_done, inc="d_wo", amt=16)

        ev_s = {}
        op_bank_free = [None, None]
        for g in range(NG):
            for d in range(DT):
                obk = d % 2
                for p in range(NPAIR):
                    lhs = woutsb[:, p * D + d * 128:p * D + (d + 1) * 128]
                    rhs = yT_all[:, p * NTOK + g * 512:
                                 p * NTOK + (g + 1) * 512]
                    P.op("tensor", lambda e, lhs=lhs, rhs=rhs, obk=obk,
                         st=(p == 0): e.matmul(pb[obk][:, 0:512], lhs, rhs,
                                               start=st, stop=False),
                         waits=[ev_wout, ev_yT_done[0], ev_yT_done[1],
                                op_bank_free[obk]])
                mm = P.op("tensor", lambda e, d=d, obk=obk: e.matmul(
                    pb[obk][:, 0:512], boutbf[0:1, d * 128:(d + 1) * 128],
                    c_o512[:], start=False, stop=True),
                    waits=[ev_consts, ev_cg], inc="PE")
                xo = xOwn2[:, d * NTOK + g * 512:d * NTOK + (g + 1) * 512]
                ssl = s_f32[:, d * NTOK + g * 512:d * NTOK + (g + 1) * 512]
                ev_s[(g, d)] = P.op("vector", lambda e, ssl=ssl, obk=obk,
                                    xo=xo: e.tensor_add(
                                        ssl, pb[obk][:, 0:512], xo),
                                    waits=[mm, ev_xo], inc="DVE")
                op_bank_free[obk] = ev_s[(g, d)]

        if tap == "s1":
            return finish_tap(s_f32[:], [ev_s[(NG - 1, DT - 1)]])

        ev_x2 = {}
        sq_free = None
        for g in range(NG):
            st_mm = None
            sq_mm = None
            for d in range(DT):
                ssl = s_f32[:, d * NTOK + g * 512:d * NTOK + (g + 1) * 512]
                sq = P.op("vector", lambda e, ssl=ssl: e.tensor_mul(
                    sqstage[:], ssl, ssl),
                    waits=[ev_s[(g, d)], sq_free], inc="DVE")
                st_mm = P.op("tensor", lambda e, ssl=ssl, st=(d == 0),
                             sp=(d == DT - 1): e.matmul(
                                 pb[2][0:1, :], c_oc[:], ssl, start=st,
                                 stop=sp),
                             waits=[ev_s[(g, d)], ev_consts], inc="PE")
                sq_mm = P.op("tensor", lambda e, st=(d == 0),
                             sp=(d == DT - 1): e.matmul(
                                 pb[3][0:1, :], c_oc[:], sqstage[:],
                                 start=st, stop=sp),
                             waits=[sq], inc="PE")
                sq_free = sq_mm
            ch = P.op("vector", lambda e: e.tensor_scalar_mul(
                wrow[0:1, 0:512], pb[2][0:1, :], 1.0 / D), waits=[st_mm],
                inc="DVE")
            for fng in (
                lambda e: e.tensor_scalar_mul(wrow[0:1, 512:1024],
                                              pb[3][0:1, :], 1.0 / D),
                lambda e: e.tensor_mul(wrow[0:1, 1024:1536], wrow[0:1, 0:512],
                                       wrow[0:1, 0:512]),
                lambda e: e.tensor_sub(wrow[0:1, 512:1024],
                                       wrow[0:1, 512:1024],
                                       wrow[0:1, 1024:1536]),
                lambda e: e.tensor_scalar_add(wrow[0:1, 512:1024],
                                              wrow[0:1, 512:1024], LN_EPS),
            ):
                ch = P.op("vector", fng, waits=[sq_mm], hard=[ch], inc="DVE")
            ve_r = ch
            P.op("scalar", lambda e: e.activation(
                wrow[0:1, 1024:1536], wrow[0:1, 512:1024], AF.Ln),
                waits=[ve_r])
            inv_r = P.op("scalar", lambda e: e.activation(
                wrow[0:1, 1536:2048], wrow[0:1, 1024:1536], AF.Exp,
                scale=-0.5), inc="ACT")
            mb = P.op("tensor", lambda e: e.matmul(
                pb[4][:, 0:512], c_or[:], wrow[0:1, 0:512], start=True,
                stop=True), waits=[ve_r, ev_consts], inc="PE")
            ib = P.op("tensor", lambda e: e.matmul(
                pb[5][:, 0:512], c_or[:], wrow[0:1, 1536:2048], start=True,
                stop=True), waits=[inv_r, ev_consts], inc="PE")
            last_n = None
            for d in range(DT):
                ssl = s_f32[:, d * NTOK + g * 512:d * NTOK + (g + 1) * 512]
                P.op("vector", lambda e, ssl=ssl: e.tensor_sub(
                    ssl, ssl, pb[4][:, 0:512]), waits=[mb])
                last_n = P.op("vector", lambda e, ssl=ssl: e.tensor_mul(
                    ssl, ssl, pb[5][:, 0:512]), waits=[ib], inc="DVE")
                x2sl = x2bf[:, d * NTOK + g * 512:d * NTOK + (g + 1) * 512]
                ev_x2[(g, d)] = P.op("scalar", lambda e, ssl=ssl, x2sl=x2sl:
                                     e.copy(x2sl, ssl), waits=[last_n],
                                     inc="ACT")
            ev_s[(g, "ln")] = last_n

        ev_c_done = [ev_s[(NG - 1, "ln")], ev_x2[(NG - 1, DT - 1)]]
        if tap == "x2":
            return finish_tap(s_f32[:], ev_c_done)

        st_c.close()
        st_ac.close()

        gw = gwork
        ev_gT = {}
        ev_pb1_free = None
        ev_exl_prev = None
        for tt in range(16):
            g = tt // 4
            mm = None
            for d in range(DT):
                lhs = s_f32[:, d * NTOK + tt * 128:d * NTOK + (tt + 1) * 128]
                mm = P.op("tensor", lambda e, lhs=lhs, d=d, st=(d == 0),
                          sp=(d == DT - 1): e.matmul(
                              pb[0][:, 0:16], lhs,
                              gatef[:, d * NE:(d + 1) * NE], start=st,
                              stop=sp),
                          waits=[ev_s[(g, "ln")], ev_consts, ev_exl_prev],
                          inc="PE" if d == DT - 1 else None)
            nm = P.op("vector", lambda e: e.reduce_max(
                gw[:, 80:81], pb[0][:, 0:16], axis=AX.X, negate=True),
                waits=[mm], inc="DVE")
            exl = P.op("scalar", lambda e: e.activation(
                gw[:, 0:16], pb[0][:, 0:16], AF.Exp, bias=gw[:, 80:81]),
                waits=[nm, mm], inc="ACT")
            ev_exl_prev = exl
            ch = exl
            gate_ops = [
                lambda e: e.reduce_sum(gw[:, 81:82], gw[:, 0:16], axis=AX.X),
                lambda e: e.reciprocal(gw[:, 82:83], gw[:, 81:82]),
                lambda e: e.tensor_scalar_mul(gw[:, 16:32], gw[:, 0:16],
                                              gw[:, 82:83]),
                lambda e: e.reduce_max(gw[:, 83:84], gw[:, 16:32], axis=AX.X),
                lambda e: e.tensor_scalar(gw[:, 32:48], gw[:, 16:32],
                                          gw[:, 83:84], None, ALU.is_ge),
                lambda e: e.scalar_tensor_tensor(gw[:, 48:64], gw[:, 32:48],
                                                 -1e9, gw[:, 16:32],
                                                 ALU.mult, ALU.add),
                lambda e: e.reduce_max(gw[:, 84:85], gw[:, 48:64], axis=AX.X),
                lambda e: e.tensor_scalar(gw[:, 32:48], gw[:, 48:64],
                                          gw[:, 84:85], None, ALU.is_ge),
                lambda e: e.scalar_tensor_tensor(gw[:, 48:64], gw[:, 32:48],
                                                 -1e9, gw[:, 48:64],
                                                 ALU.mult, ALU.add),
                lambda e: e.reduce_max(gw[:, 85:86], gw[:, 48:64], axis=AX.X),
                lambda e: e.tensor_scalar(gw[:, 32:48], gw[:, 16:32],
                                          gw[:, 85:86], None, ALU.is_ge),
                lambda e: e.tensor_mul(gw[:, 48:64], gw[:, 32:48],
                                       gw[:, 16:32]),
                lambda e: e.reduce_sum(gw[:, 86:87], gw[:, 48:64], axis=AX.X),
                lambda e: e.tensor_scalar_max(gw[:, 86:87], gw[:, 86:87],
                                              GATE_EPS),
                lambda e: e.reciprocal(gw[:, 87:88], gw[:, 86:87]),
                lambda e: e.tensor_scalar_mul(gw[:, 64:80], gw[:, 48:64],
                                              gw[:, 87:88]),
            ]
            for fng in gate_ops:
                ch = P.op("vector", fng, waits=[exl], hard=[ch], inc="DVE")
            gtl = ch
            gtr = P.op("tensor", lambda e: e.matmul(
                pb[1][0:16, 0:128], gw[:, 64:80], c_idf[:], start=True,
                stop=True), waits=[gtl, ev_pb1_free, ev_consts],
                inc="PE")
            ev_gT[tt] = P.op("vector", lambda e, tt=tt: e.tensor_copy(
                gT_sb[:, tt * 128:(tt + 1) * 128], pb[1][0:16, 0:128]),
                waits=[gtr], inc="DVE")
            ev_pb1_free = ev_gT[tt]
        ev_gT_done = ev_gT[15]

        if tap == "gT":
            return finish_tap(gT_sb[:], [ev_gT_done])
        if tap == "gw":
            return finish_tap(gwork[:], [ev_gT_done])

        st_e = contextlib.ExitStack()
        w1bf = sbr(st_e, "w1bf", [128, DT * HID], BF16)
        w2bf = sbr(st_e, "w2bf", [128, 16 * D], BF16)
        hgT = sbr(st_e, "hgT", [128, 16 * 512], BF16)
        xg = sbr(st_e, "xg", [128, DT * 512], BF16)
        b1bf = sbr(st_e, "b1bf", [33, HID], BF16)
        growbf = sbr(st_e, "growbf", [33, 1024], BF16)

        ev_w1 = [[None, None] for _ in range(NE + 1)]
        ev_w2 = [[None, None] for _ in range(NE + 1)]
        ev_b1 = [None] * (NE + 2)
        ev_ffn_space = ev_b_done + ev_c_done + [ev_gT_done]

        def dma_w1_half(e, ei, hlf):
            return e.dma_start(
                w1bf[:].rearrange("p (i m) -> p i m", i=DT)
                [:, :, hlf * 1024:(hlf + 1) * 1024],
                w1_in.ap()[ei].rearrange("(i p) m -> p i m", p=128)
                [:, :, hlf * 1024:(hlf + 1) * 1024])

        def dma_w2_half(e, ei, hlf):
            return e.dma_start(
                w2bf[:].rearrange("p (j d) -> p j d", j=16)
                [:, :, hlf * 512:(hlf + 1) * 512],
                w2_in.ap()[ei].rearrange("(j p) d -> p j d", p=128)
                [:, :, hlf * 512:(hlf + 1) * 512])

        for hlf in (0, 1):
            ev_w1[0][hlf] = P.op(
                "gpsimd", lambda e, hlf=hlf: dma_w1_half(e, 0, hlf),
                waits=ev_ffn_space, inc=f"d_w1{hlf}", amt=16)
            ev_w2[0][hlf] = P.op(
                "gpsimd", lambda e, hlf=hlf: dma_w2_half(e, 0, hlf),
                waits=ev_ffn_space, inc=f"d_w2{hlf}", amt=16)
        ev_b1[0] = P.op("gpsimd", lambda e: e.dma_start(
            b1bf[0:1, :], b1_in.ap()[0:1, :]),
            waits=ev_ffn_space, inc="d_b10", amt=16)
        ev_b1[1] = P.op("gpsimd", lambda e: e.dma_start(
            b1bf[32:33, :], b1_in.ap()[1:2, :]),
            waits=ev_ffn_space, inc="d_b11", amt=16)

        ev_b2p = {}
        b2_bank_free = [None, None]
        for cc in range(NG):
            for i in range(DT):
                ob = 5 + i % 2
                mm = P.op("tensor", lambda e, i=i, cc=cc, ob=ob: e.matmul(
                    pb[ob][:, 0:512], b2bf[:, i * 128:(i + 1) * 128],
                    gT_sb[:, cc * 512:(cc + 1) * 512], start=True, stop=True),
                    waits=[ev_consts, ev_cg, ev_gT_done,
                           b2_bank_free[i % 2]],
                    inc="PE")
                ssl = s_f32[:, i * NTOK + cc * 512:i * NTOK + (cc + 1) * 512]
                ev_b2p[(cc, i)] = P.op(
                    "vector", lambda e, ssl=ssl, ob=ob: e.tensor_add(
                        ssl, pb[ob][:, 0:512], ssl),
                    waits=[mm, ev_s[(cc, "ln")]], inc="DVE")
                b2_bank_free[i % 2] = ev_b2p[(cc, i)]

        ev_gb_free = None
        ev_xg_free = None
        ev_h_free = [None, None]
        ev_o_free = [None, None]
        ev_hg_free = {}
        ev_evac = {}
        chunks = [(ei, cc) for ei in range(NE) for cc in range(NG)]
        xgl_t = {}
        grw_t = {}

        def emit_gxg(t):
            nonlocal ev_gb_free
            ei, cc = chunks[t]
            gb = P.op("tensor", lambda e, ei=ei, cc=cc: e.matmul(
                pb[0][:, 0:512], c_esel[:, ei * 128:(ei + 1) * 128],
                gT_sb[:, cc * 512:(cc + 1) * 512], start=True, stop=True),
                waits=[ev_gT_done, ev_gb_free, ev_esel], inc="PE")
            grw_t[t] = P.op("vector", lambda e, cc=cc, ei=ei: e.tensor_copy(
                growbf[(ei % 2) * 32:(ei % 2) * 32 + 1,
                       (cc % 2) * 512:(cc % 2) * 512 + 512],
                pb[0][0:1, :]), waits=[gb], inc="DVE")
            xgl = None
            for i in range(DT):
                x2sl = x2bf[:, i * NTOK + cc * 512:
                            i * NTOK + (cc + 1) * 512]
                xgl = P.op("vector", lambda e, i=i, x2sl=x2sl:
                           e.tensor_mul(xg[:, i * 512:(i + 1) * 512],
                                        x2sl, pb[0][:, 0:512]),
                           waits=[gb, ev_xg_free, ev_x2[(cc, i)]],
                           inc="DVE" if i == DT - 1 else None)
            ev_gb_free = xgl
            xgl_t[t] = xgl

        for t, (ei, cc) in enumerate(chunks):
                if t == 0:
                    emit_gxg(0)
                xgl = xgl_t.pop(t)
                grw = grw_t.pop(t)
                m1 = None
                ev_relu_last = None
                for j in range(16):
                    hb = 1 + j % 2
                    for i in range(DT):
                        lhs = w1bf[:, i * HID + j * 128:
                                   i * HID + (j + 1) * 128]
                        P.op("tensor", lambda e, lhs=lhs, i=i, hb=hb,
                             st=(i == 0): e.matmul(
                                 pb[hb][:, 0:512], lhs,
                                 xg[:, i * 512:(i + 1) * 512], start=st,
                                 stop=False),
                             waits=[xgl, ev_w1[ei][j // 8],
                                    ev_h_free[hb - 1]])
                    m1 = P.op("tensor", lambda e, j=j, hb=hb, ei=ei, cc=cc:
                              e.matmul(
                                  pb[hb][:, 0:512],
                                  b1bf[(ei % 2) * 32:(ei % 2) * 32 + 1,
                                       j * 128:(j + 1) * 128],
                                  growbf[(ei % 2) * 32:(ei % 2) * 32 + 1,
                                         (cc % 2) * 512:
                                         (cc % 2) * 512 + 512],
                                  start=False, stop=True),
                              waits=[ev_b1[ei], grw], inc="PE")
                    rl = P.op("scalar", lambda e, j=j, hb=hb: e.activation(
                        hgT[:, j * 512:(j + 1) * 512], pb[hb][:, 0:512],
                        AF.Relu), waits=[m1, ev_hg_free.get(j)], inc="ACT")
                    ev_h_free[hb - 1] = rl
                    ev_relu_last = rl
                    if cc == NG - 1 and j == 7:
                        ev_w1A_used = m1
                ev_xg_free = m1
                if t + 1 < len(chunks):
                    emit_gxg(t + 1)
                m2 = None
                for i in range(DT):
                    ob = 3 + i % 2
                    for j in range(16):
                        lhs = w2bf[:, j * D + i * 128:j * D + (i + 1) * 128]
                        m2 = P.op("tensor", lambda e, lhs=lhs, ob=ob, j=j,
                                  st=(j == 0), sp=(j == 15): e.matmul(
                                      pb[ob][:, 0:512], lhs,
                                      hgT[:, j * 512:(j + 1) * 512],
                                      start=st, stop=sp),
                                  waits=[ev_relu_last, ev_w2[ei][i // 4],
                                         ev_o_free[ob - 3]],
                                  inc="PE" if j == 15 else None)
                    ssl = s_f32[:, i * NTOK + cc * 512:
                                i * NTOK + (cc + 1) * 512]
                    ee = P.op("vector", lambda e, ssl=ssl, ob=ob:
                              e.tensor_add(ssl, pb[ob][:, 0:512], ssl),
                              waits=[m2, ev_s[(cc, "ln")]], inc="DVE")
                    ev_evac[(ei, cc, i)] = ee
                    ev_o_free[ob - 3] = ee
                    if cc == NG - 1 and i == 3:
                        ev_w2C_used = m2
                for jj in range(16):
                    ev_hg_free[jj] = m2
                if cc != NG - 1:
                    continue
                if ei + 1 < NE:
                    ev_w1[ei + 1][0] = P.op(
                        "gpsimd", lambda e, ei=ei: dma_w1_half(e, ei + 1, 0),
                        waits=[ev_w1A_used], inc="d_w10", amt=16)
                    ev_w1[ei + 1][1] = P.op(
                        "gpsimd", lambda e, ei=ei: dma_w1_half(e, ei + 1, 1),
                        waits=[ev_xg_free], inc="d_w11", amt=16)
                    ev_w2[ei + 1][0] = P.op(
                        "gpsimd", lambda e, ei=ei: dma_w2_half(e, ei + 1, 0),
                        waits=[ev_w2C_used], inc="d_w20", amt=16)
                    ev_w2[ei + 1][1] = P.op(
                        "gpsimd", lambda e, ei=ei: dma_w2_half(e, ei + 1, 1),
                        waits=[ev_hg_free[0]], inc="d_w21", amt=16)
                    if ei + 2 < NE:
                        ev_b1[ei + 2] = P.op(
                            "gpsimd", lambda e, ei=ei: e.dma_start(
                                b1bf[(ei % 2) * 32:(ei % 2) * 32 + 1, :],
                                b1_in.ap()[ei + 2:ei + 3, :]),
                            waits=[ev_xg_free], inc=f"d_b1{ei % 2}", amt=16)
                if tap == "acc0" and ei == 0:
                    return finish_tap(
                        s_f32[:], [ev_evac[(0, c2, i2)] for c2 in range(NG)
                                   for i2 in range(DT)])

        if tap == "acc":
            return finish_tap(
                s_f32[:], [ev_evac[(NE - 1, c2, i2)] for c2 in range(NG)
                           for i2 in range(DT)])

        st_e.close()

        st_f = contextlib.ExitStack()
        xtok = sbr(st_f, "xtok", [128, 2 * D], F32)
        otile = sbr(st_f, "otile", [128, 2 * D], F32)
        sqf = sbr(st_f, "sqf", [128, D], F32)
        ev_out = [None] * 16
        ev_tr_free = [None, None]
        ev_sq_free = None
        ev_ot_free = [None, None]
        ev_f_space = [ev_evac[(NE - 1, NG - 1, DT - 1)], ev_hg_free[0]]
        for tt in range(16):
            par = tt % 2
            g = tt // 4
            cps = None
            for i in range(DT):
                trb = i % 2
                lhs = s_f32[:, i * NTOK + tt * 128:i * NTOK + (tt + 1) * 128]
                tr = P.op("tensor", lambda e, lhs=lhs, trb=trb: e.matmul(
                    pb[trb][:, 0:128], lhs, c_idf[:], start=True, stop=True),
                    waits=[ev_evac[(NE - 1, g, i)], ev_tr_free[trb],
                           ev_consts],
                    inc="PE")
                cps = P.op("vector", lambda e, i=i, par=par, trb=trb:
                           e.tensor_copy(
                               xtok[:, par * D + i * 128:
                                    par * D + (i + 1) * 128],
                               pb[trb][:, 0:128]),
                           waits=[tr] + (ev_f_space if tt == 0 else []),
                           inc="DVE")
                ev_tr_free[trb] = cps
            ch = P.op("vector", lambda e, par=par: e.tensor_mul(
                sqf[:], xtok[:, par * D:(par + 1) * D],
                xtok[:, par * D:(par + 1) * D]),
                waits=[cps, ev_sq_free], inc="DVE")
            for fng in (
                lambda e, par=par: e.reduce_sum(
                    fwork[:, 0:1], xtok[:, par * D:(par + 1) * D], axis=AX.X),
                lambda e: e.reduce_sum(fwork[:, 1:2], sqf[:], axis=AX.X),
                lambda e: e.tensor_scalar_mul(fwork[:, 2:3], fwork[:, 0:1],
                                              1.0 / D),
                lambda e: e.tensor_scalar_mul(fwork[:, 3:4], fwork[:, 1:2],
                                              1.0 / D),
                lambda e: e.tensor_mul(fwork[:, 4:5], fwork[:, 2:3],
                                       fwork[:, 2:3]),
                lambda e: e.tensor_sub(fwork[:, 5:6], fwork[:, 3:4],
                                       fwork[:, 4:5]),
                lambda e: e.tensor_scalar_add(fwork[:, 5:6], fwork[:, 5:6],
                                              LN_EPS),
            ):
                ch = P.op("vector", fng, hard=[ch], inc="DVE")
            vv = ch
            ev_sq_free = vv
            P.op("scalar", lambda e: e.activation(
                fwork[:, 6:7], fwork[:, 5:6], AF.Ln), waits=[vv])
            inv = P.op("scalar", lambda e: e.activation(
                fwork[:, 7:8], fwork[:, 6:7], AF.Exp, scale=-0.5), inc="ACT")
            ot = P.op("vector", lambda e, par=par: e.tensor_scalar(
                otile[:, par * D:(par + 1) * D],
                xtok[:, par * D:(par + 1) * D],
                fwork[:, 2:3], fwork[:, 7:8], ALU.subtract, ALU.mult),
                waits=[inv, ev_ot_free[par]], hard=[vv], inc="DVE")
            ev_out[tt] = P.op("sync", lambda e, tt=tt, par=par: e.dma_start(
                y_out.ap()[tt * 128:(tt + 1) * 128, :],
                otile[:, par * D:(par + 1) * D]),
                waits=[ot], inc=f"d_y{par}", amt=16)
            ev_ot_free[par] = ev_out[tt]

        tails = [(s, P.ctr[s]) for s in sorted(P.ctr) if s.startswith("d_")]
        P.op("sync", lambda e: e.nop(),
             waits=[ev_out[14], ev_out[15]] + tails)
        P.emit(stack, sim_serial=sim_serial)
    return nc


def _rope_tables():
    half = 32
    p = np.arange(half, dtype=np.float64) / (half - 1)
    thetas = 1.0 * (10000.0 / 1.0) ** p
    pos = np.arange(T, dtype=np.float64)[:, None] / thetas[None, :]
    sin = np.sin(pos).T
    cos = np.cos(pos).T
    cs = np.concatenate([sin, -sin, sin, -sin], axis=0)
    cw = np.concatenate([cos, cos, cos, cos], axis=0)
    return cs, cw


def _prog_perm(h):
    blocks = np.arange(8)
    if h == 1:
        blocks = blocks ^ 1
    return blocks


def _consts(h):
    idn = np.eye(128, dtype=np.float64)
    perm = np.zeros((128, 128), dtype=np.float64)
    for j in range(128):
        swap = j + 32 if (j % 64) < 32 else j - 32
        perm[swap, j] = 1.0
    cs, cw = _rope_tables()
    pos_prog = np.concatenate(
        [np.arange(512 * gb, 512 * gb + 512) for gb in _prog_perm(h)])
    msk = np.zeros((128, 4 * 512), dtype=np.float64)
    pcol = np.arange(128)[:, None]
    jcol = np.arange(512)[None, :]
    for u in range(4):
        msk[:, u * 512:(u + 1) * 512] = (128 * u + pcol <= jcol)
    mscal = np.full((128, 4), float(h), dtype=np.float32)
    kill = np.full((128, 1), 0.0 if h == 1 else 40000.0, dtype=np.float32)
    esel = np.zeros((16, NE * 128), dtype=np.float64)
    for e in range(NE):
        esel[e, e * 128:(e + 1) * 128] = 1.0
    hsel = np.zeros((128, 33), dtype=np.float64)
    hsel[0:64, 0] = 1.0
    hsel[64:128, 32] = 1.0
    bf = ml_dtypes.bfloat16
    return {
        "c_hsel": hsel.astype(bf),
        "c_idbf": idn.astype(bf), "c_idf": idn.astype(np.float32),
        "c_perm": perm.astype(bf),
        "c_csk": np.ascontiguousarray(cs[:, pos_prog]).astype(bf),
        "c_cwk": np.ascontiguousarray(cw[:, pos_prog]).astype(bf),
        "c_msk": msk.astype(bf), "c_mscal": mscal, "c_kill": kill,
        "c_esel": esel.astype(bf),
        "c_onesrow": np.ones((1, 128), np.float32),
        "c_or33": np.ones((33, 128), np.float32),
        "c_onescol": np.ones((128, 1), np.float32),
        "c_ones512": np.ones((1, 512), bf),
    }


def make_in_maps(L0_attn_x, L0_attn_W_in, L0_attn_b_in, L0_attn_W_out,
                 L0_attn_b_out, L0_ffn_gate, L0_ffn_W1, L0_ffn_b1, L0_ffn_W2,
                 L0_ffn_b2):
    x = np.ascontiguousarray(np.asarray(L0_attn_x, dtype=np.float32))
    W_in = np.asarray(L0_attn_W_in, dtype=np.float32)
    b_in = np.asarray(L0_attn_b_in, dtype=np.float32)
    W_out = np.asarray(L0_attn_W_out, dtype=np.float32)
    b_out = np.asarray(L0_attn_b_out, dtype=np.float32)

    winT = np.concatenate(
        [np.stack([W_in[128 * p:128 * (p + 1)].T,
                   W_in[1024 + 128 * p:1024 + 128 * (p + 1)].T,
                   W_in[2048 + 128 * p:2048 + 128 * (p + 1)].T], axis=1)
         .reshape(D, 384) for p in range(NPAIR)], axis=1)
    winT = np.ascontiguousarray(winT)
    binT = np.zeros((128, 24), dtype=np.float32)
    for p in range(NPAIR):
        binT[:, 3 * p] = b_in[128 * p:128 * (p + 1)]
        binT[:, 3 * p + 1] = b_in[1024 + 128 * p:1024 + 128 * (p + 1)]
        binT[:, 3 * p + 2] = b_in[2048 + 128 * p:2048 + 128 * (p + 1)]
    shared = dict(
        winT=winT, binT=binT,
        woutT=np.ascontiguousarray(W_out.T),
        bout=np.ascontiguousarray(b_out[None, :]),
        gate=np.ascontiguousarray(np.asarray(L0_ffn_gate, np.float32)),
        w1=np.ascontiguousarray(np.asarray(L0_ffn_W1, np.float32)),
        w2=np.ascontiguousarray(np.asarray(L0_ffn_W2, np.float32)),
        b1=np.ascontiguousarray(np.asarray(L0_ffn_b1, np.float32)),
        b2=np.ascontiguousarray(np.asarray(L0_ffn_b2, np.float32)),
    )
    consts = [_consts(0), _consts(1)]
    in_maps = []
    for c in range(8):
        b, h = c // 2, c % 2
        xb = x[b].reshape(8, 512, D)[_prog_perm(h)].reshape(T, D)
        m = dict(shared)
        m.update(consts[h])
        m["xin"] = np.ascontiguousarray(xb)
        in_maps.append(m)
    return in_maps


def kernel(L0_attn_x, L0_attn_W_in, L0_attn_b_in, L0_attn_W_out, L0_attn_b_out,
           L0_ffn_gate, L0_ffn_W1, L0_ffn_b1, L0_ffn_W2, L0_ffn_b2):
    in_maps = make_in_maps(L0_attn_x, L0_attn_W_in, L0_attn_b_in,
                           L0_attn_W_out, L0_attn_b_out, L0_ffn_gate,
                           L0_ffn_W1, L0_ffn_b1, L0_ffn_W2, L0_ffn_b2)
    nc = build_program()
    res = run_bass_kernel_spmd(nc, in_maps, core_ids=list(range(8)))
    out = np.empty((4, T, D), dtype=np.float32)
    for c in range(8):
        b, h = c // 2, c % 2
        yc = res.results[c]["y"]
        for g in range(NG):
            out[b, 512 * (2 * g + h):512 * (2 * g + h) + 512] = \
                yc[g * 512:(g + 1) * 512]
    return out


# version 09
# speedup vs baseline: 1.3637x; 1.0388x over previous
import contextlib

import numpy as np
import ml_dtypes
import concourse.bass as bass
import concourse.mybir as mybir
from concourse.bass_utils import run_bass_kernel_spmd

F32 = mybir.dt.float32
BF16 = mybir.dt.bfloat16
AF = mybir.ActivationFunctionType
ALU = mybir.AluOpType
AX = mybir.AxisListType

T = 4096
D = 1024
NPAIR = 8
DT = 8
NE = 16
HID = 2048
NTOK = 2048
NG = 4
LN_EPS = 1e-5
GATE_EPS = 1e-12


class Prog:
    ENGINES = ("sync", "gpsimd", "scalar", "vector", "tensor")
    ENG_OF_SEM = {"PE": "tensor", "ACT": "scalar", "DVE": "vector",
                  "PL": "gpsimd", "SP": "sync"}

    def __init__(self, nc):
        self.nc = nc
        self.ops = []
        self.ctr = {}
        self.sem_names = set()

    def op(self, eng, fn, waits=(), inc=None, amt=1, hard=()):
        evs = tuple(w for w in waits if w is not None)
        hevs = tuple(w for w in hard if w is not None)
        for (s, _v) in evs + hevs:
            self.sem_names.add(s)
        ev = None
        if inc is not None:
            self.ctr[inc] = self.ctr.get(inc, 0) + amt
            self.sem_names.add(inc)
            ev = (inc, self.ctr[inc])
        self.ops.append((eng, fn, evs, inc, amt, ev, hevs))
        return ev

    SER_ENGINES = ("vector", "scalar", "gpsimd")

    def emit(self, stack, sim_serial=False):
        for eng in self.ENGINES:
            if (sim_serial or eng in self.SER_ENGINES) and any(
                    o[0] == eng for o in self.ops):
                self.sem_names.add("ser_" + eng)
        sems = {s: stack.enter_context(self.nc.semaphore(s))
                for s in sorted(self.sem_names)}
        eng_of = self.ENG_OF_SEM
        with self.nc.Block() as blk:
            for eng_name in self.ENGINES:
                ops = [o for o in self.ops if o[0] == eng_name]
                if not ops:
                    continue

                ser = sim_serial or eng_name in self.SER_ENGINES

                def body(e, ops=ops, eng_name=eng_name, ser=ser):
                    waited = {}
                    ser_val = 0
                    last_ev = None
                    for (_eng, fn, evs, inc, amt, ev, hevs) in ops:
                        for (s, v) in evs:
                            if not ser and eng_of.get(s) == eng_name:
                                continue
                            if waited.get(s, 0) >= v:
                                continue
                            e.wait_ge(sems[s], v)
                            waited[s] = v
                        for (s, v) in hevs:
                            if waited.get(s, 0) >= v:
                                continue
                            e.wait_ge(sems[s], v)
                            waited[s] = v
                        if ser and last_ev is not None:
                            s, v = last_ev
                            if waited.get(s, 0) < v:
                                e.wait_ge(sems[s], v)
                                waited[s] = v
                        ins = fn(e)
                        if ins is not None:
                            if inc is not None:
                                ins.then_inc(sems[inc], amt)
                            is_dma = (inc or "").startswith("d_")
                            if ser and not is_dma:
                                if inc is None:
                                    ser_val += 1
                                    ins.then_inc(
                                        sems["ser_" + eng_name], 1)
                                    last_ev = ("ser_" + eng_name, ser_val)
                                else:
                                    last_ev = ev

                getattr(blk, eng_name)(body)


def build_program(tap=None, sim_serial=False):
    nc = bass.Bass("TRN2", target_bir_lowering=False)

    xin = nc.dram_tensor("xin", [T, D], F32, kind="ExternalInput")
    winT_in = nc.dram_tensor("winT", [D, 3072], F32, kind="ExternalInput")
    binT_in = nc.dram_tensor("binT", [128, 24], F32, kind="ExternalInput")
    woutT_in = nc.dram_tensor("woutT", [D, D], F32, kind="ExternalInput")
    bout_in = nc.dram_tensor("bout", [1, D], F32, kind="ExternalInput")
    gate_in = nc.dram_tensor("gate", [D, NE], F32, kind="ExternalInput")
    w1_in = nc.dram_tensor("w1", [NE, D, HID], F32, kind="ExternalInput")
    w2_in = nc.dram_tensor("w2", [NE, HID, D], F32, kind="ExternalInput")
    b1_in = nc.dram_tensor("b1", [NE, HID], F32, kind="ExternalInput")
    b2_in = nc.dram_tensor("b2", [NE, D], F32, kind="ExternalInput")
    c_idbf_in = nc.dram_tensor("c_idbf", [128, 128], BF16, kind="ExternalInput")
    c_idf_in = nc.dram_tensor("c_idf", [128, 128], F32, kind="ExternalInput")
    c_perm_in = nc.dram_tensor("c_perm", [128, 128], BF16, kind="ExternalInput")
    c_csk_in = nc.dram_tensor("c_csk", [128, T], BF16, kind="ExternalInput")
    c_cwk_in = nc.dram_tensor("c_cwk", [128, T], BF16, kind="ExternalInput")
    c_msk_in = nc.dram_tensor("c_msk", [128, 4 * 512], BF16,
                              kind="ExternalInput")
    c_mscal_in = nc.dram_tensor("c_mscal", [128, 4], F32, kind="ExternalInput")
    c_kill_in = nc.dram_tensor("c_kill", [128, 1], F32, kind="ExternalInput")
    c_esel_in = nc.dram_tensor("c_esel", [16, NE * 128], BF16,
                               kind="ExternalInput")
    c_or_in = nc.dram_tensor("c_onesrow", [1, 128], F32, kind="ExternalInput")
    c_hsel_in = nc.dram_tensor("c_hsel", [128, 33], BF16, kind="ExternalInput")
    c_or33_in = nc.dram_tensor("c_or33", [33, 128], F32, kind="ExternalInput")
    c_oc_in = nc.dram_tensor("c_onescol", [128, 1], F32, kind="ExternalInput")
    c_o512_in = nc.dram_tensor("c_ones512", [1, 512], BF16,
                               kind="ExternalInput")

    y_out = nc.dram_tensor("y", [NTOK, D], F32, kind="ExternalOutput")
    tap_shapes = {
        "xT": ([128, DT * T], BF16),
        "qT": ([128, NTOK], BF16), "kT": ([128, T], BF16),
        "V": ([128, 32 * 130], BF16), "yT": ([128, NPAIR * NTOK], BF16),
        "s1": ([128, DT * NTOK], F32), "x2": ([128, DT * NTOK], F32),
        "gT": ([16, NTOK], BF16), "gw": ([128, 96], F32),
        "acc0": ([128, DT * NTOK], F32),
        "acc": ([128, DT * NTOK], F32),
    }
    tap_out = None
    if tap is not None:
        shp, tdt = tap_shapes[tap]
        tap_out = nc.dram_tensor("tapout", shp, tdt, kind="ExternalOutput")

    P = Prog(nc)
    stack = contextlib.ExitStack()

    def sbl(name, shape, dt):
        return stack.enter_context(nc.sbuf_tensor("sb_" + name, shape, dt))

    def finish_tap(src_ap, waits):
        ev = P.op("sync", lambda e: e.dma_start(tap_out.ap(), src_ap),
                  waits=waits, inc="d_y", amt=16)
        tails = [(s, P.ctr[s]) for s in sorted(P.ctr) if s.startswith("d_")]
        P.op("sync", lambda e: e.nop(), waits=[ev] + tails)
        P.emit(stack, sim_serial=sim_serial)
        stack.close()
        return nc

    with stack:
        c_idbf = sbl("c_idbf", [128, 128], BF16)
        c_idf = sbl("c_idf", [128, 128], F32)
        c_perm = sbl("c_perm", [128, 128], BF16)
        c_or = sbl("c_or", [1, 128], F32)
        c_oc = sbl("c_oc", [128, 1], F32)
        c_o512 = sbl("c_o512", [1, 512], BF16)
        binT = sbl("binT", [128, 24], F32)
        boutbf = sbl("boutbf", [1, D], BF16)
        b2bf = sbl("b2bf", [16, D], BF16)
        gatef = sbl("gatef", [128, DT * NE], F32)
        c_mscal = sbl("c_mscal", [128, 4], F32)
        c_kill = sbl("c_kill", [128, 1], F32)
        c_hsel = sbl("c_hsel", [128, 33], BF16)
        c_or33 = sbl("c_or33", [33, 128], F32)
        normcol = sbl("normcol", [33, 16], F32)
        gwork = sbl("gwork", [128, 96], F32)
        fwork = sbl("fwork", [128, 8], F32)
        negMcol = sbl("negMcol", [128, 4], F32)
        mwork = sbl("mwork", [1, 8], F32)

        pbd = [stack.enter_context(nc.psum_tensor(f"pbd{i}", [128, 1024],
                                                   F32)) for i in range(2)]
        pbs = [stack.enter_context(nc.psum_tensor(f"pb{i}", [128, 512], F32))
               for i in range(4, 8)]
        pb = [pbd[0][:, 0:512], pbd[0][:, 512:1024],
              pbd[1][:, 0:512], pbd[1][:, 512:1024]] + \
             [t[:] for t in pbs]

        st_ac = contextlib.ExitStack()

        def sbr(st, name, shape, dt):
            return st.enter_context(
                nc.sbuf_tensor("sb_" + name, shape, dt, side="right"))

        yT_all = sbr(st_ac, "yT_all", [128, NPAIR * NTOK], BF16)
        wrow = sbr(st_ac, "wrow", [1, 2048], F32)
        xOwn2 = sbr(st_ac, "xOwn2", [128, DT * NTOK], BF16)
        st_x = contextlib.ExitStack()
        xT = sbr(st_x, "xT", [128, DT * T], BF16)

        def dma_to(dst, src_ap):
            return lambda e: e.dma_start(dst, src_ap)

        for dst, src in ((c_idbf, c_idbf_in), (c_idf, c_idf_in),
                         (c_perm, c_perm_in),
                         (c_or, c_or_in), (c_oc, c_oc_in), (c_o512, c_o512_in),
                         (binT, binT_in), (c_mscal, c_mscal_in),
                         (c_hsel, c_hsel_in), (c_or33, c_or33_in),
                         (c_kill, c_kill_in)):
            P.op("sync", dma_to(dst[:], src.ap()), inc="d_c", amt=16)
        ev_consts = P.op("sync", lambda e: e.dma_start(
            gatef[:].rearrange("p (i e) -> p i e", i=DT),
            gate_in.ap().rearrange("(i p) e -> p i e", p=128)),
            inc="d_c", amt=16)
        P.op("gpsimd", lambda e: e.dma_start(boutbf[:], bout_in.ap()),
             inc="d_cg", amt=16)
        ev_cg = P.op("gpsimd", lambda e: e.dma_start(b2bf[:], b2_in.ap()),
                     inc="d_cg", amt=16)

        from concourse.hw_specs import get_activation_tables
        set_idx = list(get_activation_tables(nc.m.arch)).index(
            "natural_log_exp_and_others")

        def load_tab(e):
            return e.add_instruction(mybir.InstLoadActFuncSet(
                name=nc.get_next_instruction_name(), act_func_set_id=set_idx,
                ins=[], outs=[]))
        P.op("scalar", load_tab)

        st_a = contextlib.ExitStack()
        astage = sbr(st_a, "astage", [128, 2 * 4096], BF16)
        ev_chunk_mm = [None] * 8
        evac_ev = [None] * 4
        last_evac = {"vector": None, "scalar": None}
        for c in range(8):
            par = c % 2
            ev_dx = P.op(
                "gpsimd", lambda e, c=c, par=par: e.dma_start(
                    astage[:, par * 4096:(par + 1) * 4096]
                    .rearrange("p (n d) -> p n d", n=4),
                    xin.ap()[c * 512:(c + 1) * 512, :]
                    .rearrange("(n p) d -> p n d", p=128)),
                waits=[ev_chunk_mm[c - 2]] if c >= 2 else [],
                inc=f"d_x{par}", amt=16)
            for nsub in range(4):
                for i in range(DT):
                    tl = nsub * DT + i
                    bank = tl % 4
                    src = astage[:, par * 4096 + nsub * 1024 + i * 128:
                                 par * 4096 + nsub * 1024 + (i + 1) * 128]
                    mm = P.op("tensor", lambda e, src=src, bank=bank:
                              e.matmul(pb[bank][:, 0:128], src, c_idbf[:],
                                       start=True, stop=True),
                              waits=[ev_dx, evac_ev[bank], ev_consts],
                              inc="PE")
                    ev_chunk_mm[c] = mm
                    dst = xT[:, i * T + c * 512 + nsub * 128:
                             i * T + c * 512 + (nsub + 1) * 128]
                    veng = "vector" if (tl % 2 == 0) else "scalar"
                    sem = "DVE" if veng == "vector" else "ACT"

                    def evac(e, dst=dst, bank=bank, veng=veng):
                        if veng == "vector":
                            return e.tensor_copy(dst, pb[bank][:, 0:128])
                        return e.copy(dst, pb[bank][:, 0:128])
                    evac_ev[bank] = P.op(veng, evac, waits=[mm], inc=sem)
                    last_evac[veng] = evac_ev[bank]
        ev_x_ready = [last_evac["vector"], last_evac["scalar"]]
        ev_a_pe = ev_chunk_mm[7]
        st_a.close()

        if tap == "xT":
            return finish_tap(xT[:], ev_x_ready)

        st_b = contextlib.ExitStack()
        kT = sbr(st_b, "kT", [128, T], BF16)
        V_sb = sbr(st_b, "V_sb", [128, 32 * 130], BF16)
        qT = sbr(st_b, "qT", [128, NTOK], BF16)
        winp = sbr(st_b, "winp", [128, 3072], BF16)
        masks = sbr(st_b, "masks", [128, 4 * 512], BF16)
        cs_k = sbr(st_b, "cs_k", [128, T], BF16)
        cw_k = sbr(st_b, "cw_k", [128, T], BF16)
        qkstage = sbr(st_b, "qkstage", [128, 2 * 512], BF16)
        vstage = sbr(st_b, "vstage", [128, 2 * 512], BF16)
        rtmp1 = sbr(st_b, "rtmp1", [128, 512], BF16)
        rtmp2 = sbr(st_b, "rtmp2", [128, 512], BF16)
        Abuf = sbr(st_b, "Abuf", [128, 4 * 1024], BF16)
        r1sb = rtmp1[0:64, :]
        r2sb = rtmp2[0:64, :]

        ev_btab = None
        for dst, src in ((masks, c_msk_in), (cs_k, c_csk_in),
                         (cw_k, c_cwk_in)):
            ev_btab = P.op("sync", dma_to(dst[:], src.ap()),
                           waits=[ev_a_pe] + ev_x_ready, inc="d_c", amt=16)

        ev_vones = P.op("vector", lambda e: e.memset(
            V_sb[:].rearrange("p (k two g) -> p k two g", two=2, g=65)
            [:, :, :, 64:65], 1.0), waits=[ev_a_pe])

        ev_winp = [None] * (NPAIR + 1)
        ev_winp[0] = P.op("gpsimd", lambda e: e.dma_start(
            winp[:].rearrange("p (i c) -> p i c", i=DT),
            winT_in.ap()[:, 0:384].rearrange("(i p) c -> p i c", p=128)),
            waits=[ev_consts, ev_a_pe] + ev_x_ready, inc="d_wp", amt=16)

        ev_inproj_done = [None] * NPAIR
        ev_norm = [None, None]
        ev_rbc_used = None
        ev_pb7_used = None
        b_free = [None, None, None, None]
        ev_vcopy = None
        ev_stage_free = [None, None]
        ev_vstage_free = [None, None]

        for p in range(NPAIR):
            ev_rope_q = None
            ev_rope_k = None
            last_inproj_mm = None
            pend = None

            chunk_list = ([(0, g) for g in range(NG)]
                          + [(1, kc) for kc in range(8)]
                          + [(2, kc) for kc in range(8)])

            for idx, (kind, ci) in enumerate(chunk_list):
                bank = idx % 2
                par = idx % 2
                tok0 = (2 * ci) * 512 if kind == 0 else ci * 512
                rhs_l = [xT[:, i * T + tok0:i * T + tok0 + 512]
                         for i in range(DT)]
                mm = None
                for i in range(DT):
                    lhs = winp[:, i * 384 + kind * 128:
                               i * 384 + (kind + 1) * 128]
                    mm = P.op("tensor", lambda e, lhs=lhs, rhs=rhs_l[i],
                              bank=bank, st=(i == 0), sp=(i == DT - 1):
                              e.matmul(pb[bank][:, 0:512], lhs, rhs,
                                       start=st, stop=sp),
                              waits=[ev_winp[p], b_free[bank]] + ev_x_ready,
                              inc="PE" if i == DT - 1 else None)
                last_inproj_mm = mm

                def make_stage2(kind=kind, ci=ci, bank=bank, par=par, mm=mm,
                                p=p, tok0=tok0):
                    def stage2():
                        nonlocal ev_rope_q, ev_rope_k, ev_vcopy
                        bcol = binT[:, 3 * p + kind:3 * p + kind + 1]
                        if kind < 2:
                            stg = qkstage[:, par * 512:(par + 1) * 512]
                            evs = P.op("scalar", lambda e, bank=bank,
                                       bcol=bcol, stg=stg: e.activation(
                                           stg, pb[bank][:, 0:512],
                                           AF.Identity, bias=bcol),
                                       waits=[mm, ev_stage_free[par],
                                              ev_btab], inc="ACT")
                            b_free[bank] = evs
                            swb = 2 + par
                            sw = P.op("tensor", lambda e, swb=swb, stg=stg:
                                      e.matmul(pb[swb][:, 0:512], c_perm[:],
                                               stg, start=True, stop=True),
                                      waits=[evs, b_free[swb], ev_consts],
                                      inc="PE")
                            dst = (qT if kind == 0 else kT)
                            off = ci * 512
                            P.op("vector", lambda e, tok0=tok0, stg=stg:
                                 e.tensor_mul(rtmp1[:], stg,
                                              cs_k[:, tok0:tok0 + 512]),
                                 waits=[evs, ev_btab])
                            P.op("vector", lambda e, tok0=tok0, swb=swb:
                                 e.tensor_mul(rtmp2[:], pb[swb][:, 0:512],
                                              cw_k[:, tok0:tok0 + 512]),
                                 waits=[sw, ev_btab])
                            t3 = P.op("vector", lambda e, dst=dst, off=off:
                                      e.tensor_add(dst[:, off:off + 512],
                                                   rtmp1[:], rtmp2[:]),
                                      inc="DVE")
                            b_free[swb] = t3
                            ev_stage_free[par] = t3
                            if kind == 0:
                                ev_rope_q = t3
                            else:
                                ev_rope_k = t3
                        else:
                            vstg = vstage[:, par * 512:(par + 1) * 512]
                            evs = P.op("scalar", lambda e, bank=bank,
                                       bcol=bcol, vstg=vstg: e.activation(
                                           vstg, pb[bank][:, 0:512],
                                           AF.Identity, bias=bcol),
                                       waits=[mm, ev_vstage_free[par],
                                              ev_btab], inc="ACT")
                            b_free[bank] = evs
                            cp = None
                            for m in range(4):
                                swb = 2 + m % 2
                                tr = P.op("tensor", lambda e, m=m, swb=swb,
                                          vstg=vstg: e.matmul(
                                              pb[swb][:, 0:128],
                                              vstg[:, m * 128:(m + 1) * 128],
                                              c_idbf[:], start=True,
                                              stop=True),
                                          waits=[evs, b_free[swb],
                                                 ev_consts], inc="PE")
                                base = (ci * 4 + m) * 130

                                def vcopy(e, base=base, swb=swb):
                                    dst = V_sb[:, base:base + 130].rearrange(
                                        "p (two g) -> p two g",
                                        two=2)[:, :, 0:64]
                                    src = pb[swb][:, 0:128].rearrange(
                                        "p (two g) -> p two g", two=2)
                                    return e.tensor_copy(dst, src)
                                cp = P.op("vector", vcopy,
                                          waits=[tr, ev_vones], inc="DVE")
                                b_free[swb] = cp
                            ev_vstage_free[par] = cp
                            ev_vcopy = cp
                    return stage2

                if pend is not None:
                    pend()
                pend = make_stage2()
            pend()
            ev_inproj_done[p] = last_inproj_mm

            if p + 1 < NPAIR:
                ev_winp[p + 1] = P.op("gpsimd", lambda e, p=p: e.dma_start(
                    winp[:].rearrange("p (i c) -> p i c", i=DT),
                    winT_in.ap()[:, (p + 1) * 384:(p + 2) * 384]
                    .rearrange("(i p) c -> p i c", p=128)),
                    waits=[last_inproj_mm], inc="d_wp", amt=16)

            if tap == "qT" and p == 0:
                return finish_tap(qT[:], [ev_rope_q])
            if tap == "kT" and p == 0:
                return finish_tap(kT[:], [ev_rope_k])
            if tap == "V" and p == 0:
                return finish_tap(V_sb[:], [ev_vcopy])

            ev_stats = None
            for ci in range(12):
                src_buf = qT if ci < 4 else kT
                off = (ci * 512) if ci < 4 else ((ci - 4) * 512)
                spar = ci % 2
                bk = 6 + ci % 2
                sqev = P.op("scalar", lambda e, src_buf=src_buf, off=off,
                            spar=spar: e.activation(
                                qkstage[:, spar * 512:(spar + 1) * 512],
                                src_buf[:, off:off + 512], AF.Square),
                            waits=[ev_rope_q if ci < 4 else ev_rope_k,
                                   ev_stage_free[spar]],
                            inc="ACT")
                smm = P.op("tensor", lambda e, spar=spar, bk=bk: e.matmul(
                    pb[bk][0:33, 0:512], c_hsel[:],
                    qkstage[:, spar * 512:(spar + 1) * 512],
                    start=True, stop=True),
                    waits=[sqev,
                           ev_rbc_used if bk == 6 else ev_pb7_used,
                           ev_consts], inc="PE")
                ev_stats = P.op("vector", lambda e, ci=ci, bk=bk:
                                e.reduce_max(
                                    normcol[:, ci:ci + 1],
                                    pb[bk][0:33, 0:512], axis=AX.X),
                                waits=[smm], inc="DVE")
                if bk == 6:
                    ev_rbc_used = ev_stats
                else:
                    ev_pb7_used = ev_stats
                ev_stage_free[spar] = smm
            ch = ev_stats
            for fng in (
                lambda e: e.reduce_max(normcol[:, 12:13], normcol[:, 0:4],
                                       axis=AX.X),
                lambda e: e.reduce_max(normcol[:, 13:14], normcol[:, 4:12],
                                       axis=AX.X),
                lambda e: e.tensor_mul(normcol[:, 14:15], normcol[:, 12:13],
                                       normcol[:, 13:14]),
                lambda e: e.tensor_scalar_max(normcol[:, 14:15],
                                              normcol[:, 14:15], 1e-12),
            ):
                ch = P.op("vector", fng, hard=[ch], inc="DVE")
            lnp = ch
            ln1 = P.op("scalar", lambda e: e.activation(
                normcol[:, 15:16], normcol[:, 14:15], AF.Ln), waits=[lnp])
            sq1 = P.op("scalar", lambda e: e.activation(
                normcol[:, 14:15], normcol[:, 15:16], AF.Exp, scale=0.5),
                inc="ACT")
            ngm = P.op("vector", lambda e: e.tensor_scalar(
                normcol[:, 15:16], normcol[:, 14:15], -0.128875, -1.0,
                ALU.mult, ALU.add), waits=[sq1], hard=[lnp], inc="DVE")
            ev_negM = [None, None]
            for head in (0, 1):
                nb = P.op("tensor", lambda e, head=head: e.matmul(
                    pb[6][:, 0:1],
                    c_or33[32 * head:32 * head + 1, :],
                    normcol[32 * head:32 * head + 1, 15:16],
                    start=True, stop=True),
                    waits=[ngm, ev_rbc_used, ev_consts], inc="PE")
                evn = P.op(
                    "vector", lambda e, head=head: e.tensor_copy(
                        negMcol[:, head:head + 1], pb[6][:, 0:1]),
                    waits=[nb], inc="DVE")
                ev_negM[head] = P.op(
                    "vector", lambda e, head=head: e.tensor_sub(
                        negMcol[:, 2 + head:3 + head],
                        negMcol[:, head:head + 1], c_kill[:]),
                    waits=[ev_consts], hard=[evn], inc="DVE")
                ev_rbc_used = ev_negM[head]

            for g in range(NG):
                nkt = 8 * g + 8
                nvp = nkt // 2
                ev_exp = {}
                ev_av = {}
                ev_y_last = [None, None]

                def emit_av(vp, head, g=g, nkt=nkt, nvp=nvp):
                    ypb = 4 + head
                    av = None
                    ab = (head * 2 + vp % 2) * 1024
                    for v in (2 * vp, 2 * vp + 1):
                        vsl = V_sb[:, v * 130 + head * 65:
                                   v * 130 + head * 65 + 65]
                        asl = Abuf[:, ab + (v % 2) * 512:
                                   ab + (v % 2) * 512 + 512]
                        av = P.op("tensor", lambda e, ypb=ypb, vsl=vsl,
                                  asl=asl, st=(v == 0), sp=(v == nkt - 1):
                                  e.matmul(pb[ypb][0:65, :], vsl, asl,
                                           start=st, stop=sp),
                                  waits=[ev_exp[(head, vp)], ev_vcopy,
                                         ev_norm[head] if v == 0 else None],
                                  inc="PE" if v % 2 == 1 else None)
                    ev_av[(head, vp)] = av
                    if vp == nvp - 1:
                        ev_y_last[head] = av

                for vp in range(nvp):
                    v0 = 2 * vp
                    for head in (0, 1):
                        smm = None
                        for v in (v0, v0 + 1):
                            hb = head * 2 + v % 2
                            ks = kT[64 * head:64 * head + 64,
                                    v * 128:(v + 1) * 128]
                            qs = qT[64 * head:64 * head + 64,
                                    g * 512:(g + 1) * 512]
                            smm = P.op("tensor", lambda e, hb=hb, ks=ks,
                                       qs=qs, tp=(64 * head, 0): e.matmul(
                                           pb[hb][:, 0:512], ks, qs,
                                           start=True, stop=True,
                                           tile_position=tp),
                                       waits=[ev_rope_q, ev_rope_k,
                                              b_free[hb]],
                                       inc="PE" if v % 2 == 1 else None)
                        u = v0 - 8 * g
                        bcol_i = head if u < 4 else 2 + head
                        ab = (head * 2 + vp % 2) * 1024
                        asl2 = Abuf[:, ab:ab + 1024]
                        ex = P.op("scalar", lambda e, head=head, asl2=asl2,
                                  bcol_i=bcol_i: e.activation(
                                      asl2, pbd[head][:, 0:1024], AF.Exp,
                                      bias=negMcol[:, bcol_i:bcol_i + 1],
                                      scale=0.125),
                                  waits=[smm, ev_negM[head]],
                                  inc="ACT")
                        b_free[head * 2] = ex
                        b_free[head * 2 + 1] = ex
                        if 0 <= u < 4:
                            ex = P.op("vector",
                                      lambda e, asl2=asl2, u=u:
                                      e.tensor_mul(
                                          asl2, asl2,
                                          masks[:, u * 512:(u + 2) * 512]),
                                      waits=[ex, ev_btab], inc="DVE")
                        ev_exp[(head, vp)] = ex
                    if vp >= 1:
                        for head in (0, 1):
                            emit_av(vp - 1, head)
                for head in (0, 1):
                    emit_av(nvp - 1, head)

                for head in (0, 1):
                    ypb = 4 + head
                    rrow = wrow[0:1, head * 512:head * 512 + 512]
                    lnrow = wrow[0:1, 1024 + head * 512:1536 + head * 512]
                    rsb = r1sb if head == 0 else r2sb
                    rc1 = P.op("scalar", lambda e, ypb=ypb, lnrow=lnrow:
                               e.activation(lnrow, pb[ypb][64:65, :],
                                            AF.Ln),
                               waits=[ev_y_last[head]], inc="ACT")
                    rc = P.op("scalar", lambda e, rrow=rrow, lnrow=lnrow:
                              e.activation(rrow, lnrow, AF.Exp,
                                           scale=-1.0),
                              waits=[rc1], inc="ACT")
                    rb = P.op("tensor", lambda e, rrow=rrow: e.matmul(
                        pb[6][0:64, :], c_or[0:1, 0:64], rrow,
                        start=True, stop=True),
                        waits=[rc, ev_rbc_used, ev_consts], inc="PE")
                    rcp = P.op("vector", lambda e, rsb=rsb: e.tensor_copy(
                        rsb, pb[6][0:64, :]), waits=[rb], inc="DVE")
                    ev_rbc_used = rcp
                    dst = yT_all[64 * head:64 * head + 64,
                                 p * NTOK + g * 512:p * NTOK + (g + 1) * 512]
                    ev_norm[head] = P.op(
                        "vector", lambda e, dst=dst, ypb=ypb, rsb=rsb:
                        e.tensor_mul(dst, pb[ypb][0:64, :], rsb),
                        inc="DVE")

        ev_yT_done = [ev_norm[0], ev_norm[1]]
        if tap == "yT":
            return finish_tap(yT_all[:], ev_yT_done)

        st_b.close()

        ev_xo = None
        for g in range(NG):
            for i in range(DT):
                src = xT[:, i * T + (2 * g) * 512:i * T + (2 * g) * 512 + 512]
                dst = xOwn2[:, i * NTOK + g * 512:i * NTOK + (g + 1) * 512]
                ev_xo = P.op("vector", lambda e, src=src, dst=dst:
                             e.tensor_copy(dst, src),
                             waits=[ev_inproj_done[NPAIR - 1]], inc="DVE")
        st_x.close()

        ev_b_done = [ev_inproj_done[NPAIR - 1]] + ev_yT_done + \
            [ev_rbc_used, ev_xo]

        s_f32 = sbl("s_f32", [128, DT * NTOK], F32)
        x2bf = sbl("x2bf", [128, DT * NTOK], BF16)
        gT_sb = sbl("gT_sb", [16, NTOK], BF16)
        c_esel = sbl("c_esel", [16, NE * 128], BF16)
        ev_esel = P.op("sync", dma_to(c_esel[:], c_esel_in.ap()),
                       waits=ev_b_done, inc="d_c", amt=16)

        st_c = contextlib.ExitStack()
        woutsb = sbr(st_c, "woutsb", [128, NPAIR * D], BF16)
        sqstage = sbr(st_c, "sqstage", [128, 512], F32)

        ev_wout = P.op("gpsimd", lambda e: e.dma_start(
            woutsb[:].rearrange("p (i d) -> p i d", i=DT),
            woutT_in.ap().rearrange("(i p) d -> p i d", p=128)),
            waits=ev_b_done, inc="d_wo", amt=16)

        ev_s = {}
        op_bank_free = [None, None]
        for g in range(NG):
            for d in range(DT):
                obk = d % 2
                for p in range(NPAIR):
                    lhs = woutsb[:, p * D + d * 128:p * D + (d + 1) * 128]
                    rhs = yT_all[:, p * NTOK + g * 512:
                                 p * NTOK + (g + 1) * 512]
                    P.op("tensor", lambda e, lhs=lhs, rhs=rhs, obk=obk,
                         st=(p == 0): e.matmul(pb[obk][:, 0:512], lhs, rhs,
                                               start=st, stop=False),
                         waits=[ev_wout, ev_yT_done[0], ev_yT_done[1],
                                op_bank_free[obk]])
                mm = P.op("tensor", lambda e, d=d, obk=obk: e.matmul(
                    pb[obk][:, 0:512], boutbf[0:1, d * 128:(d + 1) * 128],
                    c_o512[:], start=False, stop=True),
                    waits=[ev_consts, ev_cg], inc="PE")
                xo = xOwn2[:, d * NTOK + g * 512:d * NTOK + (g + 1) * 512]
                ssl = s_f32[:, d * NTOK + g * 512:d * NTOK + (g + 1) * 512]
                ev_s[(g, d)] = P.op("vector", lambda e, ssl=ssl, obk=obk,
                                    xo=xo: e.tensor_add(
                                        ssl, pb[obk][:, 0:512], xo),
                                    waits=[mm, ev_xo], inc="DVE")
                op_bank_free[obk] = ev_s[(g, d)]

        if tap == "s1":
            return finish_tap(s_f32[:], [ev_s[(NG - 1, DT - 1)]])

        ev_x2 = {}
        sq_free = None
        for g in range(NG):
            st_mm = None
            sq_mm = None
            for d in range(DT):
                ssl = s_f32[:, d * NTOK + g * 512:d * NTOK + (g + 1) * 512]
                sq = P.op("vector", lambda e, ssl=ssl: e.tensor_mul(
                    sqstage[:], ssl, ssl),
                    waits=[ev_s[(g, d)], sq_free], inc="DVE")
                st_mm = P.op("tensor", lambda e, ssl=ssl, st=(d == 0),
                             sp=(d == DT - 1): e.matmul(
                                 pb[2][0:1, :], c_oc[:], ssl, start=st,
                                 stop=sp),
                             waits=[ev_s[(g, d)], ev_consts], inc="PE")
                sq_mm = P.op("tensor", lambda e, st=(d == 0),
                             sp=(d == DT - 1): e.matmul(
                                 pb[3][0:1, :], c_oc[:], sqstage[:],
                                 start=st, stop=sp),
                             waits=[sq], inc="PE")
                sq_free = sq_mm
            ch = P.op("vector", lambda e: e.tensor_scalar_mul(
                wrow[0:1, 0:512], pb[2][0:1, :], 1.0 / D), waits=[st_mm],
                inc="DVE")
            for fng in (
                lambda e: e.tensor_scalar_mul(wrow[0:1, 512:1024],
                                              pb[3][0:1, :], 1.0 / D),
                lambda e: e.tensor_mul(wrow[0:1, 1024:1536], wrow[0:1, 0:512],
                                       wrow[0:1, 0:512]),
                lambda e: e.tensor_sub(wrow[0:1, 512:1024],
                                       wrow[0:1, 512:1024],
                                       wrow[0:1, 1024:1536]),
                lambda e: e.tensor_scalar_add(wrow[0:1, 512:1024],
                                              wrow[0:1, 512:1024], LN_EPS),
            ):
                ch = P.op("vector", fng, waits=[sq_mm], hard=[ch], inc="DVE")
            ve_r = ch
            P.op("scalar", lambda e: e.activation(
                wrow[0:1, 1024:1536], wrow[0:1, 512:1024], AF.Ln),
                waits=[ve_r])
            inv_r = P.op("scalar", lambda e: e.activation(
                wrow[0:1, 1536:2048], wrow[0:1, 1024:1536], AF.Exp,
                scale=-0.5), inc="ACT")
            mb = P.op("tensor", lambda e: e.matmul(
                pb[4][:, 0:512], c_or[:], wrow[0:1, 0:512], start=True,
                stop=True), waits=[ve_r, ev_consts], inc="PE")
            ib = P.op("tensor", lambda e: e.matmul(
                pb[5][:, 0:512], c_or[:], wrow[0:1, 1536:2048], start=True,
                stop=True), waits=[inv_r, ev_consts], inc="PE")
            last_n = None
            for d in range(DT):
                ssl = s_f32[:, d * NTOK + g * 512:d * NTOK + (g + 1) * 512]
                P.op("vector", lambda e, ssl=ssl: e.tensor_sub(
                    ssl, ssl, pb[4][:, 0:512]), waits=[mb])
                last_n = P.op("vector", lambda e, ssl=ssl: e.tensor_mul(
                    ssl, ssl, pb[5][:, 0:512]), waits=[ib], inc="DVE")
                x2sl = x2bf[:, d * NTOK + g * 512:d * NTOK + (g + 1) * 512]
                ev_x2[(g, d)] = P.op("scalar", lambda e, ssl=ssl, x2sl=x2sl:
                                     e.copy(x2sl, ssl), waits=[last_n],
                                     inc="ACT")
            ev_s[(g, "ln")] = last_n

        ev_c_done = [ev_s[(NG - 1, "ln")], ev_x2[(NG - 1, DT - 1)]]
        if tap == "x2":
            return finish_tap(s_f32[:], ev_c_done)

        st_c.close()
        st_ac.close()

        gw = gwork
        ev_gT = {}
        ev_pb1_free = None
        ev_exl_prev = None
        for tt in range(16):
            g = tt // 4
            mm = None
            for d in range(DT):
                lhs = s_f32[:, d * NTOK + tt * 128:d * NTOK + (tt + 1) * 128]
                mm = P.op("tensor", lambda e, lhs=lhs, d=d, st=(d == 0),
                          sp=(d == DT - 1): e.matmul(
                              pb[0][:, 0:16], lhs,
                              gatef[:, d * NE:(d + 1) * NE], start=st,
                              stop=sp),
                          waits=[ev_s[(g, "ln")], ev_consts, ev_exl_prev],
                          inc="PE" if d == DT - 1 else None)
            nm = P.op("vector", lambda e: e.reduce_max(
                gw[:, 80:81], pb[0][:, 0:16], axis=AX.X, negate=True),
                waits=[mm], inc="DVE")
            exl = P.op("scalar", lambda e: e.activation(
                gw[:, 0:16], pb[0][:, 0:16], AF.Exp, bias=gw[:, 80:81]),
                waits=[nm, mm], inc="ACT")
            ev_exl_prev = exl
            ch = exl
            gate_ops = [
                lambda e: e.reduce_sum(gw[:, 81:82], gw[:, 0:16], axis=AX.X),
                lambda e: e.reciprocal(gw[:, 82:83], gw[:, 81:82]),
                lambda e: e.tensor_scalar_mul(gw[:, 16:32], gw[:, 0:16],
                                              gw[:, 82:83]),
                lambda e: e.reduce_max(gw[:, 83:84], gw[:, 16:32], axis=AX.X),
                lambda e: e.tensor_scalar(gw[:, 32:48], gw[:, 16:32],
                                          gw[:, 83:84], None, ALU.is_ge),
                lambda e: e.scalar_tensor_tensor(gw[:, 48:64], gw[:, 32:48],
                                                 -1e9, gw[:, 16:32],
                                                 ALU.mult, ALU.add),
                lambda e: e.reduce_max(gw[:, 84:85], gw[:, 48:64], axis=AX.X),
                lambda e: e.tensor_scalar(gw[:, 32:48], gw[:, 48:64],
                                          gw[:, 84:85], None, ALU.is_ge),
                lambda e: e.scalar_tensor_tensor(gw[:, 48:64], gw[:, 32:48],
                                                 -1e9, gw[:, 48:64],
                                                 ALU.mult, ALU.add),
                lambda e: e.reduce_max(gw[:, 85:86], gw[:, 48:64], axis=AX.X),
                lambda e: e.tensor_scalar(gw[:, 32:48], gw[:, 16:32],
                                          gw[:, 85:86], None, ALU.is_ge),
                lambda e: e.tensor_mul(gw[:, 48:64], gw[:, 32:48],
                                       gw[:, 16:32]),
                lambda e: e.reduce_sum(gw[:, 86:87], gw[:, 48:64], axis=AX.X),
                lambda e: e.tensor_scalar_max(gw[:, 86:87], gw[:, 86:87],
                                              GATE_EPS),
                lambda e: e.reciprocal(gw[:, 87:88], gw[:, 86:87]),
                lambda e: e.tensor_scalar_mul(gw[:, 64:80], gw[:, 48:64],
                                              gw[:, 87:88]),
            ]
            for fng in gate_ops:
                ch = P.op("vector", fng, waits=[exl], hard=[ch], inc="DVE")
            gtl = ch
            gtr = P.op("tensor", lambda e: e.matmul(
                pb[1][0:16, 0:128], gw[:, 64:80], c_idf[:], start=True,
                stop=True), waits=[gtl, ev_pb1_free, ev_consts],
                inc="PE")
            ev_gT[tt] = P.op("vector", lambda e, tt=tt: e.tensor_copy(
                gT_sb[:, tt * 128:(tt + 1) * 128], pb[1][0:16, 0:128]),
                waits=[gtr], inc="DVE")
            ev_pb1_free = ev_gT[tt]
        ev_gT_done = ev_gT[15]

        if tap == "gT":
            return finish_tap(gT_sb[:], [ev_gT_done])
        if tap == "gw":
            return finish_tap(gwork[:], [ev_gT_done])

        st_e = contextlib.ExitStack()
        w1bf = sbr(st_e, "w1bf", [128, DT * HID], BF16)
        w2bf = sbr(st_e, "w2bf", [128, 16 * D], BF16)
        hgT = sbr(st_e, "hgT", [128, 16 * 512], BF16)
        xg = sbr(st_e, "xg", [128, DT * 512], BF16)
        b1ct = sbr(st_e, "b1ct", [128, 16 * NE], F32)
        b1st = sbr(st_e, "b1st", [16, 128 * NE], BF16)
        gsb = sbr(st_e, "gsb", [128, 512], BF16)

        ev_w1 = [[None, None] for _ in range(NE + 1)]
        ev_w2 = [[None, None] for _ in range(NE + 1)]
        ev_ffn_space = ev_b_done + ev_c_done + [ev_gT_done]

        def dma_w1_half(e, ei, hlf):
            return e.dma_start(
                w1bf[:].rearrange("p (i m) -> p i m", i=DT)
                [:, :, hlf * 1024:(hlf + 1) * 1024],
                w1_in.ap()[ei].rearrange("(i p) m -> p i m", p=128)
                [:, :, hlf * 1024:(hlf + 1) * 1024])

        def dma_w2_half(e, ei, hlf):
            return e.dma_start(
                w2bf[:].rearrange("p (j d) -> p j d", j=16)
                [:, :, hlf * 512:(hlf + 1) * 512],
                w2_in.ap()[ei].rearrange("(j p) d -> p j d", p=128)
                [:, :, hlf * 512:(hlf + 1) * 512])

        for hlf in (0, 1):
            ev_w1[0][hlf] = P.op(
                "gpsimd", lambda e, hlf=hlf: dma_w1_half(e, 0, hlf),
                waits=ev_ffn_space, inc=f"d_w1{hlf}", amt=16)
            ev_w2[0][hlf] = P.op(
                "gpsimd", lambda e, hlf=hlf: dma_w2_half(e, 0, hlf),
                waits=ev_ffn_space, inc=f"d_w2{hlf}", amt=16)
        ev_b1d = P.op("gpsimd", lambda e: e.dma_start(
            b1st[:].rearrange("j (E p) -> j E p", E=NE),
            b1_in.ap().rearrange("E (j p) -> j E p", p=128)),
            waits=ev_ffn_space, inc="d_b10", amt=16)
        btr = None
        for e2 in range(NE):
            btr = P.op("tensor", lambda e, e2=e2: e.matmul(
                pb[7][:, e2 * 16:(e2 + 1) * 16],
                b1st[:, e2 * 128:(e2 + 1) * 128],
                c_idbf[0:16, 0:16], start=True, stop=True),
                waits=[ev_b1d, ev_pb7_used, ev_consts],
                inc="PE" if e2 == NE - 1 else None)
        ev_b1ct = P.op("vector", lambda e: e.tensor_copy(
            b1ct[:], pb[7][:, 0:16 * NE]), waits=[btr], inc="DVE")

        ev_b2p = {}
        b2_bank_free = [None, None]
        for cc in range(NG):
            for i in range(DT):
                ob = 5 + i % 2
                mm = P.op("tensor", lambda e, i=i, cc=cc, ob=ob: e.matmul(
                    pb[ob][:, 0:512], b2bf[:, i * 128:(i + 1) * 128],
                    gT_sb[:, cc * 512:(cc + 1) * 512], start=True, stop=True),
                    waits=[ev_consts, ev_cg, ev_gT_done,
                           b2_bank_free[i % 2]],
                    inc="PE")
                ssl = s_f32[:, i * NTOK + cc * 512:i * NTOK + (cc + 1) * 512]
                ev_b2p[(cc, i)] = P.op(
                    "vector", lambda e, ssl=ssl, ob=ob: e.tensor_add(
                        ssl, pb[ob][:, 0:512], ssl),
                    waits=[mm, ev_s[(cc, "ln")]], inc="DVE")
                b2_bank_free[i % 2] = ev_b2p[(cc, i)]

        ev_gb_free = None
        ev_xg_free = None
        ev_h_free = [None, None]
        ev_o_free = [None, None]
        ev_hg_free = {}
        ev_evac = {}
        chunks = [(ei, cc) for ei in range(NE) for cc in range(NG)]
        xgl_t = {}

        def emit_gxg(t):
            nonlocal ev_gb_free
            ei, cc = chunks[t]
            gb = P.op("tensor", lambda e, ei=ei, cc=cc: e.matmul(
                pb[0][:, 0:512], c_esel[:, ei * 128:(ei + 1) * 128],
                gT_sb[:, cc * 512:(cc + 1) * 512], start=True, stop=True),
                waits=[ev_gT_done, ev_gb_free, ev_esel],
                inc="PE")
            gcp = P.op("vector", lambda e: e.tensor_copy(
                gsb[:], pb[0][:, 0:512]), waits=[gb], inc="DVE")
            ev_gb_free = gcp
            xgl = None
            for i in range(DT):
                x2sl = x2bf[:, i * NTOK + cc * 512:
                            i * NTOK + (cc + 1) * 512]
                xgl = P.op("vector", lambda e, i=i, x2sl=x2sl:
                           e.tensor_mul(xg[:, i * 512:(i + 1) * 512],
                                        x2sl, gsb[:]),
                           waits=[gcp, ev_xg_free, ev_x2[(cc, i)]],
                           inc="DVE" if i == DT - 1 else None)
            xgl_t[t] = xgl

        for t, (ei, cc) in enumerate(chunks):
                if t == 0:
                    emit_gxg(0)
                xgl = xgl_t.pop(t)
                m1 = None
                ev_relu_last = None
                for j in range(16):
                    hb = 1 + j % 2
                    for i in range(DT):
                        lhs = w1bf[:, i * HID + j * 128:
                                   i * HID + (j + 1) * 128]
                        m1 = P.op("tensor", lambda e, lhs=lhs, i=i, hb=hb,
                                  st=(i == 0), sp=(i == DT - 1): e.matmul(
                                      pb[hb][:, 0:512], lhs,
                                      xg[:, i * 512:(i + 1) * 512], start=st,
                                      stop=sp),
                                  waits=[xgl, ev_w1[ei][j // 8],
                                         ev_h_free[hb - 1]],
                                  inc="PE" if i == DT - 1 else None)
                    stt = P.op("vector", lambda e, hb=hb, j=j, ei=ei:
                               e.scalar_tensor_tensor(
                                   pb[hb][:, 0:512], gsb[:],
                                   b1ct[:, ei * 16 + j:ei * 16 + j + 1],
                                   pb[hb][:, 0:512], ALU.mult, ALU.add),
                               waits=[m1, ev_b1ct], inc="DVE")
                    rl = P.op("scalar", lambda e, j=j, hb=hb: e.activation(
                        hgT[:, j * 512:(j + 1) * 512], pb[hb][:, 0:512],
                        AF.Relu), waits=[stt, ev_hg_free.get(j)], inc="ACT")
                    ev_h_free[hb - 1] = rl
                    ev_relu_last = rl
                    if cc == NG - 1 and j == 7:
                        ev_w1A_used = m1
                ev_xg_free = m1
                m2 = None
                for i in range(DT):
                    ob = 3 + i % 2
                    for j in range(16):
                        lhs = w2bf[:, j * D + i * 128:j * D + (i + 1) * 128]
                        m2 = P.op("tensor", lambda e, lhs=lhs, ob=ob, j=j,
                                  st=(j == 0), sp=(j == 15): e.matmul(
                                      pb[ob][:, 0:512], lhs,
                                      hgT[:, j * 512:(j + 1) * 512],
                                      start=st, stop=sp),
                                  waits=[ev_relu_last, ev_w2[ei][i // 4],
                                         ev_o_free[ob - 3]],
                                  inc="PE" if j == 15 else None)
                    ssl = s_f32[:, i * NTOK + cc * 512:
                                i * NTOK + (cc + 1) * 512]
                    ee = P.op("vector", lambda e, ssl=ssl, ob=ob:
                              e.tensor_add(ssl, pb[ob][:, 0:512], ssl),
                              waits=[m2, ev_s[(cc, "ln")]], inc="DVE")
                    ev_evac[(ei, cc, i)] = ee
                    ev_o_free[ob - 3] = ee
                    if i == 0 and t + 1 < len(chunks):
                        emit_gxg(t + 1)
                    if cc == NG - 1 and i == 3:
                        ev_w2C_used = m2
                for jj in range(16):
                    ev_hg_free[jj] = m2
                if cc != NG - 1:
                    continue
                if ei + 1 < NE:
                    ev_w1[ei + 1][0] = P.op(
                        "gpsimd", lambda e, ei=ei: dma_w1_half(e, ei + 1, 0),
                        waits=[ev_w1A_used], inc="d_w10", amt=16)
                    ev_w1[ei + 1][1] = P.op(
                        "gpsimd", lambda e, ei=ei: dma_w1_half(e, ei + 1, 1),
                        waits=[ev_xg_free], inc="d_w11", amt=16)
                    ev_w2[ei + 1][0] = P.op(
                        "gpsimd", lambda e, ei=ei: dma_w2_half(e, ei + 1, 0),
                        waits=[ev_w2C_used], inc="d_w20", amt=16)
                    ev_w2[ei + 1][1] = P.op(
                        "gpsimd", lambda e, ei=ei: dma_w2_half(e, ei + 1, 1),
                        waits=[ev_hg_free[0]], inc="d_w21", amt=16)
                if tap == "acc0" and ei == 0:
                    return finish_tap(
                        s_f32[:], [ev_evac[(0, c2, i2)] for c2 in range(NG)
                                   for i2 in range(DT)])

        if tap == "acc":
            return finish_tap(
                s_f32[:], [ev_evac[(NE - 1, c2, i2)] for c2 in range(NG)
                           for i2 in range(DT)])

        st_e.close()

        st_f = contextlib.ExitStack()
        xtok = sbr(st_f, "xtok", [128, 2 * D], F32)
        otile = sbr(st_f, "otile", [128, 2 * D], F32)
        sqf = sbr(st_f, "sqf", [128, D], F32)
        ev_out = [None] * 16
        ev_tr_free = [None, None]
        ev_sq_free = None
        ev_ot_free = [None, None]
        ev_f_space = [ev_evac[(NE - 1, NG - 1, DT - 1)], ev_hg_free[0]]
        for tt in range(16):
            par = tt % 2
            g = tt // 4
            cps = None
            for i in range(DT):
                trb = i % 2
                lhs = s_f32[:, i * NTOK + tt * 128:i * NTOK + (tt + 1) * 128]
                tr = P.op("tensor", lambda e, lhs=lhs, trb=trb: e.matmul(
                    pb[trb][:, 0:128], lhs, c_idf[:], start=True, stop=True),
                    waits=[ev_evac[(NE - 1, g, i)], ev_tr_free[trb],
                           ev_consts],
                    inc="PE")
                cps = P.op("vector", lambda e, i=i, par=par, trb=trb:
                           e.tensor_copy(
                               xtok[:, par * D + i * 128:
                                    par * D + (i + 1) * 128],
                               pb[trb][:, 0:128]),
                           waits=[tr] + (ev_f_space if tt == 0 else []),
                           inc="DVE")
                ev_tr_free[trb] = cps
            ch = P.op("vector", lambda e, par=par: e.tensor_mul(
                sqf[:], xtok[:, par * D:(par + 1) * D],
                xtok[:, par * D:(par + 1) * D]),
                waits=[cps, ev_sq_free], inc="DVE")
            for fng in (
                lambda e, par=par: e.reduce_sum(
                    fwork[:, 0:1], xtok[:, par * D:(par + 1) * D], axis=AX.X),
                lambda e: e.reduce_sum(fwork[:, 1:2], sqf[:], axis=AX.X),
                lambda e: e.tensor_scalar_mul(fwork[:, 2:3], fwork[:, 0:1],
                                              1.0 / D),
                lambda e: e.tensor_scalar_mul(fwork[:, 3:4], fwork[:, 1:2],
                                              1.0 / D),
                lambda e: e.tensor_mul(fwork[:, 4:5], fwork[:, 2:3],
                                       fwork[:, 2:3]),
                lambda e: e.tensor_sub(fwork[:, 5:6], fwork[:, 3:4],
                                       fwork[:, 4:5]),
                lambda e: e.tensor_scalar_add(fwork[:, 5:6], fwork[:, 5:6],
                                              LN_EPS),
            ):
                ch = P.op("vector", fng, hard=[ch], inc="DVE")
            vv = ch
            ev_sq_free = vv
            P.op("scalar", lambda e: e.activation(
                fwork[:, 6:7], fwork[:, 5:6], AF.Ln), waits=[vv])
            inv = P.op("scalar", lambda e: e.activation(
                fwork[:, 7:8], fwork[:, 6:7], AF.Exp, scale=-0.5), inc="ACT")
            ot = P.op("vector", lambda e, par=par: e.tensor_scalar(
                otile[:, par * D:(par + 1) * D],
                xtok[:, par * D:(par + 1) * D],
                fwork[:, 2:3], fwork[:, 7:8], ALU.subtract, ALU.mult),
                waits=[inv, ev_ot_free[par]], hard=[vv], inc="DVE")
            ev_out[tt] = P.op("sync", lambda e, tt=tt, par=par: e.dma_start(
                y_out.ap()[tt * 128:(tt + 1) * 128, :],
                otile[:, par * D:(par + 1) * D]),
                waits=[ot], inc=f"d_y{par}", amt=16)
            ev_ot_free[par] = ev_out[tt]

        tails = [(s, P.ctr[s]) for s in sorted(P.ctr) if s.startswith("d_")]
        P.op("sync", lambda e: e.nop(),
             waits=[ev_out[14], ev_out[15]] + tails)
        P.emit(stack, sim_serial=sim_serial)
    return nc


def _rope_tables():
    half = 32
    p = np.arange(half, dtype=np.float64) / (half - 1)
    thetas = 1.0 * (10000.0 / 1.0) ** p
    pos = np.arange(T, dtype=np.float64)[:, None] / thetas[None, :]
    sin = np.sin(pos).T
    cos = np.cos(pos).T
    cs = np.concatenate([sin, -sin, sin, -sin], axis=0)
    cw = np.concatenate([cos, cos, cos, cos], axis=0)
    return cs, cw


def _prog_perm(h):
    blocks = np.arange(8)
    if h == 1:
        blocks = blocks ^ 1
    return blocks


def _consts(h):
    idn = np.eye(128, dtype=np.float64)
    perm = np.zeros((128, 128), dtype=np.float64)
    for j in range(128):
        swap = j + 32 if (j % 64) < 32 else j - 32
        perm[swap, j] = 1.0
    cs, cw = _rope_tables()
    pos_prog = np.concatenate(
        [np.arange(512 * gb, 512 * gb + 512) for gb in _prog_perm(h)])
    msk = np.zeros((128, 4 * 512), dtype=np.float64)
    pcol = np.arange(128)[:, None]
    jcol = np.arange(512)[None, :]
    for u in range(4):
        msk[:, u * 512:(u + 1) * 512] = (128 * u + pcol <= jcol)
    mscal = np.full((128, 4), float(h), dtype=np.float32)
    kill = np.full((128, 1), 0.0 if h == 1 else 40000.0, dtype=np.float32)
    esel = np.zeros((16, NE * 128), dtype=np.float64)
    for e in range(NE):
        esel[e, e * 128:(e + 1) * 128] = 1.0
    hsel = np.zeros((128, 33), dtype=np.float64)
    hsel[0:64, 0] = 1.0
    hsel[64:128, 32] = 1.0
    bf = ml_dtypes.bfloat16
    return {
        "c_hsel": hsel.astype(bf),
        "c_idbf": idn.astype(bf), "c_idf": idn.astype(np.float32),
        "c_perm": perm.astype(bf),
        "c_csk": np.ascontiguousarray(cs[:, pos_prog]).astype(bf),
        "c_cwk": np.ascontiguousarray(cw[:, pos_prog]).astype(bf),
        "c_msk": msk.astype(bf), "c_mscal": mscal, "c_kill": kill,
        "c_esel": esel.astype(bf),
        "c_onesrow": np.ones((1, 128), np.float32),
        "c_or33": np.ones((33, 128), np.float32),
        "c_onescol": np.ones((128, 1), np.float32),
        "c_ones512": np.ones((1, 512), bf),
    }


def make_in_maps(L0_attn_x, L0_attn_W_in, L0_attn_b_in, L0_attn_W_out,
                 L0_attn_b_out, L0_ffn_gate, L0_ffn_W1, L0_ffn_b1, L0_ffn_W2,
                 L0_ffn_b2):
    x = np.ascontiguousarray(np.asarray(L0_attn_x, dtype=np.float32))
    W_in = np.asarray(L0_attn_W_in, dtype=np.float32)
    b_in = np.asarray(L0_attn_b_in, dtype=np.float32)
    W_out = np.asarray(L0_attn_W_out, dtype=np.float32)
    b_out = np.asarray(L0_attn_b_out, dtype=np.float32)

    winT = np.concatenate(
        [np.stack([W_in[128 * p:128 * (p + 1)].T,
                   W_in[1024 + 128 * p:1024 + 128 * (p + 1)].T,
                   W_in[2048 + 128 * p:2048 + 128 * (p + 1)].T], axis=1)
         .reshape(D, 384) for p in range(NPAIR)], axis=1)
    winT = np.ascontiguousarray(winT)
    binT = np.zeros((128, 24), dtype=np.float32)
    for p in range(NPAIR):
        binT[:, 3 * p] = b_in[128 * p:128 * (p + 1)]
        binT[:, 3 * p + 1] = b_in[1024 + 128 * p:1024 + 128 * (p + 1)]
        binT[:, 3 * p + 2] = b_in[2048 + 128 * p:2048 + 128 * (p + 1)]
    shared = dict(
        winT=winT, binT=binT,
        woutT=np.ascontiguousarray(W_out.T),
        bout=np.ascontiguousarray(b_out[None, :]),
        gate=np.ascontiguousarray(np.asarray(L0_ffn_gate, np.float32)),
        w1=np.ascontiguousarray(np.asarray(L0_ffn_W1, np.float32)),
        w2=np.ascontiguousarray(np.asarray(L0_ffn_W2, np.float32)),
        b1=np.ascontiguousarray(np.asarray(L0_ffn_b1, np.float32)),
        b2=np.ascontiguousarray(np.asarray(L0_ffn_b2, np.float32)),
    )
    consts = [_consts(0), _consts(1)]
    in_maps = []
    for c in range(8):
        b, h = c // 2, c % 2
        xb = x[b].reshape(8, 512, D)[_prog_perm(h)].reshape(T, D)
        m = dict(shared)
        m.update(consts[h])
        m["xin"] = np.ascontiguousarray(xb)
        in_maps.append(m)
    return in_maps


def kernel(L0_attn_x, L0_attn_W_in, L0_attn_b_in, L0_attn_W_out, L0_attn_b_out,
           L0_ffn_gate, L0_ffn_W1, L0_ffn_b1, L0_ffn_W2, L0_ffn_b2):
    in_maps = make_in_maps(L0_attn_x, L0_attn_W_in, L0_attn_b_in,
                           L0_attn_W_out, L0_attn_b_out, L0_ffn_gate,
                           L0_ffn_W1, L0_ffn_b1, L0_ffn_W2, L0_ffn_b2)
    nc = build_program()
    res = run_bass_kernel_spmd(nc, in_maps, core_ids=list(range(8)))
    out = np.empty((4, T, D), dtype=np.float32)
    for c in range(8):
        b, h = c // 2, c % 2
        yc = res.results[c]["y"]
        for g in range(NG):
            out[b, 512 * (2 * g + h):512 * (2 * g + h) + 512] = \
                yc[g * 512:(g + 1) * 512]
    return out


# version 10
# speedup vs baseline: 1.3814x; 1.0130x over previous
import contextlib

import numpy as np
import ml_dtypes
import concourse.bass as bass
import concourse.mybir as mybir
from concourse.bass_utils import run_bass_kernel_spmd

F32 = mybir.dt.float32
BF16 = mybir.dt.bfloat16
AF = mybir.ActivationFunctionType
ALU = mybir.AluOpType
AX = mybir.AxisListType

T = 4096
D = 1024
NPAIR = 8
DT = 8
NE = 16
HID = 2048
NTOK = 2048
NG = 4
LN_EPS = 1e-5
GATE_EPS = 1e-12


class Prog:
    ENGINES = ("sync", "gpsimd", "scalar", "vector", "tensor")
    ENG_OF_SEM = {"PE": "tensor", "ACT": "scalar", "DVE": "vector",
                  "PL": "gpsimd", "SP": "sync"}

    def __init__(self, nc):
        self.nc = nc
        self.ops = []
        self.ctr = {}
        self.sem_names = set()

    def op(self, eng, fn, waits=(), inc=None, amt=1, hard=()):
        evs = tuple(w for w in waits if w is not None)
        hevs = tuple(w for w in hard if w is not None)
        for (s, _v) in evs + hevs:
            self.sem_names.add(s)
        ev = None
        if inc is not None:
            self.ctr[inc] = self.ctr.get(inc, 0) + amt
            self.sem_names.add(inc)
            ev = (inc, self.ctr[inc])
        self.ops.append((eng, fn, evs, inc, amt, ev, hevs))
        return ev

    SER_ENGINES = ("vector", "scalar", "gpsimd")

    def emit(self, stack, sim_serial=False):
        for eng in self.ENGINES:
            if (sim_serial or eng in self.SER_ENGINES) and any(
                    o[0] == eng for o in self.ops):
                self.sem_names.add("ser_" + eng)
        sems = {s: stack.enter_context(self.nc.semaphore(s))
                for s in sorted(self.sem_names)}
        eng_of = self.ENG_OF_SEM
        with self.nc.Block() as blk:
            for eng_name in self.ENGINES:
                ops = [o for o in self.ops if o[0] == eng_name]
                if not ops:
                    continue

                ser = sim_serial or eng_name in self.SER_ENGINES

                def body(e, ops=ops, eng_name=eng_name, ser=ser):
                    waited = {}
                    ser_val = 0
                    last_ev = None
                    for (_eng, fn, evs, inc, amt, ev, hevs) in ops:
                        for (s, v) in evs:
                            if not ser and eng_of.get(s) == eng_name:
                                continue
                            if waited.get(s, 0) >= v:
                                continue
                            e.wait_ge(sems[s], v)
                            waited[s] = v
                        for (s, v) in hevs:
                            if waited.get(s, 0) >= v:
                                continue
                            e.wait_ge(sems[s], v)
                            waited[s] = v
                        if ser and last_ev is not None:
                            s, v = last_ev
                            if waited.get(s, 0) < v:
                                e.wait_ge(sems[s], v)
                                waited[s] = v
                        ins = fn(e)
                        if ins is not None:
                            if inc is not None:
                                ins.then_inc(sems[inc], amt)
                            is_dma = (inc or "").startswith("d_")
                            if ser and not is_dma:
                                if inc is None:
                                    ser_val += 1
                                    ins.then_inc(
                                        sems["ser_" + eng_name], 1)
                                    last_ev = ("ser_" + eng_name, ser_val)
                                else:
                                    last_ev = ev

                getattr(blk, eng_name)(body)


def build_program(tap=None, sim_serial=False):
    nc = bass.Bass("TRN2", target_bir_lowering=False)

    xin = nc.dram_tensor("xin", [T, D], F32, kind="ExternalInput")
    winT_in = nc.dram_tensor("winT", [D, 3072], F32, kind="ExternalInput")
    binT_in = nc.dram_tensor("binT", [128, 24], F32, kind="ExternalInput")
    woutT_in = nc.dram_tensor("woutT", [D, D], F32, kind="ExternalInput")
    bout_in = nc.dram_tensor("bout", [1, D], F32, kind="ExternalInput")
    gate_in = nc.dram_tensor("gate", [D, NE], F32, kind="ExternalInput")
    w1_in = nc.dram_tensor("w1", [NE, D, HID], F32, kind="ExternalInput")
    w2_in = nc.dram_tensor("w2", [NE, HID, D], F32, kind="ExternalInput")
    b1_in = nc.dram_tensor("b1", [NE, HID], F32, kind="ExternalInput")
    b2_in = nc.dram_tensor("b2", [NE, D], F32, kind="ExternalInput")
    c_idbf_in = nc.dram_tensor("c_idbf", [128, 128], BF16, kind="ExternalInput")
    c_idf_in = nc.dram_tensor("c_idf", [128, 128], F32, kind="ExternalInput")
    c_perm_in = nc.dram_tensor("c_perm", [128, 128], BF16, kind="ExternalInput")
    c_csk_in = nc.dram_tensor("c_csk", [128, T], BF16, kind="ExternalInput")
    c_cwk_in = nc.dram_tensor("c_cwk", [128, T], BF16, kind="ExternalInput")
    c_msk_in = nc.dram_tensor("c_msk", [128, 4 * 512], BF16,
                              kind="ExternalInput")
    c_mscal_in = nc.dram_tensor("c_mscal", [128, 4], F32, kind="ExternalInput")
    c_kill_in = nc.dram_tensor("c_kill", [128, 1], F32, kind="ExternalInput")
    c_esel_in = nc.dram_tensor("c_esel", [16, NE * 128], BF16,
                               kind="ExternalInput")
    c_or_in = nc.dram_tensor("c_onesrow", [1, 128], F32, kind="ExternalInput")
    c_hsel_in = nc.dram_tensor("c_hsel", [128, 33], BF16, kind="ExternalInput")
    c_or33_in = nc.dram_tensor("c_or33", [33, 128], F32, kind="ExternalInput")
    c_oc_in = nc.dram_tensor("c_onescol", [128, 1], F32, kind="ExternalInput")
    c_o512_in = nc.dram_tensor("c_ones512", [1, 512], BF16,
                               kind="ExternalInput")

    y_out = nc.dram_tensor("y", [NTOK, D], F32, kind="ExternalOutput")
    tap_shapes = {
        "xT": ([128, DT * T], BF16),
        "qT": ([128, NTOK], BF16), "kT": ([128, T], BF16),
        "V": ([128, 32 * 130], BF16), "yT": ([128, NPAIR * NTOK], BF16),
        "s1": ([128, DT * NTOK], F32), "x2": ([128, DT * NTOK], F32),
        "gT": ([16, NTOK], BF16), "gw": ([128, 96], F32),
        "acc0": ([128, DT * NTOK], F32),
        "acc": ([128, DT * NTOK], F32),
    }
    tap_out = None
    if tap is not None:
        shp, tdt = tap_shapes[tap]
        tap_out = nc.dram_tensor("tapout", shp, tdt, kind="ExternalOutput")

    P = Prog(nc)
    stack = contextlib.ExitStack()

    def sbl(name, shape, dt):
        return stack.enter_context(nc.sbuf_tensor("sb_" + name, shape, dt))

    def finish_tap(src_ap, waits):
        ev = P.op("sync", lambda e: e.dma_start(tap_out.ap(), src_ap),
                  waits=waits, inc="d_y", amt=16)
        tails = [(s, P.ctr[s]) for s in sorted(P.ctr) if s.startswith("d_")]
        P.op("sync", lambda e: e.nop(), waits=[ev] + tails)
        P.emit(stack, sim_serial=sim_serial)
        stack.close()
        return nc

    with stack:
        c_idbf = sbl("c_idbf", [128, 128], BF16)
        c_idf = sbl("c_idf", [128, 128], F32)
        c_perm = sbl("c_perm", [128, 128], BF16)
        c_or = sbl("c_or", [1, 128], F32)
        c_oc = sbl("c_oc", [128, 1], F32)
        c_o512 = sbl("c_o512", [1, 512], BF16)
        binT = sbl("binT", [128, 24], F32)
        boutbf = sbl("boutbf", [1, D], BF16)
        b2bf = sbl("b2bf", [16, D], BF16)
        gatef = sbl("gatef", [128, DT * NE], F32)
        c_mscal = sbl("c_mscal", [128, 4], F32)
        c_kill = sbl("c_kill", [128, 1], F32)
        c_hsel = sbl("c_hsel", [128, 33], BF16)
        c_or33 = sbl("c_or33", [33, 128], F32)
        normcol = sbl("normcol", [33, 16], F32)
        gwork = sbl("gwork", [128, 272], F32)
        fwork = sbl("fwork", [128, 8], F32)
        negMcol = sbl("negMcol", [128, 4], F32)
        mwork = sbl("mwork", [1, 8], F32)

        pbd = [stack.enter_context(nc.psum_tensor(f"pbd{i}", [128, 1024],
                                                   F32)) for i in range(2)]
        pbs = [stack.enter_context(nc.psum_tensor(f"pb{i}", [128, 512], F32))
               for i in range(4, 8)]
        pb = [pbd[0][:, 0:512], pbd[0][:, 512:1024],
              pbd[1][:, 0:512], pbd[1][:, 512:1024]] + \
             [t[:] for t in pbs]

        st_ac = contextlib.ExitStack()

        def sbr(st, name, shape, dt):
            return st.enter_context(
                nc.sbuf_tensor("sb_" + name, shape, dt, side="right"))

        yT_all = sbr(st_ac, "yT_all", [128, NPAIR * NTOK], BF16)
        wrow = sbr(st_ac, "wrow", [1, 2048], F32)
        xOwn2 = sbr(st_ac, "xOwn2", [128, DT * NTOK], BF16)
        st_x = contextlib.ExitStack()
        xT = sbr(st_x, "xT", [128, DT * T], BF16)

        def dma_to(dst, src_ap):
            return lambda e: e.dma_start(dst, src_ap)

        for dst, src in ((c_idbf, c_idbf_in), (c_idf, c_idf_in),
                         (c_perm, c_perm_in),
                         (c_or, c_or_in), (c_oc, c_oc_in), (c_o512, c_o512_in),
                         (binT, binT_in), (c_mscal, c_mscal_in),
                         (c_hsel, c_hsel_in), (c_or33, c_or33_in),
                         (c_kill, c_kill_in)):
            P.op("sync", dma_to(dst[:], src.ap()), inc="d_c", amt=16)
        ev_consts = P.op("sync", lambda e: e.dma_start(
            gatef[:].rearrange("p (i e) -> p i e", i=DT),
            gate_in.ap().rearrange("(i p) e -> p i e", p=128)),
            inc="d_c", amt=16)
        P.op("gpsimd", lambda e: e.dma_start(boutbf[:], bout_in.ap()),
             inc="d_cg", amt=16)
        ev_cg = P.op("gpsimd", lambda e: e.dma_start(b2bf[:], b2_in.ap()),
                     inc="d_cg", amt=16)

        from concourse.hw_specs import get_activation_tables
        set_idx = list(get_activation_tables(nc.m.arch)).index(
            "natural_log_exp_and_others")

        def load_tab(e):
            return e.add_instruction(mybir.InstLoadActFuncSet(
                name=nc.get_next_instruction_name(), act_func_set_id=set_idx,
                ins=[], outs=[]))
        P.op("scalar", load_tab)

        st_a = contextlib.ExitStack()
        astage = sbr(st_a, "astage", [128, 2 * 4096], BF16)
        ev_chunk_mm = [None] * 8
        evac_ev = [None] * 4
        last_evac = {"vector": None, "scalar": None}
        for c in range(8):
            par = c % 2
            ev_dx = P.op(
                "gpsimd", lambda e, c=c, par=par: e.dma_start(
                    astage[:, par * 4096:(par + 1) * 4096]
                    .rearrange("p (n d) -> p n d", n=4),
                    xin.ap()[c * 512:(c + 1) * 512, :]
                    .rearrange("(n p) d -> p n d", p=128)),
                waits=[ev_chunk_mm[c - 2]] if c >= 2 else [],
                inc=f"d_x{par}", amt=16)
            for nsub in range(4):
                for i in range(DT):
                    tl = nsub * DT + i
                    bank = tl % 4
                    src = astage[:, par * 4096 + nsub * 1024 + i * 128:
                                 par * 4096 + nsub * 1024 + (i + 1) * 128]
                    mm = P.op("tensor", lambda e, src=src, bank=bank:
                              e.matmul(pb[bank][:, 0:128], src, c_idbf[:],
                                       start=True, stop=True),
                              waits=[ev_dx, evac_ev[bank], ev_consts],
                              inc="PE")
                    ev_chunk_mm[c] = mm
                    dst = xT[:, i * T + c * 512 + nsub * 128:
                             i * T + c * 512 + (nsub + 1) * 128]
                    veng = "vector" if (tl % 2 == 0) else "scalar"
                    sem = "DVE" if veng == "vector" else "ACT"

                    def evac(e, dst=dst, bank=bank, veng=veng):
                        if veng == "vector":
                            return e.tensor_copy(dst, pb[bank][:, 0:128])
                        return e.copy(dst, pb[bank][:, 0:128])
                    evac_ev[bank] = P.op(veng, evac, waits=[mm], inc=sem)
                    last_evac[veng] = evac_ev[bank]
        ev_x_ready = [last_evac["vector"], last_evac["scalar"]]
        ev_a_pe = ev_chunk_mm[7]
        st_a.close()

        if tap == "xT":
            return finish_tap(xT[:], ev_x_ready)

        st_b = contextlib.ExitStack()
        kT = sbr(st_b, "kT", [128, T], BF16)
        V_sb = sbr(st_b, "V_sb", [128, 32 * 130], BF16)
        qT = sbr(st_b, "qT", [128, NTOK], BF16)
        winp = sbr(st_b, "winp", [128, 3072], BF16)
        masks = sbr(st_b, "masks", [128, 4 * 512], BF16)
        cs_k = sbr(st_b, "cs_k", [128, T], BF16)
        cw_k = sbr(st_b, "cw_k", [128, T], BF16)
        qkstage = sbr(st_b, "qkstage", [128, 2 * 512], BF16)
        vstage = sbr(st_b, "vstage", [128, 2 * 512], BF16)
        rtmp1 = sbr(st_b, "rtmp1", [128, 512], BF16)
        rtmp2 = sbr(st_b, "rtmp2", [128, 512], BF16)
        Abuf = sbr(st_b, "Abuf", [128, 4 * 1024], BF16)
        r1sb = rtmp1[0:64, :]
        r2sb = rtmp2[0:64, :]

        ev_btab = None
        for dst, src in ((masks, c_msk_in), (cs_k, c_csk_in),
                         (cw_k, c_cwk_in)):
            ev_btab = P.op("sync", dma_to(dst[:], src.ap()),
                           waits=[ev_a_pe] + ev_x_ready, inc="d_c", amt=16)

        ev_vones = P.op("vector", lambda e: e.memset(
            V_sb[:].rearrange("p (k two g) -> p k two g", two=2, g=65)
            [:, :, :, 64:65], 1.0), waits=[ev_a_pe])

        ev_winp = [None] * (NPAIR + 1)
        ev_winp[0] = P.op("gpsimd", lambda e: e.dma_start(
            winp[:].rearrange("p (i c) -> p i c", i=DT),
            winT_in.ap()[:, 0:384].rearrange("(i p) c -> p i c", p=128)),
            waits=[ev_consts, ev_a_pe] + ev_x_ready, inc="d_wp", amt=16)

        ev_inproj_done = [None] * NPAIR
        ev_norm = [None, None]
        ev_rbc_used = None
        ev_pb7_used = None
        b_free = [None, None, None, None]
        ev_vcopy = None
        ev_stage_free = [None, None]
        ev_vstage_free = [None, None]

        for p in range(NPAIR):
            ev_rope_q = None
            ev_rope_k = None
            last_inproj_mm = None
            pend = None

            chunk_list = ([(0, g) for g in range(NG)]
                          + [(1, kc) for kc in range(8)]
                          + [(2, kc) for kc in range(8)])

            for idx, (kind, ci) in enumerate(chunk_list):
                bank = idx % 2
                par = idx % 2
                tok0 = (2 * ci) * 512 if kind == 0 else ci * 512
                rhs_l = [xT[:, i * T + tok0:i * T + tok0 + 512]
                         for i in range(DT)]
                mm = None
                for i in range(DT):
                    lhs = winp[:, i * 384 + kind * 128:
                               i * 384 + (kind + 1) * 128]
                    mm = P.op("tensor", lambda e, lhs=lhs, rhs=rhs_l[i],
                              bank=bank, st=(i == 0), sp=(i == DT - 1):
                              e.matmul(pb[bank][:, 0:512], lhs, rhs,
                                       start=st, stop=sp),
                              waits=[ev_winp[p], b_free[bank]] + ev_x_ready,
                              inc="PE" if i == DT - 1 else None)
                last_inproj_mm = mm

                def make_stage2(kind=kind, ci=ci, bank=bank, par=par, mm=mm,
                                p=p, tok0=tok0):
                    def stage2():
                        nonlocal ev_rope_q, ev_rope_k, ev_vcopy
                        bcol = binT[:, 3 * p + kind:3 * p + kind + 1]
                        if kind < 2:
                            stg = qkstage[:, par * 512:(par + 1) * 512]
                            evs = P.op("scalar", lambda e, bank=bank,
                                       bcol=bcol, stg=stg: e.activation(
                                           stg, pb[bank][:, 0:512],
                                           AF.Identity, bias=bcol),
                                       waits=[mm, ev_stage_free[par],
                                              ev_btab], inc="ACT")
                            b_free[bank] = evs
                            swb = 2 + par
                            sw = P.op("tensor", lambda e, swb=swb, stg=stg:
                                      e.matmul(pb[swb][:, 0:512], c_perm[:],
                                               stg, start=True, stop=True),
                                      waits=[evs, b_free[swb], ev_consts],
                                      inc="PE")
                            dst = (qT if kind == 0 else kT)
                            off = ci * 512
                            P.op("vector", lambda e, tok0=tok0, stg=stg:
                                 e.tensor_mul(rtmp1[:], stg,
                                              cs_k[:, tok0:tok0 + 512]),
                                 waits=[evs, ev_btab])
                            P.op("vector", lambda e, tok0=tok0, swb=swb:
                                 e.tensor_mul(rtmp2[:], pb[swb][:, 0:512],
                                              cw_k[:, tok0:tok0 + 512]),
                                 waits=[sw, ev_btab])
                            t3 = P.op("vector", lambda e, dst=dst, off=off:
                                      e.tensor_add(dst[:, off:off + 512],
                                                   rtmp1[:], rtmp2[:]),
                                      inc="DVE")
                            b_free[swb] = t3
                            ev_stage_free[par] = t3
                            if kind == 0:
                                ev_rope_q = t3
                            else:
                                ev_rope_k = t3
                        else:
                            vstg = vstage[:, par * 512:(par + 1) * 512]
                            evs = P.op("scalar", lambda e, bank=bank,
                                       bcol=bcol, vstg=vstg: e.activation(
                                           vstg, pb[bank][:, 0:512],
                                           AF.Identity, bias=bcol),
                                       waits=[mm, ev_vstage_free[par],
                                              ev_btab], inc="ACT")
                            b_free[bank] = evs
                            cp = None
                            for m in range(4):
                                swb = 2 + m % 2
                                tr = P.op("tensor", lambda e, m=m, swb=swb,
                                          vstg=vstg: e.matmul(
                                              pb[swb][:, 0:128],
                                              vstg[:, m * 128:(m + 1) * 128],
                                              c_idbf[:], start=True,
                                              stop=True),
                                          waits=[evs, b_free[swb],
                                                 ev_consts], inc="PE")
                                base = (ci * 4 + m) * 130

                                def vcopy(e, base=base, swb=swb):
                                    dst = V_sb[:, base:base + 130].rearrange(
                                        "p (two g) -> p two g",
                                        two=2)[:, :, 0:64]
                                    src = pb[swb][:, 0:128].rearrange(
                                        "p (two g) -> p two g", two=2)
                                    return e.tensor_copy(dst, src)
                                cp = P.op("vector", vcopy,
                                          waits=[tr, ev_vones], inc="DVE")
                                b_free[swb] = cp
                            ev_vstage_free[par] = cp
                            ev_vcopy = cp
                    return stage2

                if pend is not None:
                    pend()
                pend = make_stage2()
            pend()
            ev_inproj_done[p] = last_inproj_mm

            if p + 1 < NPAIR:
                ev_winp[p + 1] = P.op("gpsimd", lambda e, p=p: e.dma_start(
                    winp[:].rearrange("p (i c) -> p i c", i=DT),
                    winT_in.ap()[:, (p + 1) * 384:(p + 2) * 384]
                    .rearrange("(i p) c -> p i c", p=128)),
                    waits=[last_inproj_mm], inc="d_wp", amt=16)

            if tap == "qT" and p == 0:
                return finish_tap(qT[:], [ev_rope_q])
            if tap == "kT" and p == 0:
                return finish_tap(kT[:], [ev_rope_k])
            if tap == "V" and p == 0:
                return finish_tap(V_sb[:], [ev_vcopy])

            ev_stats = None
            for ci in range(12):
                src_buf = qT if ci < 4 else kT
                off = (ci * 512) if ci < 4 else ((ci - 4) * 512)
                spar = ci % 2
                bk = 6 + ci % 2
                sqev = P.op("scalar", lambda e, src_buf=src_buf, off=off,
                            spar=spar: e.activation(
                                qkstage[:, spar * 512:(spar + 1) * 512],
                                src_buf[:, off:off + 512], AF.Square),
                            waits=[ev_rope_q if ci < 4 else ev_rope_k,
                                   ev_stage_free[spar]],
                            inc="ACT")
                smm = P.op("tensor", lambda e, spar=spar, bk=bk: e.matmul(
                    pb[bk][0:33, 0:512], c_hsel[:],
                    qkstage[:, spar * 512:(spar + 1) * 512],
                    start=True, stop=True),
                    waits=[sqev,
                           ev_rbc_used if bk == 6 else ev_pb7_used,
                           ev_consts], inc="PE")
                ev_stats = P.op("vector", lambda e, ci=ci, bk=bk:
                                e.reduce_max(
                                    normcol[:, ci:ci + 1],
                                    pb[bk][0:33, 0:512], axis=AX.X),
                                waits=[smm], inc="DVE")
                if bk == 6:
                    ev_rbc_used = ev_stats
                else:
                    ev_pb7_used = ev_stats
                ev_stage_free[spar] = smm
            ch = ev_stats
            for fng in (
                lambda e: e.reduce_max(normcol[:, 12:13], normcol[:, 0:4],
                                       axis=AX.X),
                lambda e: e.reduce_max(normcol[:, 13:14], normcol[:, 4:12],
                                       axis=AX.X),
                lambda e: e.tensor_mul(normcol[:, 14:15], normcol[:, 12:13],
                                       normcol[:, 13:14]),
                lambda e: e.tensor_scalar_max(normcol[:, 14:15],
                                              normcol[:, 14:15], 1e-12),
            ):
                ch = P.op("vector", fng, hard=[ch], inc="DVE")
            lnp = ch
            ln1 = P.op("scalar", lambda e: e.activation(
                normcol[:, 15:16], normcol[:, 14:15], AF.Ln), waits=[lnp])
            sq1 = P.op("scalar", lambda e: e.activation(
                normcol[:, 14:15], normcol[:, 15:16], AF.Exp, scale=0.5),
                inc="ACT")
            ngm = P.op("vector", lambda e: e.tensor_scalar(
                normcol[:, 15:16], normcol[:, 14:15], -0.128875, -1.0,
                ALU.mult, ALU.add), waits=[sq1], hard=[lnp], inc="DVE")
            ev_negM = [None, None]
            for head in (0, 1):
                nb = P.op("tensor", lambda e, head=head: e.matmul(
                    pb[6][:, 0:1],
                    c_or33[32 * head:32 * head + 1, :],
                    normcol[32 * head:32 * head + 1, 15:16],
                    start=True, stop=True),
                    waits=[ngm, ev_rbc_used, ev_consts], inc="PE")
                evn = P.op(
                    "vector", lambda e, head=head: e.tensor_copy(
                        negMcol[:, head:head + 1], pb[6][:, 0:1]),
                    waits=[nb], inc="DVE")
                ev_negM[head] = P.op(
                    "vector", lambda e, head=head: e.tensor_sub(
                        negMcol[:, 2 + head:3 + head],
                        negMcol[:, head:head + 1], c_kill[:]),
                    waits=[ev_consts], hard=[evn], inc="DVE")
                ev_rbc_used = ev_negM[head]

            for g in range(NG):
                nkt = 8 * g + 8
                nvp = nkt // 2
                ev_exp = {}
                ev_av = {}
                ev_y_last = [None, None]

                def emit_av(vp, head, g=g, nkt=nkt, nvp=nvp):
                    ypb = 4 + head
                    av = None
                    ab = (head * 2 + vp % 2) * 1024
                    for v in (2 * vp, 2 * vp + 1):
                        vsl = V_sb[:, v * 130 + head * 65:
                                   v * 130 + head * 65 + 65]
                        asl = Abuf[:, ab + (v % 2) * 512:
                                   ab + (v % 2) * 512 + 512]
                        av = P.op("tensor", lambda e, ypb=ypb, vsl=vsl,
                                  asl=asl, st=(v == 0), sp=(v == nkt - 1):
                                  e.matmul(pb[ypb][0:65, :], vsl, asl,
                                           start=st, stop=sp),
                                  waits=[ev_exp[(head, vp)], ev_vcopy,
                                         ev_norm[head] if v == 0 else None],
                                  inc="PE" if v % 2 == 1 else None)
                    ev_av[(head, vp)] = av
                    if vp == nvp - 1:
                        ev_y_last[head] = av

                for vp in range(nvp):
                    v0 = 2 * vp
                    for head in (0, 1):
                        smm = None
                        for v in (v0, v0 + 1):
                            hb = head * 2 + v % 2
                            ks = kT[64 * head:64 * head + 64,
                                    v * 128:(v + 1) * 128]
                            qs = qT[64 * head:64 * head + 64,
                                    g * 512:(g + 1) * 512]
                            smm = P.op("tensor", lambda e, hb=hb, ks=ks,
                                       qs=qs, tp=(64 * head, 0): e.matmul(
                                           pb[hb][:, 0:512], ks, qs,
                                           start=True, stop=True,
                                           tile_position=tp),
                                       waits=[ev_rope_q, ev_rope_k,
                                              b_free[hb]],
                                       inc="PE" if v % 2 == 1 else None)
                        u = v0 - 8 * g
                        bcol_i = head if u < 4 else 2 + head
                        ab = (head * 2 + vp % 2) * 1024
                        asl2 = Abuf[:, ab:ab + 1024]
                        ex = P.op("scalar", lambda e, head=head, asl2=asl2,
                                  bcol_i=bcol_i: e.activation(
                                      asl2, pbd[head][:, 0:1024], AF.Exp,
                                      bias=negMcol[:, bcol_i:bcol_i + 1],
                                      scale=0.125),
                                  waits=[smm, ev_negM[head]],
                                  inc="ACT")
                        b_free[head * 2] = ex
                        b_free[head * 2 + 1] = ex
                        if 0 <= u < 4:
                            ex = P.op("vector",
                                      lambda e, asl2=asl2, u=u:
                                      e.tensor_mul(
                                          asl2, asl2,
                                          masks[:, u * 512:(u + 2) * 512]),
                                      waits=[ex, ev_btab], inc="DVE")
                        ev_exp[(head, vp)] = ex
                    if vp >= 1:
                        for head in (0, 1):
                            emit_av(vp - 1, head)
                for head in (0, 1):
                    emit_av(nvp - 1, head)

                for head in (0, 1):
                    ypb = 4 + head
                    rrow = wrow[0:1, head * 512:head * 512 + 512]
                    lnrow = wrow[0:1, 1024 + head * 512:1536 + head * 512]
                    rsb = r1sb if head == 0 else r2sb
                    rc1 = P.op("scalar", lambda e, ypb=ypb, lnrow=lnrow:
                               e.activation(lnrow, pb[ypb][64:65, :],
                                            AF.Ln),
                               waits=[ev_y_last[head]], inc="ACT")
                    rc = P.op("scalar", lambda e, rrow=rrow, lnrow=lnrow:
                              e.activation(rrow, lnrow, AF.Exp,
                                           scale=-1.0),
                              waits=[rc1], inc="ACT")
                    rb = P.op("tensor", lambda e, rrow=rrow: e.matmul(
                        pb[6][0:64, :], c_or[0:1, 0:64], rrow,
                        start=True, stop=True),
                        waits=[rc, ev_rbc_used, ev_consts], inc="PE")
                    rcp = P.op("vector", lambda e, rsb=rsb: e.tensor_copy(
                        rsb, pb[6][0:64, :]), waits=[rb], inc="DVE")
                    ev_rbc_used = rcp
                    dst = yT_all[64 * head:64 * head + 64,
                                 p * NTOK + g * 512:p * NTOK + (g + 1) * 512]
                    ev_norm[head] = P.op(
                        "vector", lambda e, dst=dst, ypb=ypb, rsb=rsb:
                        e.tensor_mul(dst, pb[ypb][0:64, :], rsb),
                        inc="DVE")

        ev_yT_done = [ev_norm[0], ev_norm[1]]
        if tap == "yT":
            return finish_tap(yT_all[:], ev_yT_done)

        st_b.close()

        ev_xo = None
        for g in range(NG):
            for i in range(DT):
                src = xT[:, i * T + (2 * g) * 512:i * T + (2 * g) * 512 + 512]
                dst = xOwn2[:, i * NTOK + g * 512:i * NTOK + (g + 1) * 512]
                ev_xo = P.op("vector", lambda e, src=src, dst=dst:
                             e.tensor_copy(dst, src),
                             waits=[ev_inproj_done[NPAIR - 1]], inc="DVE")
        st_x.close()

        ev_b_done = [ev_inproj_done[NPAIR - 1]] + ev_yT_done + \
            [ev_rbc_used, ev_xo]

        s_f32 = sbl("s_f32", [128, DT * NTOK], F32)
        x2bf = sbl("x2bf", [128, DT * NTOK], BF16)
        gT_sb = sbl("gT_sb", [16, NTOK], BF16)
        c_esel = sbl("c_esel", [16, NE * 128], BF16)
        ev_esel = P.op("sync", dma_to(c_esel[:], c_esel_in.ap()),
                       waits=ev_b_done, inc="d_c", amt=16)

        st_c = contextlib.ExitStack()
        woutsb = sbr(st_c, "woutsb", [128, NPAIR * D], BF16)
        sqstage = sbr(st_c, "sqstage", [128, 512], F32)

        ev_wout = P.op("gpsimd", lambda e: e.dma_start(
            woutsb[:].rearrange("p (i d) -> p i d", i=DT),
            woutT_in.ap().rearrange("(i p) d -> p i d", p=128)),
            waits=ev_b_done, inc="d_wo", amt=16)

        ev_s = {}
        op_bank_free = [None, None]
        for g in range(NG):
            for d in range(DT):
                obk = d % 2
                for p in range(NPAIR):
                    lhs = woutsb[:, p * D + d * 128:p * D + (d + 1) * 128]
                    rhs = yT_all[:, p * NTOK + g * 512:
                                 p * NTOK + (g + 1) * 512]
                    P.op("tensor", lambda e, lhs=lhs, rhs=rhs, obk=obk,
                         st=(p == 0): e.matmul(pb[obk][:, 0:512], lhs, rhs,
                                               start=st, stop=False),
                         waits=[ev_wout, ev_yT_done[0], ev_yT_done[1],
                                op_bank_free[obk]])
                mm = P.op("tensor", lambda e, d=d, obk=obk: e.matmul(
                    pb[obk][:, 0:512], boutbf[0:1, d * 128:(d + 1) * 128],
                    c_o512[:], start=False, stop=True),
                    waits=[ev_consts, ev_cg], inc="PE")
                xo = xOwn2[:, d * NTOK + g * 512:d * NTOK + (g + 1) * 512]
                ssl = s_f32[:, d * NTOK + g * 512:d * NTOK + (g + 1) * 512]
                ev_s[(g, d)] = P.op("vector", lambda e, ssl=ssl, obk=obk,
                                    xo=xo: e.tensor_add(
                                        ssl, pb[obk][:, 0:512], xo),
                                    waits=[mm, ev_xo], inc="DVE")
                op_bank_free[obk] = ev_s[(g, d)]

        if tap == "s1":
            return finish_tap(s_f32[:], [ev_s[(NG - 1, DT - 1)]])

        ev_x2 = {}
        sq_free = None
        for g in range(NG):
            st_mm = None
            sq_mm = None
            for d in range(DT):
                ssl = s_f32[:, d * NTOK + g * 512:d * NTOK + (g + 1) * 512]
                sq = P.op("vector", lambda e, ssl=ssl: e.tensor_mul(
                    sqstage[:], ssl, ssl),
                    waits=[ev_s[(g, d)], sq_free], inc="DVE")
                st_mm = P.op("tensor", lambda e, ssl=ssl, st=(d == 0),
                             sp=(d == DT - 1): e.matmul(
                                 pb[2][0:1, :], c_oc[:], ssl, start=st,
                                 stop=sp),
                             waits=[ev_s[(g, d)], ev_consts], inc="PE")
                sq_mm = P.op("tensor", lambda e, st=(d == 0),
                             sp=(d == DT - 1): e.matmul(
                                 pb[3][0:1, :], c_oc[:], sqstage[:],
                                 start=st, stop=sp),
                             waits=[sq], inc="PE")
                sq_free = sq_mm
            ch = P.op("vector", lambda e: e.tensor_scalar_mul(
                wrow[0:1, 0:512], pb[2][0:1, :], 1.0 / D), waits=[st_mm],
                inc="DVE")
            for fng in (
                lambda e: e.tensor_scalar_mul(wrow[0:1, 512:1024],
                                              pb[3][0:1, :], 1.0 / D),
                lambda e: e.tensor_mul(wrow[0:1, 1024:1536], wrow[0:1, 0:512],
                                       wrow[0:1, 0:512]),
                lambda e: e.tensor_sub(wrow[0:1, 512:1024],
                                       wrow[0:1, 512:1024],
                                       wrow[0:1, 1024:1536]),
                lambda e: e.tensor_scalar_add(wrow[0:1, 512:1024],
                                              wrow[0:1, 512:1024], LN_EPS),
            ):
                ch = P.op("vector", fng, waits=[sq_mm], hard=[ch], inc="DVE")
            ve_r = ch
            P.op("scalar", lambda e: e.activation(
                wrow[0:1, 1024:1536], wrow[0:1, 512:1024], AF.Ln),
                waits=[ve_r])
            inv_r = P.op("scalar", lambda e: e.activation(
                wrow[0:1, 1536:2048], wrow[0:1, 1024:1536], AF.Exp,
                scale=-0.5), inc="ACT")
            mb = P.op("tensor", lambda e: e.matmul(
                pb[4][:, 0:512], c_or[:], wrow[0:1, 0:512], start=True,
                stop=True), waits=[ve_r, ev_consts], inc="PE")
            ib = P.op("tensor", lambda e: e.matmul(
                pb[5][:, 0:512], c_or[:], wrow[0:1, 1536:2048], start=True,
                stop=True), waits=[inv_r, ev_consts], inc="PE")
            last_n = None
            for d in range(DT):
                ssl = s_f32[:, d * NTOK + g * 512:d * NTOK + (g + 1) * 512]
                P.op("vector", lambda e, ssl=ssl: e.tensor_sub(
                    ssl, ssl, pb[4][:, 0:512]), waits=[mb])
                last_n = P.op("vector", lambda e, ssl=ssl: e.tensor_mul(
                    ssl, ssl, pb[5][:, 0:512]), waits=[ib], inc="DVE")
                x2sl = x2bf[:, d * NTOK + g * 512:d * NTOK + (g + 1) * 512]
                ev_x2[(g, d)] = P.op("scalar", lambda e, ssl=ssl, x2sl=x2sl:
                                     e.copy(x2sl, ssl), waits=[last_n],
                                     inc="ACT")
            ev_s[(g, "ln")] = last_n

        ev_c_done = [ev_s[(NG - 1, "ln")], ev_x2[(NG - 1, DT - 1)]]
        if tap == "x2":
            return finish_tap(s_f32[:], ev_c_done)

        st_c.close()
        st_ac.close()

        gw = gwork

        def SL(c):
            return gw[:, c:c + 64].rearrange("p (t e) -> p t e", e=16)

        def CL(c):
            return gw[:, c:c + 4].rearrange("p (t o) -> p t o", o=1)

        def BC(c):
            return CL(c).broadcast_to((128, 4, 16))

        ev_gT = {}
        ev_pb1_free = ev_s[(NG - 1, DT - 1)]
        ev_c3_free = None
        for b4 in range(4):
            mm = None
            for q in range(4):
                tt = b4 * 4 + q
                for d in range(DT):
                    lhs = s_f32[:, d * NTOK + tt * 128:
                                d * NTOK + (tt + 1) * 128]
                    mm = P.op("tensor", lambda e, lhs=lhs, d=d, tt=tt,
                              st=(d == 0), sp=(d == DT - 1): e.matmul(
                                  pb[0][:, tt * 16:(tt + 1) * 16], lhs,
                                  gatef[:, d * NE:(d + 1) * NE], start=st,
                                  stop=sp),
                              waits=[ev_s[(b4, "ln")], ev_consts],
                              inc="PE" if (d == DT - 1 and q == 3)
                              else None)
            pslab = pb[0][:, b4 * 64:b4 * 64 + 64].rearrange(
                "p (t e) -> p t e", e=16)
            nm = P.op("vector", lambda e, pslab=pslab: e.reduce_max(
                CL(256), pslab, axis=AX.X, negate=True),
                waits=[mm], inc="DVE")
            xs = P.op("vector", lambda e, pslab=pslab: e.tensor_tensor(
                SL(0), pslab, BC(256), ALU.add), hard=[nm], inc="DVE")
            exl = P.op("scalar", lambda e: e.activation(
                gw[:, 64:128], gw[:, 0:64], AF.Exp), waits=[xs], inc="ACT")
            ch = exl
            gate_ops = [
                lambda e: e.reduce_max(CL(260), SL(64), axis=AX.X),
                lambda e: e.tensor_tensor(SL(128), SL(64), BC(260),
                                          ALU.is_ge),
                lambda e: e.scalar_tensor_tensor(SL(128), SL(128), -1e9,
                                                 SL(64), ALU.mult, ALU.add),
                lambda e: e.reduce_max(CL(260), SL(128), axis=AX.X),
                lambda e: e.tensor_tensor(SL(0), SL(128), BC(260),
                                          ALU.is_ge),
                lambda e: e.scalar_tensor_tensor(SL(128), SL(0), -1e9,
                                                 SL(128), ALU.mult,
                                                 ALU.add),
                lambda e: e.reduce_max(CL(260), SL(128), axis=AX.X),
                lambda e: e.tensor_tensor(SL(0), SL(64), BC(260),
                                          ALU.is_ge),
                lambda e: e.tensor_tensor(SL(0), SL(0), SL(64), ALU.mult),
                lambda e: e.reduce_sum(CL(264), SL(0), axis=AX.X),
                lambda e: e.tensor_scalar_max(gw[:, 264:268],
                                              gw[:, 264:268], GATE_EPS),
                lambda e: e.reciprocal(gw[:, 268:272], gw[:, 264:268]),
            ]
            for fng in gate_ops:
                ch = P.op("vector", fng, hard=[ch], inc="DVE")
            gout = P.op("vector", lambda e: e.tensor_tensor(
                SL(192), SL(0), BC(268), ALU.mult),
                waits=[ev_c3_free], hard=[ch], inc="DVE")
            gtr = None
            for q in range(4):
                tt = b4 * 4 + q
                gtr = P.op("tensor", lambda e, q=q: e.matmul(
                    pb[1][0:16, 0:128],
                    gw[:, 192 + q * 16:208 + q * 16],
                    c_idf[:], start=True, stop=True),
                    waits=[gout, ev_pb1_free, ev_consts], inc="PE")
                ev_gT[tt] = P.op("vector", lambda e, tt=tt: e.tensor_copy(
                    gT_sb[:, tt * 128:(tt + 1) * 128], pb[1][0:16, 0:128]),
                    waits=[gtr], inc="DVE")
                ev_pb1_free = ev_gT[tt]
            ev_c3_free = gtr
        ev_gT_done = ev_gT[15]

        if tap == "gT":
            return finish_tap(gT_sb[:], [ev_gT_done])
        if tap == "gw":
            return finish_tap(gwork[:], [ev_gT_done])

        st_e = contextlib.ExitStack()
        w1bf = sbr(st_e, "w1bf", [128, DT * HID], BF16)
        w2bf = sbr(st_e, "w2bf", [128, 16 * D], BF16)
        hgT = sbr(st_e, "hgT", [128, 16 * 512], BF16)
        xg = sbr(st_e, "xg", [128, DT * 512], BF16)
        b1ct = sbr(st_e, "b1ct", [128, 16 * NE], F32)
        b1st = sbr(st_e, "b1st", [16, 128 * NE], BF16)
        gsb = sbr(st_e, "gsb", [128, 512], BF16)

        ev_w1 = [[None, None] for _ in range(NE + 1)]
        ev_w2 = [[None, None] for _ in range(NE + 1)]
        ev_ffn_space = ev_b_done + ev_c_done + [ev_gT_done]

        def dma_w1_half(e, ei, hlf):
            return e.dma_start(
                w1bf[:].rearrange("p (i m) -> p i m", i=DT)
                [:, :, hlf * 1024:(hlf + 1) * 1024],
                w1_in.ap()[ei].rearrange("(i p) m -> p i m", p=128)
                [:, :, hlf * 1024:(hlf + 1) * 1024])

        def dma_w2_half(e, ei, hlf):
            return e.dma_start(
                w2bf[:].rearrange("p (j d) -> p j d", j=16)
                [:, :, hlf * 512:(hlf + 1) * 512],
                w2_in.ap()[ei].rearrange("(j p) d -> p j d", p=128)
                [:, :, hlf * 512:(hlf + 1) * 512])

        for hlf in (0, 1):
            ev_w1[0][hlf] = P.op(
                "gpsimd", lambda e, hlf=hlf: dma_w1_half(e, 0, hlf),
                waits=ev_ffn_space, inc=f"d_w1{hlf}", amt=16)
            ev_w2[0][hlf] = P.op(
                "gpsimd", lambda e, hlf=hlf: dma_w2_half(e, 0, hlf),
                waits=ev_ffn_space, inc=f"d_w2{hlf}", amt=16)
        ev_b1d = P.op("gpsimd", lambda e: e.dma_start(
            b1st[:].rearrange("j (E p) -> j E p", E=NE),
            b1_in.ap().rearrange("E (j p) -> j E p", p=128)),
            waits=ev_ffn_space, inc="d_b10", amt=16)
        btr = None
        for e2 in range(NE):
            btr = P.op("tensor", lambda e, e2=e2: e.matmul(
                pb[7][:, e2 * 16:(e2 + 1) * 16],
                b1st[:, e2 * 128:(e2 + 1) * 128],
                c_idbf[0:16, 0:16], start=True, stop=True),
                waits=[ev_b1d, ev_pb7_used, ev_consts],
                inc="PE" if e2 == NE - 1 else None)
        ev_b1ct = P.op("vector", lambda e: e.tensor_copy(
            b1ct[:], pb[7][:, 0:16 * NE]), waits=[btr], inc="DVE")

        ev_b2p = {}
        b2_bank_free = [None, None]
        for cc in range(NG):
            for i in range(DT):
                ob = 5 + i % 2
                mm = P.op("tensor", lambda e, i=i, cc=cc, ob=ob: e.matmul(
                    pb[ob][:, 0:512], b2bf[:, i * 128:(i + 1) * 128],
                    gT_sb[:, cc * 512:(cc + 1) * 512], start=True, stop=True),
                    waits=[ev_consts, ev_cg, ev_gT_done,
                           b2_bank_free[i % 2]],
                    inc="PE")
                ssl = s_f32[:, i * NTOK + cc * 512:i * NTOK + (cc + 1) * 512]
                ev_b2p[(cc, i)] = P.op(
                    "vector", lambda e, ssl=ssl, ob=ob: e.tensor_add(
                        ssl, pb[ob][:, 0:512], ssl),
                    waits=[mm, ev_s[(cc, "ln")]], inc="DVE")
                b2_bank_free[i % 2] = ev_b2p[(cc, i)]

        ev_gb_free = None
        ev_xg_free = None
        ev_h_free = [None, None]
        ev_o_free = [None, None]
        ev_hg_free = {}
        ev_evac = {}
        chunks = [(ei, cc) for ei in range(NE) for cc in range(NG)]
        xgl_t = {}

        def emit_gxg(t):
            nonlocal ev_gb_free
            ei, cc = chunks[t]
            gb = P.op("tensor", lambda e, ei=ei, cc=cc: e.matmul(
                pb[0][:, 0:512], c_esel[:, ei * 128:(ei + 1) * 128],
                gT_sb[:, cc * 512:(cc + 1) * 512], start=True, stop=True),
                waits=[ev_gT_done, ev_gb_free, ev_esel],
                inc="PE")
            gcp = P.op("vector", lambda e: e.tensor_copy(
                gsb[:], pb[0][:, 0:512]), waits=[gb], inc="DVE")
            ev_gb_free = gcp
            xgl = None
            for i in range(DT):
                x2sl = x2bf[:, i * NTOK + cc * 512:
                            i * NTOK + (cc + 1) * 512]
                xgl = P.op("vector", lambda e, i=i, x2sl=x2sl:
                           e.tensor_mul(xg[:, i * 512:(i + 1) * 512],
                                        x2sl, gsb[:]),
                           waits=[gcp, ev_xg_free, ev_x2[(cc, i)]],
                           inc="DVE" if i == DT - 1 else None)
            xgl_t[t] = xgl

        for t, (ei, cc) in enumerate(chunks):
                if t == 0:
                    emit_gxg(0)
                xgl = xgl_t.pop(t)
                m1 = None
                ev_relu_last = None
                for j in range(16):
                    hb = 1 + j % 2
                    for i in range(DT):
                        lhs = w1bf[:, i * HID + j * 128:
                                   i * HID + (j + 1) * 128]
                        m1 = P.op("tensor", lambda e, lhs=lhs, i=i, hb=hb,
                                  st=(i == 0), sp=(i == DT - 1): e.matmul(
                                      pb[hb][:, 0:512], lhs,
                                      xg[:, i * 512:(i + 1) * 512], start=st,
                                      stop=sp),
                                  waits=[xgl, ev_w1[ei][j // 8],
                                         ev_h_free[hb - 1]],
                                  inc="PE" if i == DT - 1 else None)
                    stt = P.op("vector", lambda e, hb=hb, j=j, ei=ei:
                               e.scalar_tensor_tensor(
                                   pb[hb][:, 0:512], gsb[:],
                                   b1ct[:, ei * 16 + j:ei * 16 + j + 1],
                                   pb[hb][:, 0:512], ALU.mult, ALU.add),
                               waits=[m1, ev_b1ct], inc="DVE")
                    rl = P.op("scalar", lambda e, j=j, hb=hb: e.activation(
                        hgT[:, j * 512:(j + 1) * 512], pb[hb][:, 0:512],
                        AF.Relu), waits=[stt, ev_hg_free.get(j)], inc="ACT")
                    ev_h_free[hb - 1] = rl
                    ev_relu_last = rl
                    if cc == NG - 1 and j == 7:
                        ev_w1A_used = m1
                ev_xg_free = m1
                m2 = None
                for i in range(DT):
                    ob = 3 + i % 2
                    for j in range(16):
                        lhs = w2bf[:, j * D + i * 128:j * D + (i + 1) * 128]
                        m2 = P.op("tensor", lambda e, lhs=lhs, ob=ob, j=j,
                                  st=(j == 0), sp=(j == 15): e.matmul(
                                      pb[ob][:, 0:512], lhs,
                                      hgT[:, j * 512:(j + 1) * 512],
                                      start=st, stop=sp),
                                  waits=[ev_relu_last, ev_w2[ei][i // 4],
                                         ev_o_free[ob - 3]],
                                  inc="PE" if j == 15 else None)
                    ssl = s_f32[:, i * NTOK + cc * 512:
                                i * NTOK + (cc + 1) * 512]
                    ee = P.op("vector", lambda e, ssl=ssl, ob=ob:
                              e.tensor_add(ssl, pb[ob][:, 0:512], ssl),
                              waits=[m2, ev_s[(cc, "ln")]], inc="DVE")
                    ev_evac[(ei, cc, i)] = ee
                    ev_o_free[ob - 3] = ee
                    if i == 0 and t + 1 < len(chunks):
                        emit_gxg(t + 1)
                    if cc == NG - 1 and i == 3:
                        ev_w2C_used = m2
                for jj in range(16):
                    ev_hg_free[jj] = m2
                if cc != NG - 1:
                    continue
                if ei + 1 < NE:
                    ev_w1[ei + 1][0] = P.op(
                        "gpsimd", lambda e, ei=ei: dma_w1_half(e, ei + 1, 0),
                        waits=[ev_w1A_used], inc="d_w10", amt=16)
                    ev_w1[ei + 1][1] = P.op(
                        "gpsimd", lambda e, ei=ei: dma_w1_half(e, ei + 1, 1),
                        waits=[ev_xg_free], inc="d_w11", amt=16)
                    ev_w2[ei + 1][0] = P.op(
                        "gpsimd", lambda e, ei=ei: dma_w2_half(e, ei + 1, 0),
                        waits=[ev_w2C_used], inc="d_w20", amt=16)
                    ev_w2[ei + 1][1] = P.op(
                        "gpsimd", lambda e, ei=ei: dma_w2_half(e, ei + 1, 1),
                        waits=[ev_hg_free[0]], inc="d_w21", amt=16)
                if tap == "acc0" and ei == 0:
                    return finish_tap(
                        s_f32[:], [ev_evac[(0, c2, i2)] for c2 in range(NG)
                                   for i2 in range(DT)])

        if tap == "acc":
            return finish_tap(
                s_f32[:], [ev_evac[(NE - 1, c2, i2)] for c2 in range(NG)
                           for i2 in range(DT)])

        st_e.close()

        st_f = contextlib.ExitStack()
        xtok = sbr(st_f, "xtok", [128, 2 * D], F32)
        otile = sbr(st_f, "otile", [128, 2 * D], F32)
        sqf = sbr(st_f, "sqf", [128, D], F32)
        ev_out = [None] * 16
        ev_tr_free = [None, None]
        ev_sq_free = None
        ev_ot_free = [None, None]
        ev_f_space = [ev_evac[(NE - 1, NG - 1, DT - 1)], ev_hg_free[0]]
        for tt in range(16):
            par = tt % 2
            g = tt // 4
            cps = None
            for i in range(DT):
                trb = i % 2
                lhs = s_f32[:, i * NTOK + tt * 128:i * NTOK + (tt + 1) * 128]
                tr = P.op("tensor", lambda e, lhs=lhs, trb=trb: e.matmul(
                    pb[trb][:, 0:128], lhs, c_idf[:], start=True, stop=True),
                    waits=[ev_evac[(NE - 1, g, i)], ev_tr_free[trb],
                           ev_consts],
                    inc="PE")
                cps = P.op("vector", lambda e, i=i, par=par, trb=trb:
                           e.tensor_copy(
                               xtok[:, par * D + i * 128:
                                    par * D + (i + 1) * 128],
                               pb[trb][:, 0:128]),
                           waits=[tr] + (ev_f_space if tt == 0 else []),
                           inc="DVE")
                ev_tr_free[trb] = cps
            ch = P.op("vector", lambda e, par=par: e.tensor_mul(
                sqf[:], xtok[:, par * D:(par + 1) * D],
                xtok[:, par * D:(par + 1) * D]),
                waits=[cps, ev_sq_free], inc="DVE")
            for fng in (
                lambda e, par=par: e.reduce_sum(
                    fwork[:, 0:1], xtok[:, par * D:(par + 1) * D], axis=AX.X),
                lambda e: e.reduce_sum(fwork[:, 1:2], sqf[:], axis=AX.X),
                lambda e: e.tensor_scalar_mul(fwork[:, 2:3], fwork[:, 0:1],
                                              1.0 / D),
                lambda e: e.tensor_scalar_mul(fwork[:, 3:4], fwork[:, 1:2],
                                              1.0 / D),
                lambda e: e.tensor_mul(fwork[:, 4:5], fwork[:, 2:3],
                                       fwork[:, 2:3]),
                lambda e: e.tensor_sub(fwork[:, 5:6], fwork[:, 3:4],
                                       fwork[:, 4:5]),
                lambda e: e.tensor_scalar_add(fwork[:, 5:6], fwork[:, 5:6],
                                              LN_EPS),
            ):
                ch = P.op("vector", fng, hard=[ch], inc="DVE")
            vv = ch
            ev_sq_free = vv
            P.op("scalar", lambda e: e.activation(
                fwork[:, 6:7], fwork[:, 5:6], AF.Ln), waits=[vv])
            inv = P.op("scalar", lambda e: e.activation(
                fwork[:, 7:8], fwork[:, 6:7], AF.Exp, scale=-0.5), inc="ACT")
            ot = P.op("vector", lambda e, par=par: e.tensor_scalar(
                otile[:, par * D:(par + 1) * D],
                xtok[:, par * D:(par + 1) * D],
                fwork[:, 2:3], fwork[:, 7:8], ALU.subtract, ALU.mult),
                waits=[inv, ev_ot_free[par]], hard=[vv], inc="DVE")
            ev_out[tt] = P.op("sync", lambda e, tt=tt, par=par: e.dma_start(
                y_out.ap()[tt * 128:(tt + 1) * 128, :],
                otile[:, par * D:(par + 1) * D]),
                waits=[ot], inc=f"d_y{par}", amt=16)
            ev_ot_free[par] = ev_out[tt]

        tails = [(s, P.ctr[s]) for s in sorted(P.ctr) if s.startswith("d_")]
        P.op("sync", lambda e: e.nop(),
             waits=[ev_out[14], ev_out[15]] + tails)
        P.emit(stack, sim_serial=sim_serial)
    return nc


def _rope_tables():
    half = 32
    p = np.arange(half, dtype=np.float64) / (half - 1)
    thetas = 1.0 * (10000.0 / 1.0) ** p
    pos = np.arange(T, dtype=np.float64)[:, None] / thetas[None, :]
    sin = np.sin(pos).T
    cos = np.cos(pos).T
    cs = np.concatenate([sin, -sin, sin, -sin], axis=0)
    cw = np.concatenate([cos, cos, cos, cos], axis=0)
    return cs, cw


def _prog_perm(h):
    blocks = np.arange(8)
    if h == 1:
        blocks = blocks ^ 1
    return blocks


def _consts(h):
    idn = np.eye(128, dtype=np.float64)
    perm = np.zeros((128, 128), dtype=np.float64)
    for j in range(128):
        swap = j + 32 if (j % 64) < 32 else j - 32
        perm[swap, j] = 1.0
    cs, cw = _rope_tables()
    pos_prog = np.concatenate(
        [np.arange(512 * gb, 512 * gb + 512) for gb in _prog_perm(h)])
    msk = np.zeros((128, 4 * 512), dtype=np.float64)
    pcol = np.arange(128)[:, None]
    jcol = np.arange(512)[None, :]
    for u in range(4):
        msk[:, u * 512:(u + 1) * 512] = (128 * u + pcol <= jcol)
    mscal = np.full((128, 4), float(h), dtype=np.float32)
    kill = np.full((128, 1), 0.0 if h == 1 else 40000.0, dtype=np.float32)
    esel = np.zeros((16, NE * 128), dtype=np.float64)
    for e in range(NE):
        esel[e, e * 128:(e + 1) * 128] = 1.0
    hsel = np.zeros((128, 33), dtype=np.float64)
    hsel[0:64, 0] = 1.0
    hsel[64:128, 32] = 1.0
    bf = ml_dtypes.bfloat16
    return {
        "c_hsel": hsel.astype(bf),
        "c_idbf": idn.astype(bf), "c_idf": idn.astype(np.float32),
        "c_perm": perm.astype(bf),
        "c_csk": np.ascontiguousarray(cs[:, pos_prog]).astype(bf),
        "c_cwk": np.ascontiguousarray(cw[:, pos_prog]).astype(bf),
        "c_msk": msk.astype(bf), "c_mscal": mscal, "c_kill": kill,
        "c_esel": esel.astype(bf),
        "c_onesrow": np.ones((1, 128), np.float32),
        "c_or33": np.ones((33, 128), np.float32),
        "c_onescol": np.ones((128, 1), np.float32),
        "c_ones512": np.ones((1, 512), bf),
    }


def make_in_maps(L0_attn_x, L0_attn_W_in, L0_attn_b_in, L0_attn_W_out,
                 L0_attn_b_out, L0_ffn_gate, L0_ffn_W1, L0_ffn_b1, L0_ffn_W2,
                 L0_ffn_b2):
    x = np.ascontiguousarray(np.asarray(L0_attn_x, dtype=np.float32))
    W_in = np.asarray(L0_attn_W_in, dtype=np.float32)
    b_in = np.asarray(L0_attn_b_in, dtype=np.float32)
    W_out = np.asarray(L0_attn_W_out, dtype=np.float32)
    b_out = np.asarray(L0_attn_b_out, dtype=np.float32)

    winT = np.concatenate(
        [np.stack([W_in[128 * p:128 * (p + 1)].T,
                   W_in[1024 + 128 * p:1024 + 128 * (p + 1)].T,
                   W_in[2048 + 128 * p:2048 + 128 * (p + 1)].T], axis=1)
         .reshape(D, 384) for p in range(NPAIR)], axis=1)
    winT = np.ascontiguousarray(winT)
    binT = np.zeros((128, 24), dtype=np.float32)
    for p in range(NPAIR):
        binT[:, 3 * p] = b_in[128 * p:128 * (p + 1)]
        binT[:, 3 * p + 1] = b_in[1024 + 128 * p:1024 + 128 * (p + 1)]
        binT[:, 3 * p + 2] = b_in[2048 + 128 * p:2048 + 128 * (p + 1)]
    shared = dict(
        winT=winT, binT=binT,
        woutT=np.ascontiguousarray(W_out.T),
        bout=np.ascontiguousarray(b_out[None, :]),
        gate=np.ascontiguousarray(np.asarray(L0_ffn_gate, np.float32)),
        w1=np.ascontiguousarray(np.asarray(L0_ffn_W1, np.float32)),
        w2=np.ascontiguousarray(np.asarray(L0_ffn_W2, np.float32)),
        b1=np.ascontiguousarray(np.asarray(L0_ffn_b1, np.float32)),
        b2=np.ascontiguousarray(np.asarray(L0_ffn_b2, np.float32)),
    )
    consts = [_consts(0), _consts(1)]
    in_maps = []
    for c in range(8):
        b, h = c // 2, c % 2
        xb = x[b].reshape(8, 512, D)[_prog_perm(h)].reshape(T, D)
        m = dict(shared)
        m.update(consts[h])
        m["xin"] = np.ascontiguousarray(xb)
        in_maps.append(m)
    return in_maps


def kernel(L0_attn_x, L0_attn_W_in, L0_attn_b_in, L0_attn_W_out, L0_attn_b_out,
           L0_ffn_gate, L0_ffn_W1, L0_ffn_b1, L0_ffn_W2, L0_ffn_b2):
    in_maps = make_in_maps(L0_attn_x, L0_attn_W_in, L0_attn_b_in,
                           L0_attn_W_out, L0_attn_b_out, L0_ffn_gate,
                           L0_ffn_W1, L0_ffn_b1, L0_ffn_W2, L0_ffn_b2)
    nc = build_program()
    res = run_bass_kernel_spmd(nc, in_maps, core_ids=list(range(8)))
    out = np.empty((4, T, D), dtype=np.float32)
    for c in range(8):
        b, h = c // 2, c % 2
        yc = res.results[c]["y"]
        for g in range(NG):
            out[b, 512 * (2 * g + h):512 * (2 * g + h) + 512] = \
                yc[g * 512:(g + 1) * 512]
    return out


# version 11
# speedup vs baseline: 1.3917x; 1.0074x over previous
import contextlib

import numpy as np
import ml_dtypes
import concourse.bass as bass
import concourse.mybir as mybir
from concourse.bass_utils import run_bass_kernel_spmd

F32 = mybir.dt.float32
BF16 = mybir.dt.bfloat16
AF = mybir.ActivationFunctionType
ALU = mybir.AluOpType
AX = mybir.AxisListType

T = 4096
D = 1024
NPAIR = 8
DT = 8
NE = 16
HID = 2048
NTOK = 2048
NG = 4
LN_EPS = 1e-5
GATE_EPS = 1e-12


class Prog:
    ENGINES = ("sync", "gpsimd", "scalar", "vector", "tensor")
    ENG_OF_SEM = {"PE": "tensor", "ACT": "scalar", "DVE": "vector",
                  "PL": "gpsimd", "SP": "sync"}

    def __init__(self, nc):
        self.nc = nc
        self.ops = []
        self.ctr = {}
        self.sem_names = set()

    def op(self, eng, fn, waits=(), inc=None, amt=1, hard=()):
        evs = tuple(w for w in waits if w is not None)
        hevs = tuple(w for w in hard if w is not None)
        for (s, _v) in evs + hevs:
            self.sem_names.add(s)
        ev = None
        if inc is not None:
            self.ctr[inc] = self.ctr.get(inc, 0) + amt
            self.sem_names.add(inc)
            ev = (inc, self.ctr[inc])
        self.ops.append((eng, fn, evs, inc, amt, ev, hevs))
        return ev

    SER_ENGINES = ("vector", "scalar", "gpsimd")

    def emit(self, stack, sim_serial=False):
        for eng in self.ENGINES:
            if (sim_serial or eng in self.SER_ENGINES) and any(
                    o[0] == eng for o in self.ops):
                self.sem_names.add("ser_" + eng)
        sems = {s: stack.enter_context(self.nc.semaphore(s))
                for s in sorted(self.sem_names)}
        eng_of = self.ENG_OF_SEM
        with self.nc.Block() as blk:
            for eng_name in self.ENGINES:
                ops = [o for o in self.ops if o[0] == eng_name]
                if not ops:
                    continue

                ser = sim_serial or eng_name in self.SER_ENGINES

                def body(e, ops=ops, eng_name=eng_name, ser=ser):
                    waited = {}
                    ser_val = 0
                    last_ev = None
                    for (_eng, fn, evs, inc, amt, ev, hevs) in ops:
                        for (s, v) in evs:
                            if not ser and eng_of.get(s) == eng_name:
                                continue
                            if waited.get(s, 0) >= v:
                                continue
                            e.wait_ge(sems[s], v)
                            waited[s] = v
                        for (s, v) in hevs:
                            if waited.get(s, 0) >= v:
                                continue
                            e.wait_ge(sems[s], v)
                            waited[s] = v
                        if ser and last_ev is not None:
                            s, v = last_ev
                            if waited.get(s, 0) < v:
                                e.wait_ge(sems[s], v)
                                waited[s] = v
                        ins = fn(e)
                        if ins is not None:
                            if inc is not None:
                                ins.then_inc(sems[inc], amt)
                            is_dma = (inc or "").startswith("d_")
                            if ser and not is_dma:
                                if inc is None:
                                    ser_val += 1
                                    ins.then_inc(
                                        sems["ser_" + eng_name], 1)
                                    last_ev = ("ser_" + eng_name, ser_val)
                                else:
                                    last_ev = ev

                getattr(blk, eng_name)(body)


def build_program(tap=None, sim_serial=False):
    nc = bass.Bass("TRN2", target_bir_lowering=False)

    xin = nc.dram_tensor("xin", [T, D], F32, kind="ExternalInput")
    winT_in = nc.dram_tensor("winT", [D, 3072], F32, kind="ExternalInput")
    binT_in = nc.dram_tensor("binT", [128, 24], F32, kind="ExternalInput")
    woutT_in = nc.dram_tensor("woutT", [D, D], F32, kind="ExternalInput")
    bout_in = nc.dram_tensor("bout", [1, D], F32, kind="ExternalInput")
    gate_in = nc.dram_tensor("gate", [D, NE], F32, kind="ExternalInput")
    w1_in = nc.dram_tensor("w1", [NE, D, HID], F32, kind="ExternalInput")
    w2_in = nc.dram_tensor("w2", [NE, HID, D], F32, kind="ExternalInput")
    b1_in = nc.dram_tensor("b1", [NE, HID], F32, kind="ExternalInput")
    b2_in = nc.dram_tensor("b2", [NE, D], F32, kind="ExternalInput")
    c_idbf_in = nc.dram_tensor("c_idbf", [128, 128], BF16, kind="ExternalInput")
    c_idf_in = nc.dram_tensor("c_idf", [128, 128], F32, kind="ExternalInput")
    c_perm_in = nc.dram_tensor("c_perm", [128, 128], BF16, kind="ExternalInput")
    c_csk_in = nc.dram_tensor("c_csk", [128, T], BF16, kind="ExternalInput")
    c_cwk_in = nc.dram_tensor("c_cwk", [128, T], BF16, kind="ExternalInput")
    c_msk_in = nc.dram_tensor("c_msk", [128, 4 * 512], BF16,
                              kind="ExternalInput")
    c_mscal_in = nc.dram_tensor("c_mscal", [128, 4], F32, kind="ExternalInput")
    c_kill_in = nc.dram_tensor("c_kill", [128, 1], F32, kind="ExternalInput")
    c_esel_in = nc.dram_tensor("c_esel", [16, NE * 128], BF16,
                               kind="ExternalInput")
    c_or_in = nc.dram_tensor("c_onesrow", [1, 128], F32, kind="ExternalInput")
    c_hsel_in = nc.dram_tensor("c_hsel", [128, 33], BF16, kind="ExternalInput")
    c_or33_in = nc.dram_tensor("c_or33", [33, 128], F32, kind="ExternalInput")
    c_oc_in = nc.dram_tensor("c_onescol", [128, 1], F32, kind="ExternalInput")
    c_o512_in = nc.dram_tensor("c_ones512", [1, 512], BF16,
                               kind="ExternalInput")

    y_out = nc.dram_tensor("y", [NTOK, D], F32, kind="ExternalOutput")
    tap_shapes = {
        "xT": ([128, DT * T], BF16),
        "qT": ([128, NTOK], BF16), "kT": ([128, T], BF16),
        "V": ([128, 32 * 130], BF16), "yT": ([128, NPAIR * NTOK], BF16),
        "s1": ([128, DT * NTOK], F32), "x2": ([128, DT * NTOK], F32),
        "gT": ([16, NTOK], BF16), "gw": ([128, 96], F32),
        "acc0": ([128, DT * NTOK], F32),
        "acc": ([128, DT * NTOK], F32),
    }
    tap_out = None
    if tap is not None:
        shp, tdt = tap_shapes[tap]
        tap_out = nc.dram_tensor("tapout", shp, tdt, kind="ExternalOutput")

    P = Prog(nc)
    stack = contextlib.ExitStack()

    def sbl(name, shape, dt):
        return stack.enter_context(nc.sbuf_tensor("sb_" + name, shape, dt))

    def finish_tap(src_ap, waits):
        ev = P.op("sync", lambda e: e.dma_start(tap_out.ap(), src_ap),
                  waits=waits, inc="d_y", amt=16)
        tails = [(s, P.ctr[s]) for s in sorted(P.ctr) if s.startswith("d_")]
        P.op("sync", lambda e: e.nop(), waits=[ev] + tails)
        P.emit(stack, sim_serial=sim_serial)
        stack.close()
        return nc

    with stack:
        c_idbf = sbl("c_idbf", [128, 128], BF16)
        c_idf = sbl("c_idf", [128, 128], F32)
        c_perm = sbl("c_perm", [128, 128], BF16)
        c_or = sbl("c_or", [1, 128], F32)
        c_oc = sbl("c_oc", [128, 1], F32)
        c_o512 = sbl("c_o512", [1, 512], BF16)
        binT = sbl("binT", [128, 24], F32)
        boutbf = sbl("boutbf", [1, D], BF16)
        b2bf = sbl("b2bf", [16, D], BF16)
        gatef = sbl("gatef", [128, DT * NE], F32)
        c_mscal = sbl("c_mscal", [128, 4], F32)
        c_kill = sbl("c_kill", [128, 1], F32)
        c_hsel = sbl("c_hsel", [128, 33], BF16)
        c_or33 = sbl("c_or33", [33, 128], F32)
        normcol = sbl("normcol", [33, 16], F32)
        gwork = sbl("gwork", [128, 272], F32)
        fwork = sbl("fwork", [128, 8], F32)
        negMcol = sbl("negMcol", [128, 4], F32)
        mwork = sbl("mwork", [1, 8], F32)

        pbd = [stack.enter_context(nc.psum_tensor(f"pbd{i}", [128, 1024],
                                                   F32)) for i in range(2)]
        pbs = [stack.enter_context(nc.psum_tensor(f"pb{i}", [128, 512], F32))
               for i in range(4, 8)]
        pb = [pbd[0][:, 0:512], pbd[0][:, 512:1024],
              pbd[1][:, 0:512], pbd[1][:, 512:1024]] + \
             [t[:] for t in pbs]

        st_ac = contextlib.ExitStack()

        def sbr(st, name, shape, dt):
            return st.enter_context(
                nc.sbuf_tensor("sb_" + name, shape, dt, side="right"))

        yT_all = sbr(st_ac, "yT_all", [128, NPAIR * NTOK], BF16)
        wrow = sbr(st_ac, "wrow", [1, 2048], F32)
        xOwn2 = sbr(st_ac, "xOwn2", [128, DT * NTOK], BF16)
        st_x = contextlib.ExitStack()
        xT = sbr(st_x, "xT", [128, DT * T], BF16)

        def dma_to(dst, src_ap):
            return lambda e: e.dma_start(dst, src_ap)

        for dst, src in ((c_idbf, c_idbf_in), (c_idf, c_idf_in),
                         (c_perm, c_perm_in),
                         (c_or, c_or_in), (c_oc, c_oc_in), (c_o512, c_o512_in),
                         (binT, binT_in), (c_mscal, c_mscal_in),
                         (c_hsel, c_hsel_in), (c_or33, c_or33_in),
                         (c_kill, c_kill_in)):
            P.op("sync", dma_to(dst[:], src.ap()), inc="d_c", amt=16)
        ev_consts = P.op("sync", lambda e: e.dma_start(
            gatef[:].rearrange("p (i e) -> p i e", i=DT),
            gate_in.ap().rearrange("(i p) e -> p i e", p=128)),
            inc="d_c", amt=16)
        P.op("gpsimd", lambda e: e.dma_start(boutbf[:], bout_in.ap()),
             inc="d_cg", amt=16)
        ev_cg = P.op("gpsimd", lambda e: e.dma_start(b2bf[:], b2_in.ap()),
                     inc="d_cg", amt=16)

        from concourse.hw_specs import get_activation_tables
        set_idx = list(get_activation_tables(nc.m.arch)).index(
            "natural_log_exp_and_others")

        def load_tab(e):
            return e.add_instruction(mybir.InstLoadActFuncSet(
                name=nc.get_next_instruction_name(), act_func_set_id=set_idx,
                ins=[], outs=[]))
        P.op("scalar", load_tab)

        st_a = contextlib.ExitStack()
        astage = sbr(st_a, "astage", [128, 2 * 4096], BF16)
        ev_chunk_mm = [None] * 8
        evac_ev = [None] * 4
        last_evac = {"vector": None, "scalar": None}
        for c in range(8):
            par = c % 2
            ev_dx = P.op(
                "gpsimd", lambda e, c=c, par=par: e.dma_start(
                    astage[:, par * 4096:(par + 1) * 4096]
                    .rearrange("p (n d) -> p n d", n=4),
                    xin.ap()[c * 512:(c + 1) * 512, :]
                    .rearrange("(n p) d -> p n d", p=128)),
                waits=[ev_chunk_mm[c - 2]] if c >= 2 else [],
                inc=f"d_x{par}", amt=16)
            for nsub in range(4):
                for i in range(DT):
                    tl = nsub * DT + i
                    bank = tl % 4
                    src = astage[:, par * 4096 + nsub * 1024 + i * 128:
                                 par * 4096 + nsub * 1024 + (i + 1) * 128]
                    mm = P.op("tensor", lambda e, src=src, bank=bank:
                              e.matmul(pb[bank][:, 0:128], src, c_idbf[:],
                                       start=True, stop=True),
                              waits=[ev_dx, evac_ev[bank], ev_consts],
                              inc="PE")
                    ev_chunk_mm[c] = mm
                    dst = xT[:, i * T + c * 512 + nsub * 128:
                             i * T + c * 512 + (nsub + 1) * 128]
                    veng = "vector" if (tl % 2 == 0) else "scalar"
                    sem = "DVE" if veng == "vector" else "ACT"

                    def evac(e, dst=dst, bank=bank, veng=veng):
                        if veng == "vector":
                            return e.tensor_copy(dst, pb[bank][:, 0:128])
                        return e.copy(dst, pb[bank][:, 0:128])
                    evac_ev[bank] = P.op(veng, evac, waits=[mm], inc=sem)
                    last_evac[veng] = evac_ev[bank]
        ev_x_ready = [last_evac["vector"], last_evac["scalar"]]
        ev_a_pe = ev_chunk_mm[7]
        st_a.close()

        if tap == "xT":
            return finish_tap(xT[:], ev_x_ready)

        st_b = contextlib.ExitStack()
        kT = sbr(st_b, "kT", [128, T], BF16)
        V_sb = sbr(st_b, "V_sb", [128, 32 * 130], BF16)
        qT = sbr(st_b, "qT", [128, NTOK], BF16)
        winp = sbr(st_b, "winp", [128, 3072], BF16)
        masks = sbr(st_b, "masks", [128, 4 * 512], BF16)
        cs_k = sbr(st_b, "cs_k", [128, T], BF16)
        cw_k = sbr(st_b, "cw_k", [128, T], BF16)
        qkstage = sbr(st_b, "qkstage", [128, 2 * 512], BF16)
        vstage = sbr(st_b, "vstage", [128, 2 * 512], BF16)
        rtmp1 = sbr(st_b, "rtmp1", [128, 512], BF16)
        rtmp2 = sbr(st_b, "rtmp2", [128, 512], BF16)
        Abuf = sbr(st_b, "Abuf", [128, 4 * 1024], BF16)
        r1sb = rtmp1[0:64, :]
        r2sb = rtmp2[0:64, :]

        ev_btab = None
        for dst, src in ((masks, c_msk_in), (cs_k, c_csk_in),
                         (cw_k, c_cwk_in)):
            ev_btab = P.op("sync", dma_to(dst[:], src.ap()),
                           waits=[ev_a_pe] + ev_x_ready, inc="d_c", amt=16)

        ev_vones = P.op("vector", lambda e: e.memset(
            V_sb[:].rearrange("p (k two g) -> p k two g", two=2, g=65)
            [:, :, :, 64:65], 1.0), waits=[ev_a_pe])

        ev_winp = [None] * (NPAIR + 1)
        ev_winp[0] = P.op("gpsimd", lambda e: e.dma_start(
            winp[:].rearrange("p (i c) -> p i c", i=DT),
            winT_in.ap()[:, 0:384].rearrange("(i p) c -> p i c", p=128)),
            waits=[ev_consts, ev_a_pe] + ev_x_ready, inc="d_wp", amt=16)

        ev_inproj_done = [None] * NPAIR
        ev_norm = [None, None]
        ev_rbc_used = None
        ev_pb7_used = None
        b_free = [None, None, None, None]
        ev_vcopy = None
        ev_stage_free = [None, None]
        ev_vstage_free = [None, None]

        for p in range(NPAIR):
            ev_rope_q = None
            ev_rope_k = None
            last_inproj_mm = None
            pend = None

            chunk_list = ([(0, g) for g in range(NG)]
                          + [(1, kc) for kc in range(8)]
                          + [(2, kc) for kc in range(8)])

            for idx, (kind, ci) in enumerate(chunk_list):
                bank = idx % 2
                par = idx % 2
                tok0 = (2 * ci) * 512 if kind == 0 else ci * 512
                rhs_l = [xT[:, i * T + tok0:i * T + tok0 + 512]
                         for i in range(DT)]
                mm = None
                for i in range(DT):
                    lhs = winp[:, i * 384 + kind * 128:
                               i * 384 + (kind + 1) * 128]
                    mm = P.op("tensor", lambda e, lhs=lhs, rhs=rhs_l[i],
                              bank=bank, st=(i == 0), sp=(i == DT - 1):
                              e.matmul(pb[bank][:, 0:512], lhs, rhs,
                                       start=st, stop=sp),
                              waits=[ev_winp[p], b_free[bank]] + ev_x_ready,
                              inc="PE" if i == DT - 1 else None)
                last_inproj_mm = mm

                def make_stage2(kind=kind, ci=ci, bank=bank, par=par, mm=mm,
                                p=p, tok0=tok0):
                    def stage2():
                        nonlocal ev_rope_q, ev_rope_k, ev_vcopy
                        bcol = binT[:, 3 * p + kind:3 * p + kind + 1]
                        if kind < 2:
                            stg = qkstage[:, par * 512:(par + 1) * 512]
                            evs = P.op("scalar", lambda e, bank=bank,
                                       bcol=bcol, stg=stg: e.activation(
                                           stg, pb[bank][:, 0:512],
                                           AF.Identity, bias=bcol),
                                       waits=[mm, ev_stage_free[par],
                                              ev_btab], inc="ACT")
                            b_free[bank] = evs
                            swb = 2 + par
                            sw = P.op("tensor", lambda e, swb=swb, stg=stg:
                                      e.matmul(pb[swb][:, 0:512], c_perm[:],
                                               stg, start=True, stop=True),
                                      waits=[evs, b_free[swb], ev_consts],
                                      inc="PE")
                            dst = (qT if kind == 0 else kT)
                            off = ci * 512
                            P.op("vector", lambda e, tok0=tok0, stg=stg:
                                 e.tensor_mul(rtmp1[:], stg,
                                              cs_k[:, tok0:tok0 + 512]),
                                 waits=[evs, ev_btab])
                            P.op("vector", lambda e, tok0=tok0, swb=swb:
                                 e.tensor_mul(rtmp2[:], pb[swb][:, 0:512],
                                              cw_k[:, tok0:tok0 + 512]),
                                 waits=[sw, ev_btab])
                            t3 = P.op("vector", lambda e, dst=dst, off=off:
                                      e.tensor_add(dst[:, off:off + 512],
                                                   rtmp1[:], rtmp2[:]),
                                      inc="DVE")
                            b_free[swb] = t3
                            ev_stage_free[par] = t3
                            if kind == 0:
                                ev_rope_q = t3
                            else:
                                ev_rope_k = t3
                        else:
                            vstg = vstage[:, par * 512:(par + 1) * 512]
                            evs = P.op("scalar", lambda e, bank=bank,
                                       bcol=bcol, vstg=vstg: e.activation(
                                           vstg, pb[bank][:, 0:512],
                                           AF.Identity, bias=bcol),
                                       waits=[mm, ev_vstage_free[par],
                                              ev_btab], inc="ACT")
                            b_free[bank] = evs
                            cp = None
                            for m in range(4):
                                swb = 2 + m % 2
                                tr = P.op("tensor", lambda e, m=m, swb=swb,
                                          vstg=vstg: e.matmul(
                                              pb[swb][:, 0:128],
                                              vstg[:, m * 128:(m + 1) * 128],
                                              c_idbf[:], start=True,
                                              stop=True),
                                          waits=[evs, b_free[swb],
                                                 ev_consts], inc="PE")
                                base = (ci * 4 + m) * 130

                                def vcopy(e, base=base, swb=swb):
                                    dst = V_sb[:, base:base + 130].rearrange(
                                        "p (two g) -> p two g",
                                        two=2)[:, :, 0:64]
                                    src = pb[swb][:, 0:128].rearrange(
                                        "p (two g) -> p two g", two=2)
                                    return e.tensor_copy(dst, src)
                                cp = P.op("vector", vcopy,
                                          waits=[tr, ev_vones], inc="DVE")
                                b_free[swb] = cp
                            ev_vstage_free[par] = cp
                            ev_vcopy = cp
                    return stage2

                if pend is not None:
                    pend()
                pend = make_stage2()
            pend()
            ev_inproj_done[p] = last_inproj_mm

            if p + 1 < NPAIR:
                ev_winp[p + 1] = P.op("gpsimd", lambda e, p=p: e.dma_start(
                    winp[:].rearrange("p (i c) -> p i c", i=DT),
                    winT_in.ap()[:, (p + 1) * 384:(p + 2) * 384]
                    .rearrange("(i p) c -> p i c", p=128)),
                    waits=[last_inproj_mm], inc="d_wp", amt=16)

            if tap == "qT" and p == 0:
                return finish_tap(qT[:], [ev_rope_q])
            if tap == "kT" and p == 0:
                return finish_tap(kT[:], [ev_rope_k])
            if tap == "V" and p == 0:
                return finish_tap(V_sb[:], [ev_vcopy])

            ev_stats = None
            for ci in range(12):
                src_buf = qT if ci < 4 else kT
                off = (ci * 512) if ci < 4 else ((ci - 4) * 512)
                spar = ci % 2
                bk = 6 + ci % 2
                sqev = P.op("scalar", lambda e, src_buf=src_buf, off=off,
                            spar=spar: e.activation(
                                qkstage[:, spar * 512:(spar + 1) * 512],
                                src_buf[:, off:off + 512], AF.Square),
                            waits=[ev_rope_q if ci < 4 else ev_rope_k,
                                   ev_stage_free[spar]],
                            inc="ACT")
                smm = P.op("tensor", lambda e, spar=spar, bk=bk: e.matmul(
                    pb[bk][0:33, 0:512], c_hsel[:],
                    qkstage[:, spar * 512:(spar + 1) * 512],
                    start=True, stop=True),
                    waits=[sqev,
                           ev_rbc_used if bk == 6 else ev_pb7_used,
                           ev_consts], inc="PE")
                ev_stats = P.op("vector", lambda e, ci=ci, bk=bk:
                                e.reduce_max(
                                    normcol[:, ci:ci + 1],
                                    pb[bk][0:33, 0:512], axis=AX.X),
                                waits=[smm], inc="DVE")
                if bk == 6:
                    ev_rbc_used = ev_stats
                else:
                    ev_pb7_used = ev_stats
                ev_stage_free[spar] = smm
            ch = ev_stats
            for fng in (
                lambda e: e.reduce_max(normcol[:, 12:13], normcol[:, 0:4],
                                       axis=AX.X),
                lambda e: e.reduce_max(normcol[:, 13:14], normcol[:, 4:12],
                                       axis=AX.X),
                lambda e: e.tensor_mul(normcol[:, 14:15], normcol[:, 12:13],
                                       normcol[:, 13:14]),
                lambda e: e.tensor_scalar_max(normcol[:, 14:15],
                                              normcol[:, 14:15], 1e-12),
            ):
                ch = P.op("vector", fng, hard=[ch], inc="DVE")
            lnp = ch
            ln1 = P.op("scalar", lambda e: e.activation(
                normcol[:, 15:16], normcol[:, 14:15], AF.Ln), waits=[lnp])
            sq1 = P.op("scalar", lambda e: e.activation(
                normcol[:, 14:15], normcol[:, 15:16], AF.Exp, scale=0.5),
                inc="ACT")
            ngm = P.op("vector", lambda e: e.tensor_scalar(
                normcol[:, 15:16], normcol[:, 14:15], -0.128875, -1.0,
                ALU.mult, ALU.add), waits=[sq1], hard=[lnp], inc="DVE")
            ev_negM = [None, None]
            for head in (0, 1):
                nb = P.op("tensor", lambda e, head=head: e.matmul(
                    pb[6][:, 0:1],
                    c_or33[32 * head:32 * head + 1, :],
                    normcol[32 * head:32 * head + 1, 15:16],
                    start=True, stop=True),
                    waits=[ngm, ev_rbc_used, ev_consts], inc="PE")
                evn = P.op(
                    "vector", lambda e, head=head: e.tensor_copy(
                        negMcol[:, head:head + 1], pb[6][:, 0:1]),
                    waits=[nb], inc="DVE")
                ev_negM[head] = P.op(
                    "vector", lambda e, head=head: e.tensor_sub(
                        negMcol[:, 2 + head:3 + head],
                        negMcol[:, head:head + 1], c_kill[:]),
                    waits=[ev_consts], hard=[evn], inc="DVE")
                ev_rbc_used = ev_negM[head]

            for g in range(NG):
                nkt = 8 * g + 8
                nvp = nkt // 2
                ev_exp = {}
                ev_av = {}
                ev_y_last = [None, None]

                def emit_av(vp, head, g=g, nkt=nkt, nvp=nvp):
                    ypb = 4 + head
                    av = None
                    ab = (head * 2 + vp % 2) * 1024
                    for v in (2 * vp, 2 * vp + 1):
                        vsl = V_sb[:, v * 130 + head * 65:
                                   v * 130 + head * 65 + 65]
                        asl = Abuf[:, ab + (v % 2) * 512:
                                   ab + (v % 2) * 512 + 512]
                        av = P.op("tensor", lambda e, ypb=ypb, vsl=vsl,
                                  asl=asl, st=(v == 0), sp=(v == nkt - 1):
                                  e.matmul(pb[ypb][0:65, :], vsl, asl,
                                           start=st, stop=sp),
                                  waits=[ev_exp[(head, vp)], ev_vcopy,
                                         ev_norm[head] if v == 0 else None],
                                  inc="PE" if v % 2 == 1 else None)
                    ev_av[(head, vp)] = av
                    if vp == nvp - 1:
                        ev_y_last[head] = av

                for vp in range(nvp):
                    v0 = 2 * vp
                    for head in (0, 1):
                        smm = None
                        for v in (v0, v0 + 1):
                            hb = head * 2 + v % 2
                            ks = kT[64 * head:64 * head + 64,
                                    v * 128:(v + 1) * 128]
                            qs = qT[64 * head:64 * head + 64,
                                    g * 512:(g + 1) * 512]
                            smm = P.op("tensor", lambda e, hb=hb, ks=ks,
                                       qs=qs, tp=(64 * head, 0): e.matmul(
                                           pb[hb][:, 0:512], ks, qs,
                                           start=True, stop=True,
                                           tile_position=tp),
                                       waits=[ev_rope_q, ev_rope_k,
                                              b_free[hb]],
                                       inc="PE" if v % 2 == 1 else None)
                        u = v0 - 8 * g
                        bcol_i = head if u < 4 else 2 + head
                        ab = (head * 2 + vp % 2) * 1024
                        asl2 = Abuf[:, ab:ab + 1024]
                        ex = P.op("scalar", lambda e, head=head, asl2=asl2,
                                  bcol_i=bcol_i: e.activation(
                                      asl2, pbd[head][:, 0:1024], AF.Exp,
                                      bias=negMcol[:, bcol_i:bcol_i + 1],
                                      scale=0.125),
                                  waits=[smm, ev_negM[head]],
                                  inc="ACT")
                        b_free[head * 2] = ex
                        b_free[head * 2 + 1] = ex
                        if 0 <= u < 4:
                            ex = P.op("vector",
                                      lambda e, asl2=asl2, u=u:
                                      e.tensor_mul(
                                          asl2, asl2,
                                          masks[:, u * 512:(u + 2) * 512]),
                                      waits=[ex, ev_btab], inc="DVE")
                        ev_exp[(head, vp)] = ex
                    if vp >= 1:
                        for head in (0, 1):
                            emit_av(vp - 1, head)
                for head in (0, 1):
                    emit_av(nvp - 1, head)

                for head in (0, 1):
                    ypb = 4 + head
                    rrow = wrow[0:1, head * 512:head * 512 + 512]
                    lnrow = wrow[0:1, 1024 + head * 512:1536 + head * 512]
                    rsb = r1sb if head == 0 else r2sb
                    rc1 = P.op("scalar", lambda e, ypb=ypb, lnrow=lnrow:
                               e.activation(lnrow, pb[ypb][64:65, :],
                                            AF.Ln),
                               waits=[ev_y_last[head]], inc="ACT")
                    rc = P.op("scalar", lambda e, rrow=rrow, lnrow=lnrow:
                              e.activation(rrow, lnrow, AF.Exp,
                                           scale=-1.0),
                              waits=[rc1], inc="ACT")
                    rb = P.op("tensor", lambda e, rrow=rrow: e.matmul(
                        pb[6][0:64, :], c_or[0:1, 0:64], rrow,
                        start=True, stop=True),
                        waits=[rc, ev_rbc_used, ev_consts], inc="PE")
                    rcp = P.op("vector", lambda e, rsb=rsb: e.tensor_copy(
                        rsb, pb[6][0:64, :]), waits=[rb], inc="DVE")
                    ev_rbc_used = rcp
                    dst = yT_all[64 * head:64 * head + 64,
                                 p * NTOK + g * 512:p * NTOK + (g + 1) * 512]
                    ev_norm[head] = P.op(
                        "vector", lambda e, dst=dst, ypb=ypb, rsb=rsb:
                        e.tensor_mul(dst, pb[ypb][0:64, :], rsb),
                        inc="DVE")

        ev_yT_done = [ev_norm[0], ev_norm[1]]
        if tap == "yT":
            return finish_tap(yT_all[:], ev_yT_done)

        st_b.close()

        ev_xo = None
        for g in range(NG):
            for i in range(DT):
                src = xT[:, i * T + (2 * g) * 512:i * T + (2 * g) * 512 + 512]
                dst = xOwn2[:, i * NTOK + g * 512:i * NTOK + (g + 1) * 512]
                ev_xo = P.op("vector", lambda e, src=src, dst=dst:
                             e.tensor_copy(dst, src),
                             waits=[ev_inproj_done[NPAIR - 1]], inc="DVE")
        st_x.close()

        ev_b_done = [ev_inproj_done[NPAIR - 1]] + ev_yT_done + \
            [ev_rbc_used, ev_xo]

        s_f32 = sbl("s_f32", [128, DT * NTOK], F32)
        x2bf = sbl("x2bf", [128, DT * NTOK], BF16)
        gT_sb = sbl("gT_sb", [16, NTOK], BF16)
        c_esel = sbl("c_esel", [16, NE * 128], BF16)
        ev_esel = P.op("sync", dma_to(c_esel[:], c_esel_in.ap()),
                       waits=ev_b_done, inc="d_c", amt=16)

        st_c = contextlib.ExitStack()
        woutsb = sbr(st_c, "woutsb", [128, NPAIR * D], BF16)
        sqstage = sbr(st_c, "sqstage", [128, 512], F32)

        ev_wout = P.op("gpsimd", lambda e: e.dma_start(
            woutsb[:].rearrange("p (i d) -> p i d", i=DT),
            woutT_in.ap().rearrange("(i p) d -> p i d", p=128)),
            waits=ev_b_done, inc="d_wo", amt=16)

        ev_s = {}
        op_bank_free = [None, None]
        for g in range(NG):
            for d in range(DT):
                obk = d % 2
                for p in range(NPAIR):
                    lhs = woutsb[:, p * D + d * 128:p * D + (d + 1) * 128]
                    rhs = yT_all[:, p * NTOK + g * 512:
                                 p * NTOK + (g + 1) * 512]
                    P.op("tensor", lambda e, lhs=lhs, rhs=rhs, obk=obk,
                         st=(p == 0): e.matmul(pb[obk][:, 0:512], lhs, rhs,
                                               start=st, stop=False),
                         waits=[ev_wout, ev_yT_done[0], ev_yT_done[1],
                                op_bank_free[obk]])
                mm = P.op("tensor", lambda e, d=d, obk=obk: e.matmul(
                    pb[obk][:, 0:512], boutbf[0:1, d * 128:(d + 1) * 128],
                    c_o512[:], start=False, stop=True),
                    waits=[ev_consts, ev_cg], inc="PE")
                xo = xOwn2[:, d * NTOK + g * 512:d * NTOK + (g + 1) * 512]
                ssl = s_f32[:, d * NTOK + g * 512:d * NTOK + (g + 1) * 512]
                ev_s[(g, d)] = P.op("vector", lambda e, ssl=ssl, obk=obk,
                                    xo=xo: e.tensor_add(
                                        ssl, pb[obk][:, 0:512], xo),
                                    waits=[mm, ev_xo], inc="DVE")
                op_bank_free[obk] = ev_s[(g, d)]

        if tap == "s1":
            return finish_tap(s_f32[:], [ev_s[(NG - 1, DT - 1)]])

        ev_x2 = {}
        sq_free = None
        for g in range(NG):
            st_mm = None
            sq_mm = None
            for d in range(DT):
                ssl = s_f32[:, d * NTOK + g * 512:d * NTOK + (g + 1) * 512]
                sq = P.op("vector", lambda e, ssl=ssl: e.tensor_mul(
                    sqstage[:], ssl, ssl),
                    waits=[ev_s[(g, d)], sq_free], inc="DVE")
                st_mm = P.op("tensor", lambda e, ssl=ssl, st=(d == 0),
                             sp=(d == DT - 1): e.matmul(
                                 pb[2][0:1, :], c_oc[:], ssl, start=st,
                                 stop=sp),
                             waits=[ev_s[(g, d)], ev_consts], inc="PE")
                sq_mm = P.op("tensor", lambda e, st=(d == 0),
                             sp=(d == DT - 1): e.matmul(
                                 pb[3][0:1, :], c_oc[:], sqstage[:],
                                 start=st, stop=sp),
                             waits=[sq], inc="PE")
                sq_free = sq_mm
            ch = P.op("vector", lambda e: e.tensor_scalar_mul(
                wrow[0:1, 0:512], pb[2][0:1, :], 1.0 / D), waits=[st_mm],
                inc="DVE")
            for fng in (
                lambda e: e.tensor_scalar_mul(wrow[0:1, 512:1024],
                                              pb[3][0:1, :], 1.0 / D),
                lambda e: e.tensor_mul(wrow[0:1, 1024:1536], wrow[0:1, 0:512],
                                       wrow[0:1, 0:512]),
                lambda e: e.tensor_sub(wrow[0:1, 512:1024],
                                       wrow[0:1, 512:1024],
                                       wrow[0:1, 1024:1536]),
                lambda e: e.tensor_scalar_add(wrow[0:1, 512:1024],
                                              wrow[0:1, 512:1024], LN_EPS),
            ):
                ch = P.op("vector", fng, waits=[sq_mm], hard=[ch], inc="DVE")
            ve_r = ch
            P.op("scalar", lambda e: e.activation(
                wrow[0:1, 1024:1536], wrow[0:1, 512:1024], AF.Ln),
                waits=[ve_r])
            inv_r = P.op("scalar", lambda e: e.activation(
                wrow[0:1, 1536:2048], wrow[0:1, 1024:1536], AF.Exp,
                scale=-0.5), inc="ACT")
            mb = P.op("tensor", lambda e: e.matmul(
                pb[4][:, 0:512], c_or[:], wrow[0:1, 0:512], start=True,
                stop=True), waits=[ve_r, ev_consts], inc="PE")
            ib = P.op("tensor", lambda e: e.matmul(
                pb[5][:, 0:512], c_or[:], wrow[0:1, 1536:2048], start=True,
                stop=True), waits=[inv_r, ev_consts], inc="PE")
            last_n = None
            for d in range(DT):
                ssl = s_f32[:, d * NTOK + g * 512:d * NTOK + (g + 1) * 512]
                P.op("vector", lambda e, ssl=ssl: e.tensor_sub(
                    ssl, ssl, pb[4][:, 0:512]), waits=[mb])
                last_n = P.op("vector", lambda e, ssl=ssl: e.tensor_mul(
                    ssl, ssl, pb[5][:, 0:512]), waits=[ib], inc="DVE")
                x2sl = x2bf[:, d * NTOK + g * 512:d * NTOK + (g + 1) * 512]
                ev_x2[(g, d)] = P.op("scalar", lambda e, ssl=ssl, x2sl=x2sl:
                                     e.copy(x2sl, ssl), waits=[last_n],
                                     inc="ACT")
            ev_s[(g, "ln")] = last_n

        ev_c_done = [ev_s[(NG - 1, "ln")], ev_x2[(NG - 1, DT - 1)]]
        if tap == "x2":
            return finish_tap(s_f32[:], ev_c_done)

        st_c.close()
        st_ac.close()

        st_e = contextlib.ExitStack()
        w1bf = sbr(st_e, "w1bf", [128, DT * HID], BF16)
        w2bf = sbr(st_e, "w2bf", [128, 16 * D], BF16)
        hgT = sbr(st_e, "hgT", [128, 16 * 512], BF16)
        xg = sbr(st_e, "xg", [128, DT * 512], BF16)
        b1ct = sbr(st_e, "b1ct", [128, 16 * NE], F32)
        b1st = sbr(st_e, "b1st", [16, 128 * NE], BF16)
        gsb = sbr(st_e, "gsb", [128, 512], BF16)

        ev_w1 = [[None, None] for _ in range(NE + 1)]
        ev_w2 = [[None, None] for _ in range(NE + 1)]
        ev_ffn_space = ev_b_done + ev_c_done

        def dma_w1_half(e, ei, hlf):
            return e.dma_start(
                w1bf[:].rearrange("p (i m) -> p i m", i=DT)
                [:, :, hlf * 1024:(hlf + 1) * 1024],
                w1_in.ap()[ei].rearrange("(i p) m -> p i m", p=128)
                [:, :, hlf * 1024:(hlf + 1) * 1024])

        def dma_w2_half(e, ei, hlf):
            return e.dma_start(
                w2bf[:].rearrange("p (j d) -> p j d", j=16)
                [:, :, hlf * 512:(hlf + 1) * 512],
                w2_in.ap()[ei].rearrange("(j p) d -> p j d", p=128)
                [:, :, hlf * 512:(hlf + 1) * 512])

        ev_b1d = P.op("gpsimd", lambda e: e.dma_start(
            b1st[:].rearrange("j (E p) -> j E p", E=NE),
            b1_in.ap().rearrange("E (j p) -> j E p", p=128)),
            waits=ev_ffn_space, inc="d_b10", amt=16)
        for hlf in (0, 1):
            ev_w1[0][hlf] = P.op(
                "gpsimd", lambda e, hlf=hlf: dma_w1_half(e, 0, hlf),
                waits=ev_ffn_space, inc=f"d_w1{hlf}", amt=16)
            ev_w2[0][hlf] = P.op(
                "gpsimd", lambda e, hlf=hlf: dma_w2_half(e, 0, hlf),
                waits=ev_ffn_space, inc=f"d_w2{hlf}", amt=16)

        gw = gwork

        def SL(c):
            return gw[:, c:c + 64].rearrange("p (t e) -> p t e", e=16)

        def CL(c):
            return gw[:, c:c + 4].rearrange("p (t o) -> p t o", o=1)

        def BC(c):
            return CL(c).broadcast_to((128, 4, 16))

        ev_gT = {}
        ev_pb1_free = ev_s[(NG - 1, DT - 1)]
        ev_c3_free = None
        ev_b2p = {}
        b2_bank_free = [ev_s[(NG - 1, "ln")], ev_s[(NG - 1, "ln")]]

        def emit_b2(cc):
            for i in range(DT):
                ob = 5 + i % 2
                mm = P.op("tensor", lambda e, i=i, cc=cc, ob=ob: e.matmul(
                    pb[ob][:, 0:512], b2bf[:, i * 128:(i + 1) * 128],
                    gT_sb[:, cc * 512:(cc + 1) * 512], start=True,
                    stop=True),
                    waits=[ev_consts, ev_cg, ev_gT[cc * 4 + 3],
                           b2_bank_free[i % 2]],
                    inc="PE")
                ssl = s_f32[:, i * NTOK + cc * 512:
                            i * NTOK + (cc + 1) * 512]
                ev_b2p[(cc, i)] = P.op(
                    "vector", lambda e, ssl=ssl, ob=ob: e.tensor_add(
                        ssl, pb[ob][:, 0:512], ssl),
                    waits=[mm, ev_s[(cc, "ln")]], inc="DVE")
                b2_bank_free[i % 2] = ev_b2p[(cc, i)]

        for b4 in range(4):
            mm = None
            for q in range(4):
                tt = b4 * 4 + q
                for d in range(DT):
                    lhs = s_f32[:, d * NTOK + tt * 128:
                                d * NTOK + (tt + 1) * 128]
                    mm = P.op("tensor", lambda e, lhs=lhs, d=d, tt=tt,
                              st=(d == 0), sp=(d == DT - 1): e.matmul(
                                  pb[0][:, tt * 16:(tt + 1) * 16], lhs,
                                  gatef[:, d * NE:(d + 1) * NE], start=st,
                                  stop=sp),
                              waits=[ev_s[(b4, "ln")], ev_consts],
                              inc="PE" if (d == DT - 1 and q == 3)
                              else None)
            if b4 == 0:
                btr = None
                for e2 in range(NE):
                    btr = P.op("tensor", lambda e, e2=e2: e.matmul(
                        pb[7][:, e2 * 16:(e2 + 1) * 16],
                        b1st[:, e2 * 128:(e2 + 1) * 128],
                        c_idbf[0:16, 0:16], start=True, stop=True),
                        waits=[ev_b1d, ev_pb7_used, ev_consts],
                        inc="PE" if e2 == NE - 1 else None)
                ev_b1ct = P.op("vector", lambda e: e.tensor_copy(
                    b1ct[:], pb[7][:, 0:16 * NE]), waits=[btr], inc="DVE")
            else:
                emit_b2(b4 - 1)
            pslab = pb[0][:, b4 * 64:b4 * 64 + 64].rearrange(
                "p (t e) -> p t e", e=16)
            nm = P.op("vector", lambda e, pslab=pslab: e.reduce_max(
                CL(256), pslab, axis=AX.X, negate=True),
                waits=[mm], inc="DVE")
            xs = P.op("vector", lambda e, pslab=pslab: e.tensor_tensor(
                SL(0), pslab, BC(256), ALU.add), hard=[nm], inc="DVE")
            exl = P.op("scalar", lambda e: e.activation(
                gw[:, 64:128], gw[:, 0:64], AF.Exp), waits=[xs], inc="ACT")
            ch = exl
            gate_ops = [
                lambda e: e.reduce_max(CL(260), SL(64), axis=AX.X),
                lambda e: e.tensor_tensor(SL(128), SL(64), BC(260),
                                          ALU.is_ge),
                lambda e: e.scalar_tensor_tensor(SL(128), SL(128), -1e9,
                                                 SL(64), ALU.mult, ALU.add),
                lambda e: e.reduce_max(CL(260), SL(128), axis=AX.X),
                lambda e: e.tensor_tensor(SL(0), SL(128), BC(260),
                                          ALU.is_ge),
                lambda e: e.scalar_tensor_tensor(SL(128), SL(0), -1e9,
                                                 SL(128), ALU.mult,
                                                 ALU.add),
                lambda e: e.reduce_max(CL(260), SL(128), axis=AX.X),
                lambda e: e.tensor_tensor(SL(0), SL(64), BC(260),
                                          ALU.is_ge),
                lambda e: e.tensor_tensor(SL(0), SL(0), SL(64), ALU.mult),
                lambda e: e.reduce_sum(CL(264), SL(0), axis=AX.X),
                lambda e: e.tensor_scalar_max(gw[:, 264:268],
                                              gw[:, 264:268], GATE_EPS),
                lambda e: e.reciprocal(gw[:, 268:272], gw[:, 264:268]),
            ]
            for fng in gate_ops:
                ch = P.op("vector", fng, hard=[ch], inc="DVE")
            gout = P.op("vector", lambda e: e.tensor_tensor(
                SL(192), SL(0), BC(268), ALU.mult),
                waits=[ev_c3_free], hard=[ch], inc="DVE")
            gtr = None
            for q in range(4):
                tt = b4 * 4 + q
                gtr = P.op("tensor", lambda e, q=q: e.matmul(
                    pb[1][0:16, 0:128],
                    gw[:, 192 + q * 16:208 + q * 16],
                    c_idf[:], start=True, stop=True),
                    waits=[gout, ev_pb1_free, ev_consts], inc="PE")
                ev_gT[tt] = P.op("vector", lambda e, tt=tt: e.tensor_copy(
                    gT_sb[:, tt * 128:(tt + 1) * 128], pb[1][0:16, 0:128]),
                    waits=[gtr], inc="DVE")
                ev_pb1_free = ev_gT[tt]
            ev_c3_free = gtr
        emit_b2(NG - 1)
        ev_gT_done = ev_gT[15]

        if tap == "gT":
            return finish_tap(gT_sb[:], [ev_gT_done])
        if tap == "gw":
            return finish_tap(gwork[:], [ev_gT_done])

        ev_gb_free = None
        ev_xg_free = None
        ev_h_free = [None, None]
        ev_o_free = [None, None]
        ev_hg_free = {}
        ev_evac = {}
        chunks = [(ei, cc) for ei in range(NE) for cc in range(NG)]
        xgl_t = {}

        def emit_gxg(t):
            nonlocal ev_gb_free
            ei, cc = chunks[t]
            gb = P.op("tensor", lambda e, ei=ei, cc=cc: e.matmul(
                pb[0][:, 0:512], c_esel[:, ei * 128:(ei + 1) * 128],
                gT_sb[:, cc * 512:(cc + 1) * 512], start=True, stop=True),
                waits=[ev_gT_done, ev_gb_free, ev_esel],
                inc="PE")
            gcp = P.op("vector", lambda e: e.tensor_copy(
                gsb[:], pb[0][:, 0:512]), waits=[gb], inc="DVE")
            ev_gb_free = gcp
            xgl = None
            for i in range(DT):
                x2sl = x2bf[:, i * NTOK + cc * 512:
                            i * NTOK + (cc + 1) * 512]
                xgl = P.op("vector", lambda e, i=i, x2sl=x2sl:
                           e.tensor_mul(xg[:, i * 512:(i + 1) * 512],
                                        x2sl, gsb[:]),
                           waits=[gcp, ev_xg_free, ev_x2[(cc, i)]],
                           inc="DVE" if i == DT - 1 else None)
            xgl_t[t] = xgl

        for t, (ei, cc) in enumerate(chunks):
                if t == 0:
                    emit_gxg(0)
                xgl = xgl_t.pop(t)
                m1 = None
                ev_relu_last = None
                for j in range(16):
                    hb = 1 + j % 2
                    for i in range(DT):
                        lhs = w1bf[:, i * HID + j * 128:
                                   i * HID + (j + 1) * 128]
                        m1 = P.op("tensor", lambda e, lhs=lhs, i=i, hb=hb,
                                  st=(i == 0), sp=(i == DT - 1): e.matmul(
                                      pb[hb][:, 0:512], lhs,
                                      xg[:, i * 512:(i + 1) * 512], start=st,
                                      stop=sp),
                                  waits=[xgl, ev_w1[ei][j // 8],
                                         ev_h_free[hb - 1]],
                                  inc="PE" if i == DT - 1 else None)
                    stt = P.op("vector", lambda e, hb=hb, j=j, ei=ei:
                               e.scalar_tensor_tensor(
                                   pb[hb][:, 0:512], gsb[:],
                                   b1ct[:, ei * 16 + j:ei * 16 + j + 1],
                                   pb[hb][:, 0:512], ALU.mult, ALU.add),
                               waits=[m1, ev_b1ct], inc="DVE")
                    rl = P.op("scalar", lambda e, j=j, hb=hb: e.activation(
                        hgT[:, j * 512:(j + 1) * 512], pb[hb][:, 0:512],
                        AF.Relu), waits=[stt, ev_hg_free.get(j)], inc="ACT")
                    ev_h_free[hb - 1] = rl
                    ev_relu_last = rl
                    if cc == NG - 1 and j == 7:
                        ev_w1A_used = m1
                ev_xg_free = m1
                m2 = None
                for i in range(DT):
                    ob = 3 + i % 2
                    for j in range(16):
                        lhs = w2bf[:, j * D + i * 128:j * D + (i + 1) * 128]
                        m2 = P.op("tensor", lambda e, lhs=lhs, ob=ob, j=j,
                                  st=(j == 0), sp=(j == 15): e.matmul(
                                      pb[ob][:, 0:512], lhs,
                                      hgT[:, j * 512:(j + 1) * 512],
                                      start=st, stop=sp),
                                  waits=[ev_relu_last, ev_w2[ei][i // 4],
                                         ev_o_free[ob - 3]],
                                  inc="PE" if j == 15 else None)
                    ssl = s_f32[:, i * NTOK + cc * 512:
                                i * NTOK + (cc + 1) * 512]
                    ee = P.op("vector", lambda e, ssl=ssl, ob=ob:
                              e.tensor_add(ssl, pb[ob][:, 0:512], ssl),
                              waits=[m2, ev_s[(cc, "ln")]], inc="DVE")
                    ev_evac[(ei, cc, i)] = ee
                    ev_o_free[ob - 3] = ee
                    if i == 0 and t + 1 < len(chunks):
                        emit_gxg(t + 1)
                    if cc == NG - 1 and i == 3:
                        ev_w2C_used = m2
                for jj in range(16):
                    ev_hg_free[jj] = m2
                if cc != NG - 1:
                    continue
                if ei + 1 < NE:
                    ev_w1[ei + 1][0] = P.op(
                        "gpsimd", lambda e, ei=ei: dma_w1_half(e, ei + 1, 0),
                        waits=[ev_w1A_used], inc="d_w10", amt=16)
                    ev_w1[ei + 1][1] = P.op(
                        "gpsimd", lambda e, ei=ei: dma_w1_half(e, ei + 1, 1),
                        waits=[ev_xg_free], inc="d_w11", amt=16)
                    ev_w2[ei + 1][0] = P.op(
                        "gpsimd", lambda e, ei=ei: dma_w2_half(e, ei + 1, 0),
                        waits=[ev_w2C_used], inc="d_w20", amt=16)
                    ev_w2[ei + 1][1] = P.op(
                        "gpsimd", lambda e, ei=ei: dma_w2_half(e, ei + 1, 1),
                        waits=[ev_hg_free[0]], inc="d_w21", amt=16)
                if tap == "acc0" and ei == 0:
                    return finish_tap(
                        s_f32[:], [ev_evac[(0, c2, i2)] for c2 in range(NG)
                                   for i2 in range(DT)])

        if tap == "acc":
            return finish_tap(
                s_f32[:], [ev_evac[(NE - 1, c2, i2)] for c2 in range(NG)
                           for i2 in range(DT)])

        st_e.close()

        st_f = contextlib.ExitStack()
        xtok = sbr(st_f, "xtok", [128, 2 * D], F32)
        otile = sbr(st_f, "otile", [128, 2 * D], F32)
        sqf = sbr(st_f, "sqf", [128, D], F32)
        ev_out = [None] * 16
        ev_tr_free = [None, None]
        ev_sq_free = None
        ev_ot_free = [None, None]
        ev_f_space = [ev_evac[(NE - 1, NG - 1, DT - 1)], ev_hg_free[0]]
        for tt in range(16):
            par = tt % 2
            g = tt // 4
            cps = None
            for i in range(DT):
                trb = i % 2
                lhs = s_f32[:, i * NTOK + tt * 128:i * NTOK + (tt + 1) * 128]
                tr = P.op("tensor", lambda e, lhs=lhs, trb=trb: e.matmul(
                    pb[trb][:, 0:128], lhs, c_idf[:], start=True, stop=True),
                    waits=[ev_evac[(NE - 1, g, i)], ev_tr_free[trb],
                           ev_consts],
                    inc="PE")
                cps = P.op("vector", lambda e, i=i, par=par, trb=trb:
                           e.tensor_copy(
                               xtok[:, par * D + i * 128:
                                    par * D + (i + 1) * 128],
                               pb[trb][:, 0:128]),
                           waits=[tr] + (ev_f_space if tt == 0 else []),
                           inc="DVE")
                ev_tr_free[trb] = cps
            ch = P.op("vector", lambda e, par=par: e.tensor_mul(
                sqf[:], xtok[:, par * D:(par + 1) * D],
                xtok[:, par * D:(par + 1) * D]),
                waits=[cps, ev_sq_free], inc="DVE")
            for fng in (
                lambda e, par=par: e.reduce_sum(
                    fwork[:, 0:1], xtok[:, par * D:(par + 1) * D], axis=AX.X),
                lambda e: e.reduce_sum(fwork[:, 1:2], sqf[:], axis=AX.X),
                lambda e: e.tensor_scalar_mul(fwork[:, 2:3], fwork[:, 0:1],
                                              1.0 / D),
                lambda e: e.tensor_scalar_mul(fwork[:, 3:4], fwork[:, 1:2],
                                              1.0 / D),
                lambda e: e.tensor_mul(fwork[:, 4:5], fwork[:, 2:3],
                                       fwork[:, 2:3]),
                lambda e: e.tensor_sub(fwork[:, 5:6], fwork[:, 3:4],
                                       fwork[:, 4:5]),
                lambda e: e.tensor_scalar_add(fwork[:, 5:6], fwork[:, 5:6],
                                              LN_EPS),
            ):
                ch = P.op("vector", fng, hard=[ch], inc="DVE")
            vv = ch
            ev_sq_free = vv
            P.op("scalar", lambda e: e.activation(
                fwork[:, 6:7], fwork[:, 5:6], AF.Ln), waits=[vv])
            inv = P.op("scalar", lambda e: e.activation(
                fwork[:, 7:8], fwork[:, 6:7], AF.Exp, scale=-0.5), inc="ACT")
            ot = P.op("vector", lambda e, par=par: e.tensor_scalar(
                otile[:, par * D:(par + 1) * D],
                xtok[:, par * D:(par + 1) * D],
                fwork[:, 2:3], fwork[:, 7:8], ALU.subtract, ALU.mult),
                waits=[inv, ev_ot_free[par]], hard=[vv], inc="DVE")
            ev_out[tt] = P.op("sync", lambda e, tt=tt, par=par: e.dma_start(
                y_out.ap()[tt * 128:(tt + 1) * 128, :],
                otile[:, par * D:(par + 1) * D]),
                waits=[ot], inc=f"d_y{par}", amt=16)
            ev_ot_free[par] = ev_out[tt]

        tails = [(s, P.ctr[s]) for s in sorted(P.ctr) if s.startswith("d_")]
        P.op("sync", lambda e: e.nop(),
             waits=[ev_out[14], ev_out[15]] + tails)
        P.emit(stack, sim_serial=sim_serial)
    return nc


def _rope_tables():
    half = 32
    p = np.arange(half, dtype=np.float64) / (half - 1)
    thetas = 1.0 * (10000.0 / 1.0) ** p
    pos = np.arange(T, dtype=np.float64)[:, None] / thetas[None, :]
    sin = np.sin(pos).T
    cos = np.cos(pos).T
    cs = np.concatenate([sin, -sin, sin, -sin], axis=0)
    cw = np.concatenate([cos, cos, cos, cos], axis=0)
    return cs, cw


def _prog_perm(h):
    blocks = np.arange(8)
    if h == 1:
        blocks = blocks ^ 1
    return blocks


def _consts(h):
    idn = np.eye(128, dtype=np.float64)
    perm = np.zeros((128, 128), dtype=np.float64)
    for j in range(128):
        swap = j + 32 if (j % 64) < 32 else j - 32
        perm[swap, j] = 1.0
    cs, cw = _rope_tables()
    pos_prog = np.concatenate(
        [np.arange(512 * gb, 512 * gb + 512) for gb in _prog_perm(h)])
    msk = np.zeros((128, 4 * 512), dtype=np.float64)
    pcol = np.arange(128)[:, None]
    jcol = np.arange(512)[None, :]
    for u in range(4):
        msk[:, u * 512:(u + 1) * 512] = (128 * u + pcol <= jcol)
    mscal = np.full((128, 4), float(h), dtype=np.float32)
    kill = np.full((128, 1), 0.0 if h == 1 else 40000.0, dtype=np.float32)
    esel = np.zeros((16, NE * 128), dtype=np.float64)
    for e in range(NE):
        esel[e, e * 128:(e + 1) * 128] = 1.0
    hsel = np.zeros((128, 33), dtype=np.float64)
    hsel[0:64, 0] = 1.0
    hsel[64:128, 32] = 1.0
    bf = ml_dtypes.bfloat16
    return {
        "c_hsel": hsel.astype(bf),
        "c_idbf": idn.astype(bf), "c_idf": idn.astype(np.float32),
        "c_perm": perm.astype(bf),
        "c_csk": np.ascontiguousarray(cs[:, pos_prog]).astype(bf),
        "c_cwk": np.ascontiguousarray(cw[:, pos_prog]).astype(bf),
        "c_msk": msk.astype(bf), "c_mscal": mscal, "c_kill": kill,
        "c_esel": esel.astype(bf),
        "c_onesrow": np.ones((1, 128), np.float32),
        "c_or33": np.ones((33, 128), np.float32),
        "c_onescol": np.ones((128, 1), np.float32),
        "c_ones512": np.ones((1, 512), bf),
    }


def make_in_maps(L0_attn_x, L0_attn_W_in, L0_attn_b_in, L0_attn_W_out,
                 L0_attn_b_out, L0_ffn_gate, L0_ffn_W1, L0_ffn_b1, L0_ffn_W2,
                 L0_ffn_b2):
    x = np.ascontiguousarray(np.asarray(L0_attn_x, dtype=np.float32))
    W_in = np.asarray(L0_attn_W_in, dtype=np.float32)
    b_in = np.asarray(L0_attn_b_in, dtype=np.float32)
    W_out = np.asarray(L0_attn_W_out, dtype=np.float32)
    b_out = np.asarray(L0_attn_b_out, dtype=np.float32)

    winT = np.concatenate(
        [np.stack([W_in[128 * p:128 * (p + 1)].T,
                   W_in[1024 + 128 * p:1024 + 128 * (p + 1)].T,
                   W_in[2048 + 128 * p:2048 + 128 * (p + 1)].T], axis=1)
         .reshape(D, 384) for p in range(NPAIR)], axis=1)
    winT = np.ascontiguousarray(winT)
    binT = np.zeros((128, 24), dtype=np.float32)
    for p in range(NPAIR):
        binT[:, 3 * p] = b_in[128 * p:128 * (p + 1)]
        binT[:, 3 * p + 1] = b_in[1024 + 128 * p:1024 + 128 * (p + 1)]
        binT[:, 3 * p + 2] = b_in[2048 + 128 * p:2048 + 128 * (p + 1)]
    shared = dict(
        winT=winT, binT=binT,
        woutT=np.ascontiguousarray(W_out.T),
        bout=np.ascontiguousarray(b_out[None, :]),
        gate=np.ascontiguousarray(np.asarray(L0_ffn_gate, np.float32)),
        w1=np.ascontiguousarray(np.asarray(L0_ffn_W1, np.float32)),
        w2=np.ascontiguousarray(np.asarray(L0_ffn_W2, np.float32)),
        b1=np.ascontiguousarray(np.asarray(L0_ffn_b1, np.float32)),
        b2=np.ascontiguousarray(np.asarray(L0_ffn_b2, np.float32)),
    )
    consts = [_consts(0), _consts(1)]
    in_maps = []
    for c in range(8):
        b, h = c // 2, c % 2
        xb = x[b].reshape(8, 512, D)[_prog_perm(h)].reshape(T, D)
        m = dict(shared)
        m.update(consts[h])
        m["xin"] = np.ascontiguousarray(xb)
        in_maps.append(m)
    return in_maps


def kernel(L0_attn_x, L0_attn_W_in, L0_attn_b_in, L0_attn_W_out, L0_attn_b_out,
           L0_ffn_gate, L0_ffn_W1, L0_ffn_b1, L0_ffn_W2, L0_ffn_b2):
    in_maps = make_in_maps(L0_attn_x, L0_attn_W_in, L0_attn_b_in,
                           L0_attn_W_out, L0_attn_b_out, L0_ffn_gate,
                           L0_ffn_W1, L0_ffn_b1, L0_ffn_W2, L0_ffn_b2)
    nc = build_program()
    res = run_bass_kernel_spmd(nc, in_maps, core_ids=list(range(8)))
    out = np.empty((4, T, D), dtype=np.float32)
    for c in range(8):
        b, h = c // 2, c % 2
        yc = res.results[c]["y"]
        for g in range(NG):
            out[b, 512 * (2 * g + h):512 * (2 * g + h) + 512] = \
                yc[g * 512:(g + 1) * 512]
    return out
